# version 43
# baseline (speedup 1.0000x reference)
import numpy as np
import ml_dtypes
import concourse.bass as bass
import concourse.mybir as mybir
from concourse.bass_utils import run_bass_kernel_spmd

F32 = mybir.dt.float32
BF16 = mybir.dt.bfloat16
U8 = mybir.dt.uint8
AF = mybir.ActivationFunctionType
ALU = mybir.AluOpType
AX = mybir.AxisListType

D = 1024
H = 16
DH = 64
NG = 16
CHUNK = 512
QTS = 128
NEG = -1.0e30

ENGINES = ("te", "ve", "se", "po", "sy")
QDEPTH = 2


class Step:
    __slots__ = ("eng", "emit", "dma_queue", "done", "waits", "tag", "rec")

    def __init__(self, eng, emit, dma_queue, tag):
        self.eng = eng
        self.emit = emit
        self.dma_queue = dma_queue
        self.done = None
        self.waits = []
        self.tag = tag


class Prog:

    def __init__(self, nc):
        self.nc = nc
        self.steps = []
        self.eng_count = {e: 0 for e in ENGINES}
        self.dma_count = {}
        self.writer = {}
        self.readers = {}

    def step(self, eng, emit, reads=(), writes=(), dma_queue=None, tag=""):
        st = Step(eng, emit, dma_queue, tag)
        reads = list(reads)
        writes = list(writes)
        for k in list(reads):
            if k.startswith("ps"):
                reads.remove(k)
                writes.append(k)
        deps = []
        for k in reads:
            w = self.writer.get(k)
            if w is not None:
                deps.append(w)
        for k in writes:
            w = self.writer.get(k)
            if w is not None:
                deps.append(w)
            deps.extend(self.readers.get(k, ()))
        seen = set()
        for d in deps:
            if id(d) not in seen and d is not st:
                seen.add(id(d))
                st.waits.append(d)
        if dma_queue is None:
            self.eng_count[eng] += 1
            st.done = ("e_" + eng, self.eng_count[eng])
        else:
            n = self.dma_count.get(dma_queue, 0) + 1
            self.dma_count[dma_queue] = n
            st.done = ("d_" + dma_queue, 16 * n)
        for k in writes:
            self.writer[k] = st
            self.readers[k] = []
        for k in reads:
            self.readers.setdefault(k, []).append(st)
        self.steps.append(st)
        return st

    def barrier_deps(self):
        last = {}
        for st in self.steps:
            last[st.done[0]] = st
        return list(last.values())

    def seed(self, key, dep_steps):
        self.readers.setdefault(key, []).extend(dep_steps)

    def _cost(self, st):
        t = st.tag
        if st.dma_queue is not None:
            for pre, c in (("ldw_", 18.0), ("ldwo_", 5.0), ("ld_C", 9.0),
                           ("ld_S", 9.0),
                           ("x_in", 4.0), ("st_", 4.0), ("mp_out", 5.0),
                           ("ld_M", 9.0)):
                if t.startswith(pre):
                    return c
            return 1.5
        for pre, c in (("s2_", 0.9), ("s2g_", 1.8), ("sum_", 1.0),
                       ("mn_", 0.9), ("expmg", 0.6), ("expm", 0.6),
                       ("rcp_", 1.0), ("n1_", 0.5), ("n2_", 0.5),
                       ("av_", 1.1), ("ave_", 0.6), ("wo_", 4.1),
                       ("ffn_", 5.0), ("q_fm_mm", 3.6), ("k_fm_mm", 3.6),
                       ("v_mm", 1.9), ("q_fm_ev", 0.6), ("k_fm_ev", 0.6),
                       ("v_ev", 0.6), ("PTr_mm", 7.0), ("QTr_mm", 7.0), ("Mp_mm", 1.8),
                       ("PTr_ev", 0.6), ("QTr_ev", 0.6), ("Mp_ev", 0.6),
                       ("mtT", 0.3), ("lnz_ev", 1.0), ("lny_ev", 1.0),
                       ("wT_", 0.25), ("wTe_", 0.3), ("warm_", 3.5)):
            if t.startswith(pre):
                return c
        if "T_" in t and (t.startswith("h1_") or t.startswith("h2_")):
            return 0.25 if "Te_" not in t else 0.3
        if t.endswith("_bs"):
            return 0.9
        if "_sum" in t or "_sq" in t:
            return 1.3
        if "xm_" in t or "x2_" in t or "out_" in t or t.startswith("n1")  \
           or t.startswith("n2"):
            return 0.9
        return 0.2

    def schedule(self):
        import heapq
        nsteps = len(self.steps)
        for i, st in enumerate(self.steps):
            st.rec = i
        children = [[] for _ in range(nsteps)]
        nleft = [0] * nsteps
        for st in self.steps:
            deps = {id(d): d for d in st.waits}
            nleft[st.rec] = len(deps)
            for d in deps.values():
                children[d.rec].append(st)
        cp = [0.0] * nsteps
        for st in reversed(self.steps):
            m = 0.0
            for ch in children[st.rec]:
                m = max(m, 0.5 + cp[ch.rec])
            cp[st.rec] = self._cost(st) + m
        for st in self.steps:
            if st.tag.startswith("x_in0"):
                cp[st.rec] += 100.0
        lanes = list(ENGINES) + ["q_" + q for q in self.dma_count]
        eng_time = {e: 0.0 for e in lanes}
        pend = {e: [] for e in lanes}
        run = {e: [] for e in lanes}
        finish = [0.0] * nsteps
        ready_at = [0.0] * nsteps

        def lane(st):
            return st.eng if st.dma_queue is None else "q_" + st.dma_queue

        for st in self.steps:
            if nleft[st.rec] == 0:
                heapq.heappush(pend[lane(st)], (0.0, st.rec, st))
        order = {e: [] for e in lanes}
        n_done = 0
        while n_done < nsteps:
            best = None
            for e in lanes:
                while pend[e] and pend[e][0][0] <= eng_time[e]:
                    rt, rec, st = heapq.heappop(pend[e])
                    heapq.heappush(run[e], (-cp[rec], rec, st))
                cand = None
                if run[e]:
                    _, rec, st = run[e][0]
                    cand = (eng_time[e], rec, st, e, False)
                elif pend[e]:
                    rt, rec, st = pend[e][0]
                    cand = (rt, rec, st, e, True)
                if cand and (best is None or (cand[0], cand[1]) < (best[0], best[1])):
                    best = cand
            start, rec, st, e, from_pend = best
            if from_pend:
                heapq.heappop(pend[e])
            else:
                heapq.heappop(run[e])
            f = start + self._cost(st)
            eng_time[e] = f
            finish[rec] = f
            ready_at[rec] = start
            order[e].append(st)
            n_done += 1
            for ch in children[rec]:
                nleft[ch.rec] -= 1
                if nleft[ch.rec] == 0:
                    rt = 0.0
                    for d in ch.waits:
                        lat = 0.5 if (d.eng != ch.eng or d.dma_queue
                                      or ch.dma_queue) else 0.05
                        rt = max(rt, finish[d.rec] + lat)
                    heapq.heappush(pend[lane(ch)], (rt, ch.rec, ch))
        starts = {}
        for e in lanes:
            for pos, st in enumerate(order[e]):
                starts[st.rec] = ready_at[st.rec]
        per_eng = {e: [] for e in ENGINES}
        for st in self.steps:
            per_eng[st.eng].append(st)
        for e in ENGINES:
            per_eng[e].sort(key=lambda st: (st.tag == "final",
                                            starts[st.rec], st.rec))
        eng_n = {e: 0 for e in ENGINES}
        dma_n = {q: 0 for q in self.dma_count}
        for e in ENGINES:
            for st in per_eng[e]:
                if st.dma_queue is None:
                    eng_n[e] += 1
                    st.done = ("e_" + e, eng_n[e])
                else:
                    dma_n[st.dma_queue] += 1
                    st.done = ("d_" + st.dma_queue, 16 * dma_n[st.dma_queue])
        return per_eng

    def finalize(self, block, sems):
        per_eng = self.schedule()
        hw_mark = {}
        for e in ENGINES:
            for st in per_eng[e]:
                resolved = []
                for dep in st.waits:
                    sem_name, val = dep.done
                    k = (e, sem_name)
                    if hw_mark.get(k, -1) < val:
                        hw_mark[k] = val
                        resolved.append((sem_name, val))
                if st.dma_queue is not None:
                    sem_name = "d_" + st.dma_queue
                    prev = st.done[1] - 16 * QDEPTH
                    if prev > 0 and hw_mark.get((e, sem_name), -1) < prev:
                        hw_mark[(e, sem_name)] = prev
                        resolved.append((sem_name, prev))
                st.waits = resolved

        def make_body(e):
            def body(eng):
                for st in per_eng[e]:
                    for sem_name, val in st.waits:
                        eng.wait_ge(sems[sem_name], val)
                    last = st.emit(eng)
                    if st.dma_queue is None:
                        assert last is not None, f"step {st.tag} returned None"
                        last.then_inc(sems["e_" + e], 1)
            return body

        block.tensor(make_body("te"))
        block.vector(make_body("ve"))
        block.scalar(make_body("se"))
        block.gpsimd(make_body("po"))
        block.sync(make_body("sy"))


def host_constants():
    k = np.arange(D)
    ang = 2.0 * np.pi * np.outer(k, k) / D
    cns = {}
    cns["cst_C"] = np.cos(ang).astype(ml_dtypes.bfloat16)
    cns["cst_S"] = np.sin(ang).astype(ml_dtypes.bfloat16)
    i = np.arange(QTS)
    same = (i[:, None] // DH) == (i[None, :] // DH)
    causal = i[None, :] <= i[:, None]
    m2 = (same & causal)
    cns["cst_maskneg2T"] = np.ascontiguousarray(
        np.where(m2.T, 0.0, NEG)).astype(np.float32)
    cns["cst_mask4"] = np.ascontiguousarray(
        np.tile(np.where(m2.T, 0.0, NEG), (1, 4))).astype(
        ml_dtypes.bfloat16)
    mg0 = np.zeros((NG, QTS), np.float32)
    mg0[:, 0:DH] = NEG
    cns["cst_maskg0T"] = mg0
    cns["cst_ident"] = np.eye(128, dtype=ml_dtypes.bfloat16)
    cns["cst_ones128"] = np.ones((128, 128), ml_dtypes.bfloat16)
    sel = np.zeros((128, H * QTS), np.float32)
    for h in range(H):
        sel[h, h * QTS:(h + 1) * QTS] = -1.0
        sel[64 + h, h * QTS:(h + 1) * QTS] = -1.0
    sel[32, 0:QTS] = -1.0
    cns["cst_sel"] = sel.astype(ml_dtypes.bfloat16)
    cns["cst_ones"] = np.ones((1, 128), ml_dtypes.bfloat16)
    return cns


def build_nc(T, ncores=1):
    nchunks = T // CHUNK
    nc = bass.Bass("TRN2", target_bir_lowering=False)

    x_h = nc.dram_tensor("x", [T, D], F32, kind="ExternalInput").ap()
    out_h = nc.dram_tensor("out", [T, D], F32, kind="ExternalOutput").ap()
    wq_h = nc.dram_tensor("wq", [D, D], F32, kind="ExternalInput").ap()
    wk_h = nc.dram_tensor("wk", [D, D], F32, kind="ExternalInput").ap()
    wv_h = nc.dram_tensor("wv", [D, D], F32, kind="ExternalInput").ap()
    wo_h = nc.dram_tensor("wo", [D, D], F32, kind="ExternalInput").ap()
    wr_h = nc.dram_tensor("wr_sl", [D, 128], F32, kind="ExternalInput").ap()
    wi_h = nc.dram_tensor("wi_sl", [D, 128], F32, kind="ExternalInput").ap()
    csl_h = nc.dram_tensor("cst_c_sl", [128, D], BF16, kind="ExternalInput").ap()
    ssl_h = nc.dram_tensor("cst_s_sl", [128, D], BF16, kind="ExternalInput").ap()
    mpart_h = nc.dram_tensor("mpart", [D, D], BF16, kind="Internal").ap()
    mfull_h = nc.dram_tensor("mfull", [D, D], BF16, kind="Internal").ap()
    bias_h = nc.dram_tensor("fbias", [1, D], F32, kind="ExternalInput").ap()
    C_h = nc.dram_tensor("cst_C", [D, D], BF16, kind="ExternalInput").ap()
    S_h = nc.dram_tensor("cst_S", [D, D], BF16, kind="ExternalInput").ap()
    mn2_h = nc.dram_tensor("cst_maskneg2T", [QTS, QTS], F32, kind="ExternalInput").ap()
    m4_h = nc.dram_tensor("cst_mask4", [QTS, 4 * QTS], BF16, kind="ExternalInput").ap()
    mg0_h = nc.dram_tensor("cst_maskg0T", [NG, QTS], F32, kind="ExternalInput").ap()
    o128_h = nc.dram_tensor("cst_ones128", [128, 128], BF16, kind="ExternalInput").ap()
    id_h = nc.dram_tensor("cst_ident", [128, 128], BF16, kind="ExternalInput").ap()
    sel_h = nc.dram_tensor("cst_sel", [128, H * QTS], BF16, kind="ExternalInput").ap()
    ones_h = nc.dram_tensor("cst_ones", [1, 128], BF16, kind="ExternalInput").ap()

    def sbuf(name, shape, dt):
        return nc.alloc_sbuf_tensor(name, shape, dt).ap()

    wqt = sbuf("wqt", [128, 8 * D], BF16)
    wkt = sbuf("wkt", [128, 8 * D], BF16)
    wvt = sbuf("wvt", [128, 8 * D], BF16)
    wot = sbuf("wot", [128, 8 * D], BF16)
    Msb = sbuf("Msb", [128, 8 * D], BF16)
    maskn2T = sbuf("maskn2T", [QTS, QTS], F32)
    maskg0T = sbuf("maskg0T", [NG, QTS], F32)
    allones = sbuf("allones", [128, 128], BF16)
    ident = sbuf("ident", [128, 128], BF16)
    ones_r = sbuf("ones_r", [1, 128], BF16)
    bias_b = sbuf("bias_b", [1, D], BF16)
    kg = sbuf("kg", [128, 8 * NG], BF16)
    vg = sbuf("vg", [48, D], BF16)

    def pair(name, shape, dt):
        return [sbuf(name + "0", shape, dt), sbuf(name + "1", shape, dt)]

    mcol_p = pair("mcol", [128, 1], F32)
    rcol_p = pair("rcol", [128, 1], F32)
    ncol_p = pair("ncol", [128, 1], F32)
    scol_p = pair("scol", [128, 1], F32)
    s2col_p = pair("s2col", [128, 1], F32)
    rcol1_p = pair("rcol1", [128, 1], F32)
    ncol1_p = pair("ncol1", [128, 1], F32)
    bnb_p = pair("bnb", [128, 12], F32)
    mv_p = pair("mv", [128, 2], F32)
    bnb1_p = pair("bnb1", [128, 12], F32)
    mv1_p = pair("mv1", [128, 2], F32)
    epsc = sbuf("epsc", [128, 1], F32)

    AREN = 115 * 1024
    arena = nc.alloc_sbuf_tensor("arena", [128, AREN], U8)
    ar_f = arena.bitcast(F32).ap()
    ar_b = arena.bitcast(BF16).ap()

    off = [0]
    extents = []

    def carve(nbytes, align=64):
        off[0] = (off[0] + align - 1) // align * align
        o = off[0]
        off[0] += nbytes
        assert off[0] <= AREN, f"arena overflow {off[0]}"
        extents.append((o, off[0]))
        return o

    def cf(cols):
        o = carve(cols * 4)
        return ar_f[:, o // 4: o // 4 + cols]

    def cb(cols):
        o = carve(cols * 2)
        return ar_b[:, o // 2: o // 2 + cols]

    extA = {}

    def mark(name):
        extA.setdefault(name, extents[-1])
    offA = off[0]
    C_sb = cb(8 * D)
    mark("C")
    S_sb = cb(8 * D)
    mark("S")
    Sn_sb = S_sb
    wrsl_sb = cb(8 * 128)
    mark("wrsl")
    wisl_sb = cb(8 * 128)
    mark("wisl")
    csl_sb = cb(D)
    mark("csl")
    ssl_sb = cb(D)
    mark("ssl")
    PTr_sb = cb(D)
    mark("PTr")
    QTr_sb = cb(D)
    mark("QTr")
    Mp_sb = cb(8 * D)
    mark("Mp")
    Wst_sb = cb(8 * D)
    mark("WstA")
    Wst2_sb = cb(8 * D)
    mark("WstB")
    Wfo_sb = [cf(2 * D), cf(2 * D)]
    mark("WfoB")
    extA["WfoA"] = extents[-2]
    end_A = off[0]

    off[0] = offA
    extM = {}

    def markM(keys):
        extM[extents[-1]] = keys
    x_p = [cf(4 * D), cf(4 * D)]
    markM([f"x1_{q}" for q in range(4)])
    extM[extents[-2]] = [f"x0_{q}" for q in range(4)]
    xm_p = [cf(D), cf(D)]
    markM(["xm1"])
    extM[extents[-2]] = ["xm0"]
    h_tm = cb(4 * D)
    markM([f"h{q}{s}" for q in range(4) for s in "ab"])
    h_fm = cb(4 * D)
    markM([f"hfm{j}_{q}" for j in range(8) for q in range(4)])
    q_fm = cb(4 * D)
    markM([f"q_fm{i}" for i in range(8)])
    k_fm = cb(4 * D)
    markM([f"k_fm{i}" for i in range(8)])
    v_tm = cb(4 * D)
    markM([f"v{q}" for q in range(4)])
    ao_fm = cb(4 * D)
    markM([f"ao{g}_{q}" for g in range(2) for q in range(4)])
    p2_p = [cb(8 * QTS), cb(8 * QTS)]
    markM(["p2a0_1", "p2b0_1"])
    extM[extents[-2]] = ["p2a0_0", "p2b0_0"]
    p2B_p = [cb(8 * QTS), cb(8 * QTS)]
    markM(["p2a1_1", "p2b1_1"])
    extM[extents[-2]] = ["p2a1_0", "p2b1_0"]
    p2g_p = [cb(512), cb(512)]
    markM(["p2g_1"])
    extM[extents[-2]] = ["p2g_0"]
    z_p = [cb(D), cb(D)]
    markM(["z1"])
    extM[extents[-2]] = ["z0"]
    rec_p = [cf(512), cf(512)]
    markM(["rec1"])
    extM[extents[-2]] = ["rec0"]
    junk = cb(D)
    markM(["junk"])
    end_main = off[0]
    assert max(end_A, end_main) <= AREN, (end_A, end_main)

    psA = nc.alloc_psum_tensor("psA", [128, 4096], F32)
    ps_f = psA.ap()
    ps_b = psA.bitcast(BF16).ap()

    ATOF = [0, 2048]
    WOBANKS = [["ps0", "ps1"], ["ps4", "ps5"]]
    PARKA_BF = [2688, 5760]
    def parkA(pb):
        return ps_b[:, PARKA_BF[pb]: PARKA_BF[pb] + 128]

    P = Prog(nc)
    ST = P.step
    sems = {}

    def ld(dst, src, queue, tag, writes, reads=()):
        def em(eng):
            return eng.dma_start(out=dst, in_=src).then_inc(
                sems["d_" + queue], 16)
        return ST({"syq": "sy", "poq": "po", "seq": "se",
                   "veq": "ve"}[queue], em,
                  reads=reads, writes=writes, dma_queue=queue, tag=tag)

    ld(maskn2T, mn2_h, "syq", "ld_mn2t", ["maskn2T"])
    ld(maskg0T, mg0_h, "syq", "ld_mg0t", ["maskg0T"])

    def em_eps(v):
        return v.memset(epsc, 1e-5)
    ST("ve", em_eps, reads=[], writes=["epsc"], tag="epsc")
    ld(allones, o128_h, "syq", "ld_o128", ["allones"])
    ld(ident, id_h, "syq", "ld_id", ["ident"])
    ld(ones_r, ones_h, "syq", "ld_ones", ["ones_r"])
    ld(bias_b, bias_h, "poq", "ld_bias", ["bias_b"])

    def chunks(t):
        return t.rearrange("p (c d) -> p c d", d=D)

    ld(chunks(C_sb), C_h.rearrange("(j p) m -> p j m", p=128),
       "syq", "ld_C", [f"C{jc}" for jc in range(8)])
    ld(chunks(S_sb), S_h.rearrange("(j p) m -> p j m", p=128),
       "syq", "ld_S", [f"S{jc}" for jc in range(8)])
    ld(wrsl_sb.rearrange("p (b m) -> p b m", m=128),
       wr_h.rearrange("(b p) m -> p b m", p=128), "poq", "ld_wrsl", ["wrsl"])
    ld(wisl_sb.rearrange("p (b m) -> p b m", m=128),
       wi_h.rearrange("(b p) m -> p b m", p=128), "poq", "ld_wisl", ["wisl"])
    ld(csl_sb, csl_h, "syq", "ld_csl", ["csl"])
    ld(ssl_sb, ssl_h, "syq", "ld_ssl", ["ssl"])

    def sneg_steps():
        for jc in range(8):
            def em(eng, jc=jc):
                return eng.activation(chunks(Sn_sb)[:, jc, :],
                                      chunks(S_sb)[:, jc, :], AF.Copy,
                                      scale=-1.0)
            ST("se", em, reads=[f"S{jc}"], writes=[f"S{jc}", f"Sn{jc}"],
               tag=f"sneg{jc}")

    ck = [f"C{j}" for j in range(8)]
    sk = [f"S{j}" for j in range(8)]
    snk = [f"Sn{j}" for j in range(8)]

    def shard_chain(dst_sb, dkey, l_a, ka, l_b, kb, r_a, r_b, rka, rkb,
                    scale, tag):
        def em_mm(eng):
            last = None
            for ah in range(2):
                for b in range(8):
                    last = eng.matmul(
                        ps_f[:, 512 * ah:512 * ah + 512],
                        l_a.rearrange("p (b m) -> p b m", m=128)[:, b, :],
                        chunks(r_a)[:, b, 512 * ah:512 * (ah + 1)],
                        start=(b == 0), stop=False)
                for b in range(8):
                    last = eng.matmul(
                        ps_f[:, 512 * ah:512 * ah + 512],
                        l_b.rearrange("p (b m) -> p b m", m=128)[:, b, :],
                        chunks(r_b)[:, b, 512 * ah:512 * (ah + 1)],
                        start=False, stop=(b == 7))
            return last
        ST("te", em_mm, reads=[ka, kb] + rka + rkb,
           writes=["ps0", "ps1"], tag=f"{tag}_mm")

        def em_ev(eng):
            return eng.activation(dst_sb, ps_f[:, 0:1024], AF.Copy,
                                  scale=scale)
        ST("se", em_ev, reads=["ps0", "ps1"], writes=[dkey],
           tag=f"{tag}_ev")

    shard_chain(PTr_sb, "PTr", wrsl_sb, "wrsl", wisl_sb, "wisl",
                C_sb, S_sb, ck, sk, 1.0, "PTr")
    sneg_steps()

    def em_snl(eng):
        return eng.activation(ssl_sb, ssl_sb, AF.Copy, scale=-1.0)
    ST("se", em_snl, reads=["ssl"], writes=["ssl", "snl"], tag="snl")

    shard_chain(QTr_sb, "QTr", wisl_sb, "wisl", wrsl_sb, "wrsl",
                C_sb, Sn_sb, ck, snk, 1.0, "QTr")

    for tc in range(0, 8, 2):
        def em_mm(eng, tc=tc):
            last = None
            for k in range(4):
                t2 = tc + k // 2
                ah = k % 2
                eng.matmul(ps_f[:, 512 * k:512 * k + 512],
                           PTr_sb[:, 128 * t2:128 * (t2 + 1)],
                           csl_sb[:, 512 * ah:512 * (ah + 1)],
                           start=True, stop=False)
                last = eng.matmul(ps_f[:, 512 * k:512 * k + 512],
                                  QTr_sb[:, 128 * t2:128 * (t2 + 1)],
                                  ssl_sb[:, 512 * ah:512 * (ah + 1)],
                                  start=False, stop=True)
            return last
        ST("te", em_mm, reads=["PTr", "QTr", "csl", "snl"],
           writes=["ps0", "ps1", "ps2", "ps3"], tag=f"Mp_mm{tc}")

        def em_ev(eng, tc=tc):
            return eng.activation(
                chunks(Mp_sb).transpose([0, 1, 2])[:, tc:tc + 2, :]
                .rearrange("p a b -> p (a b)"),
                ps_f[:, 0:2048], AF.Copy, scale=1.0 / D)
        ST("se", em_ev, reads=["ps0", "ps1", "ps2", "ps3"],
           writes=[f"Mp{tc}", f"Mp{tc + 1}"], tag=f"Mp_ev{tc}")

    ld(mpart_h.rearrange("(tc p) u -> p tc u", p=128),
       Mp_sb.rearrange("p (tc u) -> p tc u", u=D), "syq", "mp_out",
       ["mpartd"], reads=[f"Mp{j}" for j in range(8)])
    if ncores > 1:
        def em_cc(eng):
            return eng.collective_compute(
                "AllReduce", ALU.add,
                replica_groups=[list(range(ncores))],
                ins=[mpart_h.opt()], outs=[mfull_h.opt()])
        ST("po", em_cc, reads=["mpartd"], writes=["mfulld"],
           tag="allreduce")
    else:
        ld(mfull_h, mpart_h, "poq", "mp_copy", ["mfulld"], reads=["mpartd"])
    ld(chunks(Msb), mfull_h.rearrange("(j p) m -> p j m", p=128),
       "syq", "ld_M", [f"M{jc}" for jc in range(8)], reads=["mfulld"])

    for (w_hbm, wt_sb, nm, st_sb, sk, q) in (
            (wq_h, wqt, "q", Wst_sb, "WstA", "poq"),
            (wk_h, wkt, "k", Wst2_sb, "WstB", "poq"),
            (wv_h, wvt, "v", Wst_sb, "WstA", "poq"),
            (wo_h, wot, "o", Wst2_sb, "WstB", "poq")):
        if nm == "o":
            wo_src = wo_h.rearrange("(i p) m -> p i m", p=128)
            for qtr in range(4):
                wf = Wfo_sb[qtr % 2]
                wfv = wf.rearrange("p (i m) -> p i m", m=D)
                ld(wfv, wo_src[:, 2 * qtr:2 * qtr + 2, :], "seq",
                   f"ldwo_{qtr}", [f"Wfo{qtr % 2}"])
                for k2 in range(2):
                    ic = 2 * qtr + k2
                    if k2 == 0:
                        def em_cast(v, ic=ic, k2=k2, wfv=wfv):
                            return v.tensor_copy(
                                chunks(Wst2_sb)[:, ic, :], wfv[:, k2, :])
                        ST("ve", em_cast, reads=[f"Wfo{qtr % 2}"],
                           writes=[f"WstB{ic}"], tag=f"wcast_o{ic}")
                    else:
                        def em_cast(se, ic=ic, k2=k2, wfv=wfv):
                            return se.copy(
                                chunks(Wst2_sb)[:, ic, :], wfv[:, k2, :])
                        ST("se", em_cast, reads=[f"Wfo{qtr % 2}"],
                           writes=[f"WstB{ic}"], tag=f"wcast_o{ic}")
        else:
            ld(chunks(st_sb), w_hbm.rearrange("(i p) m -> p i m", p=128),
               q, f"ldw_{nm}", [f"{sk}{ic}" for ic in range(8)])
        for ic in range(8):
            for jc in range(8):
                pb = (ic * 8 + jc) % 2

                def em_t(eng, ic=ic, jc=jc, pb=pb, st_sb=st_sb):
                    return eng.transpose(
                        parkA(pb),
                        chunks(st_sb)[:, ic, 128 * jc:128 * (jc + 1)], ident)
                ST("te", em_t, reads=[f"{sk}{ic}", "ident"],
                   writes=["ps2" if pb == 0 else "ps5"], tag=f"wT_{nm}{ic}_{jc}")

                def em_e(eng, wt_sb=wt_sb, ic=ic, jc=jc, pb=pb):
                    return eng.tensor_copy(
                        chunks(wt_sb)[:, jc, 128 * ic:128 * (ic + 1)],
                        parkA(pb))
                ST("ve", em_e, reads=["ps2" if pb == 0 else "ps5"], writes=[f"wt_{nm}{jc}"],
                   tag=f"wTe_{nm}{ic}_{jc}")

    barrier = P.barrier_deps()
    for mkeys in extM.values():
        for k in mkeys:
            P.seed(k, barrier)

    x_src = x_h.rearrange("(c q p) d -> c p q d", p=128, q=4)
    out_dst = out_h.rearrange("(c q p) d -> c p q d", p=128, q=4)
    hfmv = h_fm.rearrange("p (j t) -> p j t", t=CHUNK)
    qfmv = q_fm.rearrange("p (i t) -> p i t", t=CHUNK)
    kfmv = k_fm.rearrange("p (i t) -> p i t", t=CHUNK)
    aov = ao_fm.rearrange("p (r t) -> p r t", t=CHUNK)
    kgv = kg.rearrange("p (j g) -> p j g", g=NG)
    vtv = v_tm.rearrange("p (q i) -> p q i", i=D)
    xv_p = [x_p[0].rearrange("p (q d) -> p q d", d=D),
            x_p[1].rearrange("p (q d) -> p q d", d=D)]
    hv = h_tm.rearrange("p (q d) -> p q d", d=D)

    def qslice(hh, qt, a=0, b=QTS):
        return qfmv[:, hh // 2, 128 * qt + a:128 * qt + b][
            64 * (hh % 2):64 * (hh % 2) + 64, :]

    def kslice(hh, qt, a=0, b=QTS):
        return kfmv[:, hh // 2, 128 * qt + a:128 * qt + b][
            64 * (hh % 2):64 * (hh % 2) + 64, :]

    def kgsl(hh):
        return kgv[:, hh // 2, :][64 * (hh % 2):64 * (hh % 2) + 64, :]

    def ln_stats(src_ap, src_keys, pp, tag, own=False):
        if own:
            bnb, mv = bnb1_p[pp], mv1_p[pp]
            rcol, ncol = rcol1_p[pp], ncol1_p[pp]
            sx = "b"
        else:
            bnb, mv = bnb_p[pp], mv_p[pp]
            rcol, ncol = rcol_p[pp], ncol_p[pp]
            sx = ""
        K = lambda n: n + sx + str(pp)

        def em_bs(v):
            v.bn_stats(bnb[:, 0:6], src_ap[:, 0:512])
            return v.bn_stats(bnb[:, 6:12], src_ap[:, 512:1024])
        ST("ve", em_bs, reads=src_keys, writes=[K("bnb")], tag=f"{tag}_bs")

        def em_ba(v):
            return v.bn_aggr(mv, bnb)
        ST("ve", em_ba, reads=[K("bnb")], writes=[K("mv")],
           tag=f"{tag}_ba")

        def em_rr(se):
            se.activation(ncol, mv[:, 1:2], AF.Ln, bias=epsc[:, 0:1])
            return se.activation(rcol, ncol, AF.Exp, scale=-0.5)
        ST("se", em_rr, reads=[K("mv"), "epsc"],
           writes=[K("rcol"), K("ncol")], tag=f"{tag}_rr")

    def transpose_htm(qt, tag):
        pp = qt % 2
        bk = WOBANKS[pp] + WOBANKS[1 - pp]
        c0 = [2 * ATOF[pp], 2 * ATOF[pp] + 1024,
              2 * ATOF[1 - pp], 2 * ATOF[1 - pp] + 1024]

        def park(i):
            return ps_b[:, c0[i]: c0[i] + 128]
        for jc in range(8):
            pb = jc % 4

            def em_t(eng, qt=qt, jc=jc, pb=pb, park=park):
                return eng.transpose(
                    park(pb),
                    h_tm.rearrange("p (q j d) -> p q j d", q=4, j=8)[:, qt, jc, :],
                    ident)
            ST("te", em_t, reads=[f"h{qt}{'ab'[jc // 4]}", "ident"],
               writes=[bk[pb]], tag=f"{tag}T_{qt}_{jc}")

            if pb % 2 == 0:
                def em_e(eng, qt=qt, jc=jc, pb=pb, park=park):
                    return eng.tensor_copy(
                        hfmv[:, jc, 128 * qt:128 * qt + 128], park(pb))
                ST("ve", em_e, reads=[bk[pb]], writes=[f"hfm{jc}_{qt}"],
                   tag=f"{tag}Te_{qt}_{jc}")
            else:
                def em_e(eng, qt=qt, jc=jc, pb=pb, park=park):
                    return eng.copy(
                        hfmv[:, jc, 128 * qt:128 * qt + 128], park(pb))
                ST("se", em_e, reads=[bk[pb]], writes=[f"hfm{jc}_{qt}"],
                   tag=f"{tag}Te_{qt}_{jc}")

    def stage_ln1(c, qt):
        pp = qt % 2
        cp = c % 2
        xq = xv_p[cp][:, qt, :]
        hq = hv[:, qt, :]
        ln_stats(xq, [f"x{cp}_{qt}"], pp, f"ln1_{c}_{qt}", own=True)

        for hf in range(2):
            def em_n1(v, xq=xq, hq=hq, pp=pp, hf=hf):
                return v.tensor_scalar(hq[:, 512 * hf:512 * (hf + 1)],
                                       xq[:, 512 * hf:512 * (hf + 1)],
                                       mv1_p[pp][:, 0:1],
                                       rcol1_p[pp][:, 0:1],
                                       op0=ALU.subtract, op1=ALU.mult)
            ST("ve", em_n1,
               reads=[f"x{cp}_{qt}", f"mvb{pp}", f"rcolb{pp}"],
               writes=[f"h{qt}{'ab'[hf]}"], tag=f"n1_{c}_{qt}{hf}")
        transpose_htm(qt, f"h1_{c}")

    def stage_qkv(c):
        hfm_keys = [f"hfm{j}_{q}" for j in range(8) for q in range(4)]
        rot = [("ps0", 0), ("ps1", 512), ("ps3", 1536), ("ps4", 2048)]
        rr = [0]
        for (wt_sb, wtk, dst, dk, scale) in (
                (wqt, "q", q_fm, "q_fm", 1.0), (wkt, "k", k_fm, "k_fm", 0.125)):
            for ic0 in range(0, 8, 2):
                pk = [rot[rr[0] % 4], rot[(rr[0] + 1) % 4]]
                rr[0] += 2

                def em_mm(eng, wt_sb=wt_sb, ic0=ic0, pk=pk):
                    last = None
                    for k in range(2):
                        ic = ic0 + k
                        c0 = pk[k][1]
                        for jc in range(8):
                            last = eng.matmul(
                                ps_f[:, c0:c0 + 512],
                                chunks(wt_sb)[:, jc, 128 * ic:128 * (ic + 1)],
                                hfmv[:, jc, :],
                                start=(jc == 0), stop=(jc == 7))
                    return last
                ST("te", em_mm,
                   reads=hfm_keys + [f"wt_{wtk}{j}" for j in range(8)],
                   writes=[pk[0][0], pk[1][0]], tag=f"{dk}_mm{c}_{ic0}")

                for k in range(2):
                    def em_ev(eng, dst=dst, ic=ic0 + k, c0=pk[k][1],
                              scale=scale):
                        return eng.tensor_scalar(
                            dst.rearrange("p (i t) -> p i t", t=CHUNK)[:, ic, :],
                            ps_f[:, c0:c0 + 512], scale, None, op0=ALU.mult)
                    ST("ve", em_ev, reads=[pk[k][0]], writes=[f"{dk}{ic0 + k}"],
                       tag=f"{dk}_ev{c}_{ic0 + k}")
        for qt in range(4):
            for ih in range(2):
                bk, c0 = rot[rr[0] % 4]
                rr[0] += 1

                def em_mm(eng, qt=qt, ih=ih, c0=c0):
                    last = None
                    for jc in range(8):
                        last = eng.matmul(
                            ps_f[:, c0:c0 + 512],
                            hfmv[:, jc, 128 * qt:128 * qt + 128],
                            chunks(wvt)[:, jc, 512 * ih:512 * (ih + 1)],
                            start=(jc == 0), stop=(jc == 7))
                    return last
                ST("te", em_mm,
                   reads=[f"hfm{j}_{qt}" for j in range(8)]
                   + [f"wt_v{j}" for j in range(8)],
                   writes=[bk], tag=f"v_mm{c}_{qt}_{ih}")

                def em_ev(eng, qt=qt, ih=ih, c0=c0):
                    return eng.tensor_copy(
                        vtv[:, qt, 512 * ih:512 * (ih + 1)],
                        ps_f[:, c0:c0 + 512])
                ST("ve", em_ev, reads=[bk], writes=[f"v{qt}"],
                   tag=f"v_ev{c}_{qt}_{ih}")

    def stage_kgvg():
        def em_kg(v):
            return v.tensor_copy(kgv, kfmv[:, :, 0:NG])
        ST("ve", em_kg, reads=[f"k_fm{j}" for j in range(8)],
           writes=["kg"], tag="kg")

        def em_vg(v):
            v.tensor_copy(vg[0:NG, :], vtv[0:NG, 0, :])
            return v.tensor_copy(vg[32:32 + NG, :], vtv[0:NG, 0, :])
        ST("ve", em_vg, reads=["v0"], writes=["vg"], tag="vg")

    def stage_p2(c, qt):
        pp = qt % 2
        OF = ATOF[pp]
        B = [f"ps{4 * pp + i}" for i in range(4)]
        p2g, rec = p2g_p[pp], rec_p[pp]
        o_q = [OF, OF + 512]
        o_g = OF + 1024
        o_s = OF + 1536
        for g in range(2):
            h0 = 8 * g
            p2 = (p2_p if g == 0 else p2B_p)[pp]
            KA, KB = f"p2a{g}_{pp}", f"p2b{g}_{pp}"

            for quarter in range(2):
                def em_raw(eng, qt=qt, h0=h0, quarter=quarter):
                    last = None
                    for i in range(4):
                        hh = h0 + 2 * i + quarter
                        o = o_q[quarter] + i * QTS
                        last = eng.matmul(ps_f[:, o:o + QTS], kslice(hh, qt),
                                          qslice(hh, qt), start=(i == 0),
                                          stop=(i == 3))
                    return last
                ST("te", em_raw,
                   reads=[f"k_fm{j}" for j in range(8)]
                   + [f"q_fm{j}" for j in range(8)],
                   writes=[B[quarter]], tag=f"s2_{c}_{qt}_{g}{quarter}")

            for quarter in range(2):
                def em_mn(v, quarter=quarter):
                    o = o_q[quarter]
                    return v.tensor_tensor(
                        ps_f[:, o:o + 512].rearrange("p (h k) -> p h k", k=QTS),
                        ps_f[:, o:o + 512].rearrange("p (h k) -> p h k", k=QTS),
                        maskn2T.unsqueeze(1).broadcast_to([QTS, 4, QTS]),
                        op=ALU.add)
                ST("ve", em_mn, reads=[B[quarter], "maskn2T"],
                   writes=[B[quarter]], tag=f"mn_{c}_{qt}_{g}{quarter}")

            def em_graw(eng, qt=qt, h0=h0):
                last = None
                for par in range(2):
                    pb = 32 * par
                    for i in range(4):
                        hh = h0 + 2 * i + par
                        o = o_g + i * QTS
                        last = eng.matmul(ps_f[pb:pb + 16, o:o + QTS],
                                          kgsl(hh), qslice(hh, qt),
                                          start=(i == 0), stop=(i == 3))
                return last
            ST("te", em_graw,
               reads=[f"q_fm{j}" for j in range(8)] + ["kg"],
               writes=[B[2]], tag=f"s2g_{c}_{qt}_{g}")

            if c == 0 and qt == 0:
                def em_mng(v):
                    v.tensor_tensor(
                        ps_f[0:16, o_g:o_g + 512].rearrange(
                            "p (h k) -> p h k", k=QTS),
                        ps_f[0:16, o_g:o_g + 512].rearrange(
                            "p (h k) -> p h k", k=QTS),
                        maskg0T.unsqueeze(1).broadcast_to([NG, 4, QTS]),
                        op=ALU.add)
                    return v.tensor_tensor(
                        ps_f[32:48, o_g:o_g + 512].rearrange(
                            "p (h k) -> p h k", k=QTS),
                        ps_f[32:48, o_g:o_g + 512].rearrange(
                            "p (h k) -> p h k", k=QTS),
                        maskg0T.unsqueeze(1).broadcast_to([NG, 4, QTS]),
                        op=ALU.add)
                ST("ve", em_mng, reads=[B[2], "maskg0T"], writes=[B[2]],
                   tag=f"mng_{c}_{qt}_{g}")

            for quarter in range(2):
                def em_expm(se, p2=p2, quarter=quarter):
                    return se.activation(
                        p2[:, 512 * quarter:512 * (quarter + 1)],
                        ps_f[:, o_q[quarter]:o_q[quarter] + 512], AF.Exp)
                ST("se", em_expm, reads=[B[quarter]],
                   writes=[[KA, KB][quarter]],
                   tag=f"expm_{c}_{qt}_{g}{quarter}")

            def em_expmg(se, p2g=p2g):
                se.activation(p2g[0:16, 0:512],
                              ps_f[0:16, o_g:o_g + 512], AF.Exp)
                return se.activation(p2g[32:48, 0:512],
                                     ps_f[32:48, o_g:o_g + 512], AF.Exp)
            ST("se", em_expmg, reads=[B[2]], writes=[f"p2g_{pp}"],
               tag=f"expmg_{c}_{qt}_{g}")

            def em_sum(eng, p2=p2, p2g=p2g):
                eng.matmul(ps_f[0:64, o_g:o_g + 512], allones[:, 0:64],
                           p2[:, 0:512], start=True, stop=False)
                eng.matmul(ps_f[0:64, o_g:o_g + 512], allones[0:16, 0:64],
                           p2g[0:16, 0:512], start=False, stop=False)
                eng.matmul(ps_f[64:128, o_g:o_g + 512], allones[:, 64:128],
                           p2[:, 512:1024], start=True, stop=False)
                return eng.matmul(ps_f[64:128, o_g:o_g + 512],
                                  allones[32:48, 64:128],
                                  p2g[32:48, 0:512], start=False, stop=True)
            ST("te", em_sum, reads=[KA, KB, f"p2g_{pp}",
                                    "allones", B[2]],
               writes=[B[2]], tag=f"sum_{c}_{qt}_{g}")

            def em_rec(se, rec=rec):
                se.activation(rec, ps_f[:, o_g:o_g + 512], AF.Ln)
                return se.activation(rec, rec, AF.Exp, scale=-1.0)
            ST("se", em_rec, reads=[B[2]], writes=[f"rec{pp}"],
               tag=f"rcp_{c}_{qt}_{g}")

            for par in range(2):
                def em_av(eng, qt=qt, h0=h0, p2=p2, p2g=p2g, par=par):
                    last = None
                    for i in range(par, 8, 2):
                        hh = h0 + i
                        pos = (i % 2) * 4 + i // 2
                        ob = o_s + (i // 2) * QTS
                        dst = ps_f[:, ob:ob + QTS][
                            64 * (hh % 2):64 * (hh % 2) + 64, :]
                        eng.matmul(
                            dst,
                            v_tm[:, qt * D + hh * DH: qt * D + hh * DH + DH],
                            p2[:, pos * QTS:(pos + 1) * QTS],
                            start=True, stop=False)
                        vb = 32 * (i % 2)
                        last = eng.matmul(
                            dst, vg[vb:vb + NG, hh * DH:(hh + 1) * DH],
                            p2g[vb:vb + NG,
                                (i // 2) * QTS:(i // 2) * QTS + QTS],
                            start=False, stop=True)
                    return last
                ST("te", em_av,
                   reads=[f"v{qt}", "vg", [KA, KB][par],
                          f"p2g_{pp}"],
                   writes=[B[3]], tag=f"av_{c}_{qt}_{g}{par}")

            def em_ave(v, qt=qt, g=g, rec=rec):
                return v.tensor_tensor(
                    aov[:, 4 * g:4 * g + 4, 128 * qt:128 * qt + 128],
                    ps_f[:, o_s:o_s + 512].rearrange(
                        "p (r t) -> p r t", t=QTS),
                    rec.rearrange("p (r t) -> p r t", t=QTS),
                    op=ALU.mult)
            ST("ve", em_ave, reads=[B[3], f"rec{pp}"], writes=[f"ao{g}_{qt}"],
               tag=f"ave_{c}_{qt}_{g}")

    def stage_w(c, qt):
        pp = qt % 2
        cp = c % 2
        of = ATOF[pp]
        bks = WOBANKS[pp]
        xq = xv_p[cp][:, qt, :]
        mcol, rcol, ncol = mcol_p[pp], rcol_p[pp], ncol_p[pp]
        scol, s2col = scol_p[pp], s2col_p[pp]
        xm_sb = xm_p[pp]

        def em_wo(eng, qt=qt, of=of):
            last = None
            for ih in range(2):
                for jc in range(8):
                    last = eng.matmul(
                        ps_f[:, of + 512 * ih:of + 512 * ih + 512],
                        aov[:, jc, 128 * qt:128 * qt + 128],
                        chunks(wot)[:, jc, 512 * ih:512 * (ih + 1)],
                        start=(jc == 0), stop=(jc == 7))
            return last
        ST("te", em_wo,
           reads=[f"ao{g}_{qt}" for g in range(2)]
           + [f"wt_o{j}" for j in range(8)],
           writes=bks, tag=f"wo_{c}_{qt}")

        z_sb = z_p[pp]

        def em_zev(se, of=of, scol=scol, z_sb=z_sb):
            return se.activation(z_sb, ps_f[:, of:of + 1024], AF.Copy,
                                 accum_out=scol)
        ST("se", em_zev, reads=bks + [], writes=[f"z{pp}", f"scol{pp}"],
           tag=f"lnz_ev_{c}_{qt}")

        def em_zsq(se, s2col=s2col, z_sb=z_sb):
            return se.activation(junk, z_sb, AF.Square, accum_out=s2col)
        ST("se", em_zsq, reads=[f"z{pp}"], writes=["junk", f"s2col{pp}"],
           tag=f"lnz_sq_{c}_{qt}")

        def em_mean(v, mcol=mcol, scol=scol):
            return v.tensor_scalar(mcol, scol, 1.0 / D, None, op0=ALU.mult)
        ST("ve", em_mean, reads=[f"scol{pp}"], writes=[f"mcol{pp}"],
           tag=f"lnz_m_{c}_{qt}")

        def em_m2c(v, mcol=mcol, ncol=ncol):
            return v.tensor_scalar(ncol, mcol, mcol[:, 0:1], -1e-5,
                                   op0=ALU.mult, op1=ALU.add)
        ST("ve", em_m2c, reads=[f"mcol{pp}"], writes=[f"ncol{pp}"],
           tag=f"lnz_m2_{c}_{qt}")

        def em_var(v, rcol=rcol, ncol=ncol, s2col=s2col):
            return v.tensor_scalar(rcol, s2col, 1.0 / D, ncol[:, 0:1],
                                   op0=ALU.mult, op1=ALU.subtract)
        ST("ve", em_var, reads=[f"s2col{pp}", f"ncol{pp}"],
           writes=[f"rcol{pp}"], tag=f"lnz_v_{c}_{qt}")

        def em_rr(se, rcol=rcol, ncol=ncol):
            se.activation(ncol, rcol, AF.Ln)
            return se.activation(rcol, ncol, AF.Exp, scale=-0.5)
        ST("se", em_rr, reads=[f"rcol{pp}"],
           writes=[f"rcol{pp}", f"ncol{pp}"], tag=f"lnz_rr_{c}_{qt}")

        def em_nmr(v, rcol=rcol, mcol=mcol, ncol=ncol):
            return v.tensor_scalar(ncol, rcol, mcol[:, 0:1], -1.0,
                                   op0=ALU.mult, op1=ALU.mult)
        ST("ve", em_nmr, reads=[f"rcol{pp}", f"mcol{pp}"],
           writes=[f"ncol{pp}"], tag=f"lnz_nmr_{c}_{qt}")

        def em_xm(v, xq=xq, ncol=ncol, xm_sb=xm_sb):
            return v.tensor_scalar(xm_sb, xq, ncol[:, 0:1], None, op0=ALU.add)
        ST("ve", em_xm, reads=[f"x{cp}_{qt}", f"ncol{pp}"],
           writes=[f"xm{pp}"], tag=f"lnz_xm_{c}_{qt}")

        def em_x2(v, xq=xq, rcol=rcol, xm_sb=xm_sb, z_sb=z_sb):
            return v.scalar_tensor_tensor(xq, z_sb, rcol[:, 0:1], xm_sb,
                                          op0=ALU.mult, op1=ALU.add)
        ST("ve", em_x2, reads=[f"z{pp}", f"rcol{pp}", f"xm{pp}"],
           writes=[f"x{cp}_{qt}"], tag=f"x2_{c}_{qt}")

    def stage_h2(c, qt):
        pp = qt % 2
        cp = c % 2
        xq = xv_p[cp][:, qt, :]
        hq = hv[:, qt, :]
        ln_stats(xq, [f"x{cp}_{qt}"], pp, f"ln2_{c}_{qt}")

        for hf in range(2):
            def em_n2(v, xq=xq, hq=hq, pp=pp, hf=hf):
                return v.tensor_scalar(hq[:, 512 * hf:512 * (hf + 1)],
                                       xq[:, 512 * hf:512 * (hf + 1)],
                                       mv_p[pp][:, 0:1],
                                       rcol_p[pp][:, 0:1],
                                       op0=ALU.subtract, op1=ALU.mult)
            ST("ve", em_n2,
               reads=[f"x{cp}_{qt}", f"mv{pp}", f"rcol{pp}"],
               writes=[f"h{qt}{'ab'[hf]}"], tag=f"n2_{c}_{qt}{hf}")
        transpose_htm(qt, f"h2_{c}")

    def stage_f(c, qt):
        pp = qt % 2
        cp = c % 2
        of = ATOF[pp]
        bks = WOBANKS[pp]
        xq = xv_p[cp][:, qt, :]
        mcol, rcol, ncol = mcol_p[pp], rcol_p[pp], ncol_p[pp]
        scol, s2col = scol_p[pp], s2col_p[pp]
        xm_sb = xm_p[pp]

        def em_ffn(eng, qt=qt, of=of):
            last = None
            for ih in range(2):
                for jc in range(8):
                    eng.matmul(
                        ps_f[:, of + 512 * ih:of + 512 * ih + 512],
                        hfmv[:, jc, 128 * qt:128 * qt + 128],
                        chunks(Msb)[:, jc, 512 * ih:512 * (ih + 1)],
                        start=(jc == 0), stop=False)
                last = eng.matmul(
                    ps_f[:, of + 512 * ih:of + 512 * ih + 512], ones_r,
                    bias_b[:, 512 * ih:512 * (ih + 1)],
                    start=False, stop=True)
            return last
        ST("te", em_ffn,
           reads=[f"hfm{j}_{qt}" for j in range(8)]
           + [f"M{j}" for j in range(8)] + ["ones_r", "bias_b"],
           writes=bks, tag=f"ffn_{c}_{qt}")

        z_sb = z_p[pp]

        def em_yev(se, of=of, scol=scol, z_sb=z_sb):
            return se.activation(z_sb, ps_f[:, of:of + 1024], AF.Copy,
                                 accum_out=scol)
        ST("se", em_yev, reads=bks, writes=[f"z{pp}", f"scol{pp}"],
           tag=f"lny_ev_{c}_{qt}")

        def em_ysq(se, s2col=s2col, z_sb=z_sb):
            return se.activation(junk, z_sb, AF.Square, accum_out=s2col)
        ST("se", em_ysq, reads=[f"z{pp}"], writes=["junk", f"s2col{pp}"],
           tag=f"lny_sq_{c}_{qt}")

        def em_mean(v, mcol=mcol, scol=scol):
            return v.tensor_scalar(mcol, scol, 1.0 / D, None, op0=ALU.mult)
        ST("ve", em_mean, reads=[f"scol{pp}"], writes=[f"mcol{pp}"],
           tag=f"lny_m_{c}_{qt}")

        def em_m2c(v, mcol=mcol, ncol=ncol):
            return v.tensor_scalar(ncol, mcol, mcol[:, 0:1], -1e-5,
                                   op0=ALU.mult, op1=ALU.add)
        ST("ve", em_m2c, reads=[f"mcol{pp}"], writes=[f"ncol{pp}"],
           tag=f"lny_m2_{c}_{qt}")

        def em_var(v, rcol=rcol, ncol=ncol, s2col=s2col):
            return v.tensor_scalar(rcol, s2col, 1.0 / D, ncol[:, 0:1],
                                   op0=ALU.mult, op1=ALU.subtract)
        ST("ve", em_var, reads=[f"s2col{pp}", f"ncol{pp}"],
           writes=[f"rcol{pp}"], tag=f"lny_v_{c}_{qt}")

        def em_rr(se, rcol=rcol, ncol=ncol):
            se.activation(ncol, rcol, AF.Ln)
            return se.activation(rcol, ncol, AF.Exp, scale=-0.5)
        ST("se", em_rr, reads=[f"rcol{pp}"],
           writes=[f"rcol{pp}", f"ncol{pp}"], tag=f"lny_rr_{c}_{qt}")

        def em_nmr(v, rcol=rcol, mcol=mcol, ncol=ncol):
            return v.tensor_scalar(ncol, rcol, mcol[:, 0:1], -1.0,
                                   op0=ALU.mult, op1=ALU.mult)
        ST("ve", em_nmr, reads=[f"rcol{pp}", f"mcol{pp}"],
           writes=[f"ncol{pp}"], tag=f"lny_nmr_{c}_{qt}")

        def em_xm(v, xq=xq, ncol=ncol, xm_sb=xm_sb):
            return v.tensor_scalar(xm_sb, xq, ncol[:, 0:1], None, op0=ALU.add)
        ST("ve", em_xm, reads=[f"x{cp}_{qt}", f"ncol{pp}"],
           writes=[f"xm{pp}"], tag=f"lny_xm_{c}_{qt}")

        def em_out(v, xq=xq, rcol=rcol, xm_sb=xm_sb, z_sb=z_sb):
            return v.scalar_tensor_tensor(xq, z_sb, rcol[:, 0:1], xm_sb,
                                          op0=ALU.mult, op1=ALU.add)
        ST("ve", em_out, reads=[f"z{pp}", f"rcol{pp}", f"xm{pp}"],
           writes=[f"x{cp}_{qt}"], tag=f"out_{c}_{qt}")

        def em_st(eng, c=c, qt=qt, xq=xq):
            return eng.dma_start(out=out_dst[c][:, qt, :], in_=xq).then_inc(
                sems["d_seq"], 16)
        ST("se", em_st, reads=[f"x{cp}_{qt}"], writes=[], dma_queue="seq",
           tag=f"st_{c}_{qt}")
        if c + 1 < nchunks:
            ncp = (c + 1) % 2
            ld(xv_p[ncp][:, qt, :], x_src[c + 1][:, qt, :], "syq",
               f"x_in{c + 1}_{qt}", [f"x{ncp}_{qt}"])

    for qt in range(4):
        ld(xv_p[0][:, qt, :], x_src[0][:, qt, :],
           "seq" if qt < 2 else "syq", f"x_in0_{qt}",
           [f"x0_{qt}"])
    for c in range(nchunks):
        if c == 0:
            for qt in range(4):
                stage_ln1(c, qt)
        stage_qkv(c)
        if c == 0:
            stage_kgvg()
        stage_p2(c, 0)
        stage_w(c, 0)
        stage_p2(c, 1)
        stage_h2(c, 0)
        stage_w(c, 1)
        stage_f(c, 0)
        stage_p2(c, 2)
        stage_h2(c, 1)
        stage_w(c, 2)
        stage_f(c, 1)
        stage_p2(c, 3)
        stage_h2(c, 2)
        stage_w(c, 3)
        stage_f(c, 2)
        if c + 1 < nchunks:
            stage_ln1(c + 1, 0)
            stage_ln1(c + 1, 1)

        stage_h2(c, 3)
        stage_f(c, 3)
        if c + 1 < nchunks:
            stage_ln1(c + 1, 2)
            stage_ln1(c + 1, 3)

    with (
        nc.semaphore("e_te") as s0, nc.semaphore("e_ve") as s1,
        nc.semaphore("e_se") as s2, nc.semaphore("e_po") as s3,
        nc.semaphore("e_sy") as s4, nc.semaphore("d_syq") as s5,
        nc.semaphore("d_poq") as s6, nc.semaphore("d_seq") as s7,
        nc.semaphore("d_ccq") as s8, nc.semaphore("d_veq") as s9,
        nc.Block() as block,
    ):
        sems.update({"e_te": s0, "e_ve": s1, "e_se": s2, "e_po": s3,
                     "e_sy": s4, "d_syq": s5, "d_poq": s6, "d_seq": s7,
                     "d_ccq": s8, "d_veq": s9})

        qcounts = [(q, n) for q, n in P.dma_count.items() if n]

        def em_fin(eng):
            for q, n in qcounts:
                eng.wait_ge(sems["d_" + q], 16 * n)
            return eng.nop()
        P.step("sy", em_fin, reads=[], writes=[], tag="final")
        P.finalize(block, sems)
    return nc


_NC_CACHE = {}


def kernel(L0_attn_hidden_states, L0_attn_Wq, L0_attn_Wk, L0_attn_Wv,
           L0_attn_Wo, L0_ffn_w_freq_real, L0_ffn_w_freq_imag, L0_ffn_bias):
    x = np.ascontiguousarray(np.asarray(L0_attn_hidden_states, np.float32))
    B, T, _ = x.shape
    if T not in _NC_CACHE:
        _NC_CACHE[T] = build_nc(T, B)
    nc = _NC_CACHE[T]
    cns = host_constants()
    base = {
        "wq": np.ascontiguousarray(np.asarray(L0_attn_Wq, np.float32)),
        "wk": np.ascontiguousarray(np.asarray(L0_attn_Wk, np.float32)),
        "wv": np.ascontiguousarray(np.asarray(L0_attn_Wv, np.float32)),
        "wo": np.ascontiguousarray(np.asarray(L0_attn_Wo, np.float32)),

        "fbias": np.ascontiguousarray(
            np.asarray(L0_ffn_bias, np.float32).reshape(1, D)),
    }
    base.update(cns)
    wr = np.asarray(L0_ffn_w_freq_real, np.float32)
    wi = np.asarray(L0_ffn_w_freq_imag, np.float32)
    Cb = np.asarray(cns["cst_C"])
    Sb = np.asarray(cns["cst_S"])
    in_maps = [dict(base, x=np.ascontiguousarray(x[b]),
                    wr_sl=np.ascontiguousarray(wr[:, 128 * b:128 * (b + 1)]),
                    wi_sl=np.ascontiguousarray(wi[:, 128 * b:128 * (b + 1)]),
                    cst_c_sl=np.ascontiguousarray(Cb[128 * b:128 * (b + 1), :]),
                    cst_s_sl=np.ascontiguousarray(Sb[128 * b:128 * (b + 1), :]))
               for b in range(B)]
    res = run_bass_kernel_spmd(nc, in_maps, core_ids=list(range(B)))
    out = np.stack([np.asarray(r["out"], np.float32) for r in res.results])
    return out



# version 45
# speedup vs baseline: 1.0341x; 1.0341x over previous
import numpy as np
import ml_dtypes
import concourse.bass as bass
import concourse.mybir as mybir
from concourse.bass_utils import run_bass_kernel_spmd

F32 = mybir.dt.float32
BF16 = mybir.dt.bfloat16
U8 = mybir.dt.uint8
AF = mybir.ActivationFunctionType
ALU = mybir.AluOpType
AX = mybir.AxisListType

D = 1024
H = 16
DH = 64
NG = 16
CHUNK = 512
QTS = 128
NEG = -1.0e30

ENGINES = ("te", "ve", "se", "po", "sy")
QDEPTH = 2


class Step:
    __slots__ = ("eng", "emit", "dma_queue", "done", "waits", "tag", "rec")

    def __init__(self, eng, emit, dma_queue, tag):
        self.eng = eng
        self.emit = emit
        self.dma_queue = dma_queue
        self.done = None
        self.waits = []
        self.tag = tag


class Prog:

    def __init__(self, nc):
        self.nc = nc
        self.steps = []
        self.eng_count = {e: 0 for e in ENGINES}
        self.dma_count = {}
        self.writer = {}
        self.readers = {}

    def step(self, eng, emit, reads=(), writes=(), dma_queue=None, tag=""):
        st = Step(eng, emit, dma_queue, tag)
        reads = list(reads)
        writes = list(writes)
        for k in list(reads):
            if k.startswith("ps"):
                reads.remove(k)
                writes.append(k)
        deps = []
        for k in reads:
            w = self.writer.get(k)
            if w is not None:
                deps.append(w)
        for k in writes:
            w = self.writer.get(k)
            if w is not None:
                deps.append(w)
            deps.extend(self.readers.get(k, ()))
        seen = set()
        for d in deps:
            if id(d) not in seen and d is not st:
                seen.add(id(d))
                st.waits.append(d)
        if dma_queue is None:
            self.eng_count[eng] += 1
            st.done = ("e_" + eng, self.eng_count[eng])
        else:
            n = self.dma_count.get(dma_queue, 0) + 1
            self.dma_count[dma_queue] = n
            st.done = ("d_" + dma_queue, 16 * n)
        for k in writes:
            self.writer[k] = st
            self.readers[k] = []
        for k in reads:
            self.readers.setdefault(k, []).append(st)
        self.steps.append(st)
        return st

    def barrier_deps(self):
        last = {}
        for st in self.steps:
            last[st.done[0]] = st
        return list(last.values())

    def seed(self, key, dep_steps):
        self.readers.setdefault(key, []).extend(dep_steps)

    def _cost(self, st):
        t = st.tag
        if st.dma_queue is not None:
            for pre, c in (("ldw_", 18.0), ("ld_C", 9.0), ("ld_S", 9.0),
                           ("x_in", 4.0), ("st_", 4.0), ("mp_out", 5.0),
                           ("ld_M", 9.0)):
                if t.startswith(pre):
                    return c
            return 1.5
        for pre, c in (("s2_", 0.9), ("s2g_", 1.8), ("sum_", 1.0),
                       ("mn_", 0.9), ("expmg", 0.6), ("expm", 0.6),
                       ("rcp_", 1.0), ("n1_", 0.5), ("n2_", 0.5),
                       ("av_", 1.1), ("ave_", 0.6), ("wo_", 4.1),
                       ("ffn_", 5.0), ("q_fm_mm", 3.6), ("k_fm_mm", 3.6),
                       ("v_mm", 1.9), ("q_fm_ev", 0.6), ("k_fm_ev", 0.6),
                       ("v_ev", 0.6), ("PTr_mm", 7.0), ("QTr_mm", 7.0), ("Mp_mm", 1.8),
                       ("PTr_ev", 0.6), ("QTr_ev", 0.6), ("Mp_ev", 0.6),
                       ("mtT", 0.3), ("lnz_ev", 1.0), ("lny_ev", 1.0),
                       ("wT_", 0.25), ("wTe_", 0.3), ("warm_", 3.5)):
            if t.startswith(pre):
                return c
        if "T_" in t and (t.startswith("h1_") or t.startswith("h2_")):
            return 0.25 if "Te_" not in t else 0.3
        if t.endswith("_bs"):
            return 0.9
        if "_sum" in t or "_sq" in t:
            return 1.3
        if "xm_" in t or "x2_" in t or "out_" in t or t.startswith("n1")  \
           or t.startswith("n2"):
            return 0.9
        return 0.2

    def schedule(self):
        import heapq
        nsteps = len(self.steps)
        for i, st in enumerate(self.steps):
            st.rec = i
        children = [[] for _ in range(nsteps)]
        nleft = [0] * nsteps
        for st in self.steps:
            deps = {id(d): d for d in st.waits}
            nleft[st.rec] = len(deps)
            for d in deps.values():
                children[d.rec].append(st)
        cp = [0.0] * nsteps
        for st in reversed(self.steps):
            m = 0.0
            for ch in children[st.rec]:
                m = max(m, 0.75 + cp[ch.rec])
            cp[st.rec] = self._cost(st) + m
        for st in self.steps:
            if st.tag.startswith("x_in0"):
                cp[st.rec] += 100.0
        lanes = list(ENGINES) + ["q_" + q for q in self.dma_count]
        eng_time = {e: 0.0 for e in lanes}
        pend = {e: [] for e in lanes}
        run = {e: [] for e in lanes}
        finish = [0.0] * nsteps
        ready_at = [0.0] * nsteps

        def lane(st):
            return st.eng if st.dma_queue is None else "q_" + st.dma_queue

        for st in self.steps:
            if nleft[st.rec] == 0:
                heapq.heappush(pend[lane(st)], (0.0, st.rec, st))
        order = {e: [] for e in lanes}
        n_done = 0
        while n_done < nsteps:
            best = None
            for e in lanes:
                while pend[e] and pend[e][0][0] <= eng_time[e]:
                    rt, rec, st = heapq.heappop(pend[e])
                    heapq.heappush(run[e], (-cp[rec], rec, st))
                cand = None
                if run[e]:
                    _, rec, st = run[e][0]
                    cand = (eng_time[e], rec, st, e, False)
                elif pend[e]:
                    rt, rec, st = pend[e][0]
                    cand = (rt, rec, st, e, True)
                if cand and (best is None or (cand[0], cand[1]) < (best[0], best[1])):
                    best = cand
            start, rec, st, e, from_pend = best
            if from_pend:
                heapq.heappop(pend[e])
            else:
                heapq.heappop(run[e])
            f = start + self._cost(st)
            eng_time[e] = f
            finish[rec] = f
            ready_at[rec] = start
            order[e].append(st)
            n_done += 1
            for ch in children[rec]:
                nleft[ch.rec] -= 1
                if nleft[ch.rec] == 0:
                    rt = 0.0
                    for d in ch.waits:
                        lat = 0.75 if (d.eng != ch.eng or d.dma_queue
                                      or ch.dma_queue) else 0.05
                        rt = max(rt, finish[d.rec] + lat)
                    heapq.heappush(pend[lane(ch)], (rt, ch.rec, ch))
        starts = {}
        for e in lanes:
            for pos, st in enumerate(order[e]):
                starts[st.rec] = ready_at[st.rec]
        per_eng = {e: [] for e in ENGINES}
        for st in self.steps:
            per_eng[st.eng].append(st)
        for e in ENGINES:
            per_eng[e].sort(key=lambda st: (st.tag == "final",
                                            starts[st.rec], st.rec))
        eng_n = {e: 0 for e in ENGINES}
        dma_n = {q: 0 for q in self.dma_count}
        for e in ENGINES:
            for st in per_eng[e]:
                if st.dma_queue is None:
                    eng_n[e] += 1
                    st.done = ("e_" + e, eng_n[e])
                else:
                    dma_n[st.dma_queue] += 1
                    st.done = ("d_" + st.dma_queue, 16 * dma_n[st.dma_queue])
        return per_eng

    def finalize(self, block, sems):
        per_eng = self.schedule()
        hw_mark = {}
        for e in ENGINES:
            for st in per_eng[e]:
                resolved = []
                for dep in st.waits:
                    sem_name, val = dep.done
                    k = (e, sem_name)
                    if hw_mark.get(k, -1) < val:
                        hw_mark[k] = val
                        resolved.append((sem_name, val))
                if st.dma_queue is not None:
                    sem_name = "d_" + st.dma_queue
                    prev = st.done[1] - 16 * QDEPTH
                    if prev > 0 and hw_mark.get((e, sem_name), -1) < prev:
                        hw_mark[(e, sem_name)] = prev
                        resolved.append((sem_name, prev))
                st.waits = resolved

        def make_body(e):
            def body(eng):
                for st in per_eng[e]:
                    for sem_name, val in st.waits:
                        eng.wait_ge(sems[sem_name], val)
                    last = st.emit(eng)
                    if st.dma_queue is None:
                        assert last is not None, f"step {st.tag} returned None"
                        last.then_inc(sems["e_" + e], 1)
            return body

        block.tensor(make_body("te"))
        block.vector(make_body("ve"))
        block.scalar(make_body("se"))
        block.gpsimd(make_body("po"))
        block.sync(make_body("sy"))


def host_constants():
    k = np.arange(D)
    ang = 2.0 * np.pi * np.outer(k, k) / D
    cns = {}
    cns["cst_C"] = np.cos(ang).astype(ml_dtypes.bfloat16)
    cns["cst_S"] = np.sin(ang).astype(ml_dtypes.bfloat16)
    i = np.arange(QTS)
    same = (i[:, None] // DH) == (i[None, :] // DH)
    causal = i[None, :] <= i[:, None]
    m2 = (same & causal)
    cns["cst_maskneg2T"] = np.ascontiguousarray(
        np.where(m2.T, 0.0, NEG)).astype(np.float32)
    cns["cst_mask4"] = np.ascontiguousarray(
        np.tile(np.where(m2.T, 0.0, NEG), (1, 4))).astype(
        ml_dtypes.bfloat16)
    mg0 = np.zeros((NG, QTS), np.float32)
    mg0[:, 0:DH] = NEG
    cns["cst_maskg0T"] = mg0
    cns["cst_ident"] = np.eye(128, dtype=ml_dtypes.bfloat16)
    cns["cst_ones128"] = np.ones((128, 128), ml_dtypes.bfloat16)
    sel = np.zeros((128, H * QTS), np.float32)
    for h in range(H):
        sel[h, h * QTS:(h + 1) * QTS] = -1.0
        sel[64 + h, h * QTS:(h + 1) * QTS] = -1.0
    sel[32, 0:QTS] = -1.0
    cns["cst_sel"] = sel.astype(ml_dtypes.bfloat16)
    cns["cst_ones"] = np.ones((1, 128), ml_dtypes.bfloat16)
    return cns


def build_nc(T, ncores=1):
    nchunks = T // CHUNK
    nc = bass.Bass("TRN2", target_bir_lowering=False)

    x_h = nc.dram_tensor("x", [T, D], F32, kind="ExternalInput").ap()
    out_h = nc.dram_tensor("out", [T, D], F32, kind="ExternalOutput").ap()
    wq_h = nc.dram_tensor("wq", [D, D], F32, kind="ExternalInput").ap()
    wk_h = nc.dram_tensor("wk", [D, D], F32, kind="ExternalInput").ap()
    wv_h = nc.dram_tensor("wv", [D, D], F32, kind="ExternalInput").ap()
    wo_h = nc.dram_tensor("wo", [D, D], F32, kind="ExternalInput").ap()
    wr_h = nc.dram_tensor("wr_sl", [D, 128], F32, kind="ExternalInput").ap()
    wi_h = nc.dram_tensor("wi_sl", [D, 128], F32, kind="ExternalInput").ap()
    csl_h = nc.dram_tensor("cst_c_sl", [128, D], BF16, kind="ExternalInput").ap()
    ssl_h = nc.dram_tensor("cst_s_sl", [128, D], BF16, kind="ExternalInput").ap()
    mpart_h = nc.dram_tensor("mpart", [D, D], BF16, kind="Internal").ap()
    mfull_h = nc.dram_tensor("mfull", [D, D], BF16, kind="Internal").ap()
    bias_h = nc.dram_tensor("fbias", [1, D], F32, kind="ExternalInput").ap()
    C_h = nc.dram_tensor("cst_C", [D, D], BF16, kind="ExternalInput").ap()
    S_h = nc.dram_tensor("cst_S", [D, D], BF16, kind="ExternalInput").ap()
    mn2_h = nc.dram_tensor("cst_maskneg2T", [QTS, QTS], F32, kind="ExternalInput").ap()
    m4_h = nc.dram_tensor("cst_mask4", [QTS, 4 * QTS], BF16, kind="ExternalInput").ap()
    mg0_h = nc.dram_tensor("cst_maskg0T", [NG, QTS], F32, kind="ExternalInput").ap()
    o128_h = nc.dram_tensor("cst_ones128", [128, 128], BF16, kind="ExternalInput").ap()
    id_h = nc.dram_tensor("cst_ident", [128, 128], BF16, kind="ExternalInput").ap()
    sel_h = nc.dram_tensor("cst_sel", [128, H * QTS], BF16, kind="ExternalInput").ap()
    ones_h = nc.dram_tensor("cst_ones", [1, 128], BF16, kind="ExternalInput").ap()

    def sbuf(name, shape, dt):
        return nc.alloc_sbuf_tensor(name, shape, dt).ap()

    wqt = sbuf("wqt", [128, 8 * D], BF16)
    wkt = sbuf("wkt", [128, 8 * D], BF16)
    wvt = sbuf("wvt", [128, 8 * D], BF16)
    wot = sbuf("wot", [128, 8 * D], BF16)
    Msb = sbuf("Msb", [128, 8 * D], BF16)
    maskn2T = sbuf("maskn2T", [QTS, QTS], F32)
    maskg0T = sbuf("maskg0T", [NG, QTS], F32)
    allones = sbuf("allones", [128, 128], BF16)
    ident = sbuf("ident", [128, 128], BF16)
    ones_r = sbuf("ones_r", [1, 128], BF16)
    bias_b = sbuf("bias_b", [1, D], BF16)
    kg = sbuf("kg", [128, 8 * NG], BF16)
    vg = sbuf("vg", [48, D], BF16)

    def pair(name, shape, dt):
        return [sbuf(name + "0", shape, dt), sbuf(name + "1", shape, dt)]

    mcol_p = pair("mcol", [128, 1], F32)
    rcol_p = pair("rcol", [128, 1], F32)
    ncol_p = pair("ncol", [128, 1], F32)
    scol_p = pair("scol", [128, 1], F32)
    s2col_p = pair("s2col", [128, 1], F32)
    rcol1_p = pair("rcol1", [128, 1], F32)
    ncol1_p = pair("ncol1", [128, 1], F32)
    bnb_p = pair("bnb", [128, 12], F32)
    mv_p = pair("mv", [128, 2], F32)
    bnb1_p = pair("bnb1", [128, 12], F32)
    mv1_p = pair("mv1", [128, 2], F32)
    epsc = sbuf("epsc", [128, 1], F32)

    AREN = 115 * 1024
    arena = nc.alloc_sbuf_tensor("arena", [128, AREN], U8)
    ar_f = arena.bitcast(F32).ap()
    ar_b = arena.bitcast(BF16).ap()

    off = [0]
    extents = []

    def carve(nbytes, align=64):
        off[0] = (off[0] + align - 1) // align * align
        o = off[0]
        off[0] += nbytes
        assert off[0] <= AREN, f"arena overflow {off[0]}"
        extents.append((o, off[0]))
        return o

    def cf(cols):
        o = carve(cols * 4)
        return ar_f[:, o // 4: o // 4 + cols]

    def cb(cols):
        o = carve(cols * 2)
        return ar_b[:, o // 2: o // 2 + cols]

    extA = {}

    def mark(name):
        extA.setdefault(name, extents[-1])
    offA = off[0]
    C_sb = cb(8 * D)
    mark("C")
    S_sb = cb(8 * D)
    mark("S")
    Sn_sb = S_sb
    wrsl_sb = cb(8 * 128)
    mark("wrsl")
    wisl_sb = cb(8 * 128)
    mark("wisl")
    csl_sb = cb(D)
    mark("csl")
    ssl_sb = cb(D)
    mark("ssl")
    PTr_sb = cb(D)
    mark("PTr")
    QTr_sb = cb(D)
    mark("QTr")
    Mp_sb = cb(8 * D)
    mark("Mp")
    Wst_sb = cb(8 * D)
    mark("WstA")
    Wst2_sb = cb(8 * D)
    mark("WstB")
    end_A = off[0]

    off[0] = offA
    extM = {}

    def markM(keys):
        extM[extents[-1]] = keys
    x_p = [cf(4 * D), cf(4 * D)]
    markM([f"x1_{q}" for q in range(4)])
    extM[extents[-2]] = [f"x0_{q}" for q in range(4)]
    xm_p = [cf(D), cf(D)]
    markM(["xm1"])
    extM[extents[-2]] = ["xm0"]
    h_tm = cb(4 * D)
    markM([f"h{q}{s}" for q in range(4) for s in "ab"])
    h_fm = cb(4 * D)
    markM([f"hfm{j}_{q}" for j in range(8) for q in range(4)])
    q_fm = cb(4 * D)
    markM([f"q_fm{i}" for i in range(8)])
    k_fm = cb(4 * D)
    markM([f"k_fm{i}" for i in range(8)])
    v_tm = cb(4 * D)
    markM([f"v{q}" for q in range(4)])
    ao_fm = cb(4 * D)
    markM([f"ao{g}_{q}" for g in range(2) for q in range(4)])
    p2_p = [cb(8 * QTS), cb(8 * QTS)]
    markM(["p2a0_1", "p2b0_1"])
    extM[extents[-2]] = ["p2a0_0", "p2b0_0"]
    p2B_p = [cb(8 * QTS), cb(8 * QTS)]
    markM(["p2a1_1", "p2b1_1"])
    extM[extents[-2]] = ["p2a1_0", "p2b1_0"]
    p2g_p = [cb(512), cb(512)]
    markM(["p2g_1"])
    extM[extents[-2]] = ["p2g_0"]
    z_p = [cb(D), cb(D)]
    markM(["z1"])
    extM[extents[-2]] = ["z0"]
    rec_p = [cf(512), cf(512)]
    markM(["rec1"])
    extM[extents[-2]] = ["rec0"]
    junk = cb(D)
    markM(["junk"])
    end_main = off[0]
    assert max(end_A, end_main) <= AREN, (end_A, end_main)

    psA = nc.alloc_psum_tensor("psA", [128, 4096], F32)
    ps_f = psA.ap()
    ps_b = psA.bitcast(BF16).ap()

    ATOF = [0, 2048]
    WOBANKS = [["ps0", "ps1"], ["ps4", "ps5"]]
    PARKA_BF = [2688, 5760]
    def parkA(pb):
        return ps_b[:, PARKA_BF[pb]: PARKA_BF[pb] + 128]

    P = Prog(nc)
    ST = P.step
    sems = {}

    def ld(dst, src, queue, tag, writes, reads=()):
        def em(eng):
            return eng.dma_start(out=dst, in_=src).then_inc(
                sems["d_" + queue], 16)
        return ST({"syq": "sy", "poq": "po", "seq": "se",
                   "veq": "ve"}[queue], em,
                  reads=reads, writes=writes, dma_queue=queue, tag=tag)

    ld(maskn2T, mn2_h, "syq", "ld_mn2t", ["maskn2T"])
    ld(maskg0T, mg0_h, "syq", "ld_mg0t", ["maskg0T"])

    def em_eps(v):
        return v.memset(epsc, 1e-5)
    ST("ve", em_eps, reads=[], writes=["epsc"], tag="epsc")
    ld(allones, o128_h, "syq", "ld_o128", ["allones"])
    ld(ident, id_h, "syq", "ld_id", ["ident"])
    ld(ones_r, ones_h, "syq", "ld_ones", ["ones_r"])
    ld(bias_b, bias_h, "poq", "ld_bias", ["bias_b"])

    def chunks(t):
        return t.rearrange("p (c d) -> p c d", d=D)

    ld(chunks(C_sb), C_h.rearrange("(j p) m -> p j m", p=128),
       "syq", "ld_C", [f"C{jc}" for jc in range(8)])
    ld(chunks(S_sb), S_h.rearrange("(j p) m -> p j m", p=128),
       "syq", "ld_S", [f"S{jc}" for jc in range(8)])
    ld(wrsl_sb.rearrange("p (b m) -> p b m", m=128),
       wr_h.rearrange("(b p) m -> p b m", p=128), "poq", "ld_wrsl", ["wrsl"])
    ld(wisl_sb.rearrange("p (b m) -> p b m", m=128),
       wi_h.rearrange("(b p) m -> p b m", p=128), "poq", "ld_wisl", ["wisl"])
    ld(csl_sb, csl_h, "syq", "ld_csl", ["csl"])
    ld(ssl_sb, ssl_h, "syq", "ld_ssl", ["ssl"])

    def sneg_steps():
        for jc in range(8):
            def em(eng, jc=jc):
                return eng.activation(chunks(Sn_sb)[:, jc, :],
                                      chunks(S_sb)[:, jc, :], AF.Copy,
                                      scale=-1.0)
            ST("se", em, reads=[f"S{jc}"], writes=[f"S{jc}", f"Sn{jc}"],
               tag=f"sneg{jc}")

    ck = [f"C{j}" for j in range(8)]
    sk = [f"S{j}" for j in range(8)]
    snk = [f"Sn{j}" for j in range(8)]

    def shard_chain(dst_sb, dkey, l_a, ka, l_b, kb, r_a, r_b, rka, rkb,
                    scale, tag):
        def em_mm(eng):
            last = None
            for ah in range(2):
                for b in range(8):
                    last = eng.matmul(
                        ps_f[:, 512 * ah:512 * ah + 512],
                        l_a.rearrange("p (b m) -> p b m", m=128)[:, b, :],
                        chunks(r_a)[:, b, 512 * ah:512 * (ah + 1)],
                        start=(b == 0), stop=False)
                for b in range(8):
                    last = eng.matmul(
                        ps_f[:, 512 * ah:512 * ah + 512],
                        l_b.rearrange("p (b m) -> p b m", m=128)[:, b, :],
                        chunks(r_b)[:, b, 512 * ah:512 * (ah + 1)],
                        start=False, stop=(b == 7))
            return last
        ST("te", em_mm, reads=[ka, kb] + rka + rkb,
           writes=["ps0", "ps1"], tag=f"{tag}_mm")

        def em_ev(eng):
            return eng.activation(dst_sb, ps_f[:, 0:1024], AF.Copy,
                                  scale=scale)
        ST("se", em_ev, reads=["ps0", "ps1"], writes=[dkey],
           tag=f"{tag}_ev")

    shard_chain(PTr_sb, "PTr", wrsl_sb, "wrsl", wisl_sb, "wisl",
                C_sb, S_sb, ck, sk, 1.0, "PTr")
    sneg_steps()

    def em_snl(eng):
        return eng.activation(ssl_sb, ssl_sb, AF.Copy, scale=-1.0)
    ST("se", em_snl, reads=["ssl"], writes=["ssl", "snl"], tag="snl")

    shard_chain(QTr_sb, "QTr", wisl_sb, "wisl", wrsl_sb, "wrsl",
                C_sb, Sn_sb, ck, snk, 1.0, "QTr")

    for tc in range(0, 8, 2):
        def em_mm(eng, tc=tc):
            last = None
            for k in range(4):
                t2 = tc + k // 2
                ah = k % 2
                eng.matmul(ps_f[:, 512 * k:512 * k + 512],
                           PTr_sb[:, 128 * t2:128 * (t2 + 1)],
                           csl_sb[:, 512 * ah:512 * (ah + 1)],
                           start=True, stop=False)
                last = eng.matmul(ps_f[:, 512 * k:512 * k + 512],
                                  QTr_sb[:, 128 * t2:128 * (t2 + 1)],
                                  ssl_sb[:, 512 * ah:512 * (ah + 1)],
                                  start=False, stop=True)
            return last
        ST("te", em_mm, reads=["PTr", "QTr", "csl", "snl"],
           writes=["ps0", "ps1", "ps2", "ps3"], tag=f"Mp_mm{tc}")

        def em_ev(eng, tc=tc):
            return eng.activation(
                chunks(Mp_sb).transpose([0, 1, 2])[:, tc:tc + 2, :]
                .rearrange("p a b -> p (a b)"),
                ps_f[:, 0:2048], AF.Copy, scale=1.0 / D)
        ST("se", em_ev, reads=["ps0", "ps1", "ps2", "ps3"],
           writes=[f"Mp{tc}", f"Mp{tc + 1}"], tag=f"Mp_ev{tc}")

    ld(mpart_h.rearrange("(tc p) u -> p tc u", p=128),
       Mp_sb.rearrange("p (tc u) -> p tc u", u=D), "syq", "mp_out",
       ["mpartd"], reads=[f"Mp{j}" for j in range(8)])
    if ncores > 1:
        def em_cc(eng):
            return eng.collective_compute(
                "AllReduce", ALU.add,
                replica_groups=[list(range(ncores))],
                ins=[mpart_h.opt()], outs=[mfull_h.opt()])
        ST("po", em_cc, reads=["mpartd"], writes=["mfulld"],
           tag="allreduce")
    else:
        ld(mfull_h, mpart_h, "poq", "mp_copy", ["mfulld"], reads=["mpartd"])
    ld(chunks(Msb), mfull_h.rearrange("(j p) m -> p j m", p=128),
       "syq", "ld_M", [f"M{jc}" for jc in range(8)], reads=["mfulld"])

    for (w_hbm, wt_sb, nm, st_sb, sk, q) in (
            (wq_h, wqt, "q", Wst_sb, "WstA", "poq"),
            (wk_h, wkt, "k", Wst2_sb, "WstB", "poq"),
            (wv_h, wvt, "v", Wst_sb, "WstA", "poq"),
            (wo_h, wot, "o", Wst2_sb, "WstB", "poq")):
        ld(chunks(st_sb), w_hbm.rearrange("(i p) m -> p i m", p=128),
           q, f"ldw_{nm}", [f"{sk}{ic}" for ic in range(8)])
        for ic in range(8):
            for jc in range(8):
                pb = (ic * 8 + jc) % 2

                def em_t(eng, ic=ic, jc=jc, pb=pb, st_sb=st_sb):
                    return eng.transpose(
                        parkA(pb),
                        chunks(st_sb)[:, ic, 128 * jc:128 * (jc + 1)], ident)
                ST("te", em_t, reads=[f"{sk}{ic}", "ident"],
                   writes=["ps2" if pb == 0 else "ps5"], tag=f"wT_{nm}{ic}_{jc}")

                def em_e(eng, wt_sb=wt_sb, ic=ic, jc=jc, pb=pb):
                    return eng.tensor_copy(
                        chunks(wt_sb)[:, jc, 128 * ic:128 * (ic + 1)],
                        parkA(pb))
                ST("ve", em_e, reads=["ps2" if pb == 0 else "ps5"], writes=[f"wt_{nm}{jc}"],
                   tag=f"wTe_{nm}{ic}_{jc}")

    barrier = P.barrier_deps()
    for mkeys in extM.values():
        for k in mkeys:
            P.seed(k, barrier)

    x_src = x_h.rearrange("(c q p) d -> c p q d", p=128, q=4)
    out_dst = out_h.rearrange("(c q p) d -> c p q d", p=128, q=4)
    hfmv = h_fm.rearrange("p (j t) -> p j t", t=CHUNK)
    qfmv = q_fm.rearrange("p (i t) -> p i t", t=CHUNK)
    kfmv = k_fm.rearrange("p (i t) -> p i t", t=CHUNK)
    aov = ao_fm.rearrange("p (r t) -> p r t", t=CHUNK)
    kgv = kg.rearrange("p (j g) -> p j g", g=NG)
    vtv = v_tm.rearrange("p (q i) -> p q i", i=D)
    xv_p = [x_p[0].rearrange("p (q d) -> p q d", d=D),
            x_p[1].rearrange("p (q d) -> p q d", d=D)]
    hv = h_tm.rearrange("p (q d) -> p q d", d=D)

    def qslice(hh, qt, a=0, b=QTS):
        return qfmv[:, hh // 2, 128 * qt + a:128 * qt + b][
            64 * (hh % 2):64 * (hh % 2) + 64, :]

    def kslice(hh, qt, a=0, b=QTS):
        return kfmv[:, hh // 2, 128 * qt + a:128 * qt + b][
            64 * (hh % 2):64 * (hh % 2) + 64, :]

    def kgsl(hh):
        return kgv[:, hh // 2, :][64 * (hh % 2):64 * (hh % 2) + 64, :]

    def ln_stats(src_ap, src_keys, pp, tag, own=False):
        if own:
            bnb, mv = bnb1_p[pp], mv1_p[pp]
            rcol, ncol = rcol1_p[pp], ncol1_p[pp]
            sx = "b"
        else:
            bnb, mv = bnb_p[pp], mv_p[pp]
            rcol, ncol = rcol_p[pp], ncol_p[pp]
            sx = ""
        K = lambda n: n + sx + str(pp)

        def em_bs(v):
            v.bn_stats(bnb[:, 0:6], src_ap[:, 0:512])
            return v.bn_stats(bnb[:, 6:12], src_ap[:, 512:1024])
        ST("ve", em_bs, reads=src_keys, writes=[K("bnb")], tag=f"{tag}_bs")

        def em_ba(v):
            return v.bn_aggr(mv, bnb)
        ST("ve", em_ba, reads=[K("bnb")], writes=[K("mv")],
           tag=f"{tag}_ba")

        def em_rr(se):
            se.activation(ncol, mv[:, 1:2], AF.Ln, bias=epsc[:, 0:1])
            return se.activation(rcol, ncol, AF.Exp, scale=-0.5)
        ST("se", em_rr, reads=[K("mv"), "epsc"],
           writes=[K("rcol"), K("ncol")], tag=f"{tag}_rr")

    def transpose_htm(qt, tag):
        pp = qt % 2
        bk = WOBANKS[pp] + WOBANKS[1 - pp]
        c0 = [2 * ATOF[pp], 2 * ATOF[pp] + 1024,
              2 * ATOF[1 - pp], 2 * ATOF[1 - pp] + 1024]

        def park(i):
            return ps_b[:, c0[i]: c0[i] + 128]
        for jc in range(8):
            pb = jc % 4

            def em_t(eng, qt=qt, jc=jc, pb=pb, park=park):
                return eng.transpose(
                    park(pb),
                    h_tm.rearrange("p (q j d) -> p q j d", q=4, j=8)[:, qt, jc, :],
                    ident)
            ST("te", em_t, reads=[f"h{qt}{'ab'[jc // 4]}", "ident"],
               writes=[bk[pb]], tag=f"{tag}T_{qt}_{jc}")

            if pb % 2 == 0:
                def em_e(eng, qt=qt, jc=jc, pb=pb, park=park):
                    return eng.tensor_copy(
                        hfmv[:, jc, 128 * qt:128 * qt + 128], park(pb))
                ST("ve", em_e, reads=[bk[pb]], writes=[f"hfm{jc}_{qt}"],
                   tag=f"{tag}Te_{qt}_{jc}")
            else:
                def em_e(eng, qt=qt, jc=jc, pb=pb, park=park):
                    return eng.copy(
                        hfmv[:, jc, 128 * qt:128 * qt + 128], park(pb))
                ST("se", em_e, reads=[bk[pb]], writes=[f"hfm{jc}_{qt}"],
                   tag=f"{tag}Te_{qt}_{jc}")

    def stage_ln1(c, qt):
        pp = qt % 2
        cp = c % 2
        xq = xv_p[cp][:, qt, :]
        hq = hv[:, qt, :]
        ln_stats(xq, [f"x{cp}_{qt}"], pp, f"ln1_{c}_{qt}", own=True)

        for hf in range(2):
            def em_n1(v, xq=xq, hq=hq, pp=pp, hf=hf):
                return v.tensor_scalar(hq[:, 512 * hf:512 * (hf + 1)],
                                       xq[:, 512 * hf:512 * (hf + 1)],
                                       mv1_p[pp][:, 0:1],
                                       rcol1_p[pp][:, 0:1],
                                       op0=ALU.subtract, op1=ALU.mult)
            ST("ve", em_n1,
               reads=[f"x{cp}_{qt}", f"mvb{pp}", f"rcolb{pp}"],
               writes=[f"h{qt}{'ab'[hf]}"], tag=f"n1_{c}_{qt}{hf}")
        transpose_htm(qt, f"h1_{c}")

    def stage_qkv(c):
        hfm_keys = [f"hfm{j}_{q}" for j in range(8) for q in range(4)]
        rot = [("ps0", 0), ("ps1", 512), ("ps3", 1536), ("ps4", 2048)]
        rr = [0]
        for (wt_sb, wtk, dst, dk, scale) in (
                (wqt, "q", q_fm, "q_fm", 1.0), (wkt, "k", k_fm, "k_fm", 0.125)):
            for ic0 in range(0, 8, 2):
                pk = [rot[rr[0] % 4], rot[(rr[0] + 1) % 4]]
                rr[0] += 2

                def em_mm(eng, wt_sb=wt_sb, ic0=ic0, pk=pk):
                    last = None
                    for k in range(2):
                        ic = ic0 + k
                        c0 = pk[k][1]
                        for jc in range(8):
                            last = eng.matmul(
                                ps_f[:, c0:c0 + 512],
                                chunks(wt_sb)[:, jc, 128 * ic:128 * (ic + 1)],
                                hfmv[:, jc, :],
                                start=(jc == 0), stop=(jc == 7))
                    return last
                ST("te", em_mm,
                   reads=hfm_keys + [f"wt_{wtk}{j}" for j in range(8)],
                   writes=[pk[0][0], pk[1][0]], tag=f"{dk}_mm{c}_{ic0}")

                for k in range(2):
                    def em_ev(eng, dst=dst, ic=ic0 + k, c0=pk[k][1],
                              scale=scale):
                        return eng.tensor_scalar(
                            dst.rearrange("p (i t) -> p i t", t=CHUNK)[:, ic, :],
                            ps_f[:, c0:c0 + 512], scale, None, op0=ALU.mult)
                    ST("ve", em_ev, reads=[pk[k][0]], writes=[f"{dk}{ic0 + k}"],
                       tag=f"{dk}_ev{c}_{ic0 + k}")
        for qt in range(4):
            for ih in range(2):
                bk, c0 = rot[rr[0] % 4]
                rr[0] += 1

                def em_mm(eng, qt=qt, ih=ih, c0=c0):
                    last = None
                    for jc in range(8):
                        last = eng.matmul(
                            ps_f[:, c0:c0 + 512],
                            hfmv[:, jc, 128 * qt:128 * qt + 128],
                            chunks(wvt)[:, jc, 512 * ih:512 * (ih + 1)],
                            start=(jc == 0), stop=(jc == 7))
                    return last
                ST("te", em_mm,
                   reads=[f"hfm{j}_{qt}" for j in range(8)]
                   + [f"wt_v{j}" for j in range(8)],
                   writes=[bk], tag=f"v_mm{c}_{qt}_{ih}")

                def em_ev(eng, qt=qt, ih=ih, c0=c0):
                    return eng.tensor_copy(
                        vtv[:, qt, 512 * ih:512 * (ih + 1)],
                        ps_f[:, c0:c0 + 512])
                ST("ve", em_ev, reads=[bk], writes=[f"v{qt}"],
                   tag=f"v_ev{c}_{qt}_{ih}")

    def stage_kgvg():
        def em_kg(v):
            return v.tensor_copy(kgv, kfmv[:, :, 0:NG])
        ST("ve", em_kg, reads=[f"k_fm{j}" for j in range(8)],
           writes=["kg"], tag="kg")

        def em_vg(v):
            v.tensor_copy(vg[0:NG, :], vtv[0:NG, 0, :])
            return v.tensor_copy(vg[32:32 + NG, :], vtv[0:NG, 0, :])
        ST("ve", em_vg, reads=["v0"], writes=["vg"], tag="vg")

    def stage_p2(c, qt):
        pp = qt % 2
        OF = ATOF[pp]
        B = [f"ps{4 * pp + i}" for i in range(4)]
        p2g, rec = p2g_p[pp], rec_p[pp]
        o_q = [OF, OF + 512]
        o_g = OF + 1024
        o_s = OF + 1536
        for g in range(2):
            h0 = 8 * g
            p2 = (p2_p if g == 0 else p2B_p)[pp]
            KA, KB = f"p2a{g}_{pp}", f"p2b{g}_{pp}"

            for quarter in range(2):
                def em_raw(eng, qt=qt, h0=h0, quarter=quarter):
                    last = None
                    for i in range(4):
                        hh = h0 + 2 * i + quarter
                        o = o_q[quarter] + i * QTS
                        last = eng.matmul(ps_f[:, o:o + QTS], kslice(hh, qt),
                                          qslice(hh, qt), start=(i == 0),
                                          stop=(i == 3))
                    return last
                ST("te", em_raw,
                   reads=[f"k_fm{j}" for j in range(8)]
                   + [f"q_fm{j}" for j in range(8)],
                   writes=[B[quarter]], tag=f"s2_{c}_{qt}_{g}{quarter}")

            for quarter in range(2):
                def em_mn(v, quarter=quarter):
                    o = o_q[quarter]
                    return v.tensor_tensor(
                        ps_f[:, o:o + 512].rearrange("p (h k) -> p h k", k=QTS),
                        ps_f[:, o:o + 512].rearrange("p (h k) -> p h k", k=QTS),
                        maskn2T.unsqueeze(1).broadcast_to([QTS, 4, QTS]),
                        op=ALU.add)
                ST("ve", em_mn, reads=[B[quarter], "maskn2T"],
                   writes=[B[quarter]], tag=f"mn_{c}_{qt}_{g}{quarter}")

            def em_graw(eng, qt=qt, h0=h0):
                last = None
                for par in range(2):
                    pb = 32 * par
                    for i in range(4):
                        hh = h0 + 2 * i + par
                        o = o_g + i * QTS
                        last = eng.matmul(ps_f[pb:pb + 16, o:o + QTS],
                                          kgsl(hh), qslice(hh, qt),
                                          start=(i == 0), stop=(i == 3))
                return last
            ST("te", em_graw,
               reads=[f"q_fm{j}" for j in range(8)] + ["kg"],
               writes=[B[2]], tag=f"s2g_{c}_{qt}_{g}")

            if c == 0 and qt == 0:
                def em_mng(v):
                    v.tensor_tensor(
                        ps_f[0:16, o_g:o_g + 512].rearrange(
                            "p (h k) -> p h k", k=QTS),
                        ps_f[0:16, o_g:o_g + 512].rearrange(
                            "p (h k) -> p h k", k=QTS),
                        maskg0T.unsqueeze(1).broadcast_to([NG, 4, QTS]),
                        op=ALU.add)
                    return v.tensor_tensor(
                        ps_f[32:48, o_g:o_g + 512].rearrange(
                            "p (h k) -> p h k", k=QTS),
                        ps_f[32:48, o_g:o_g + 512].rearrange(
                            "p (h k) -> p h k", k=QTS),
                        maskg0T.unsqueeze(1).broadcast_to([NG, 4, QTS]),
                        op=ALU.add)
                ST("ve", em_mng, reads=[B[2], "maskg0T"], writes=[B[2]],
                   tag=f"mng_{c}_{qt}_{g}")

            for quarter in range(2):
                def em_expm(se, p2=p2, quarter=quarter):
                    return se.activation(
                        p2[:, 512 * quarter:512 * (quarter + 1)],
                        ps_f[:, o_q[quarter]:o_q[quarter] + 512], AF.Exp)
                ST("se", em_expm, reads=[B[quarter]],
                   writes=[[KA, KB][quarter]],
                   tag=f"expm_{c}_{qt}_{g}{quarter}")

            def em_expmg(se, p2g=p2g):
                se.activation(p2g[0:16, 0:512],
                              ps_f[0:16, o_g:o_g + 512], AF.Exp)
                return se.activation(p2g[32:48, 0:512],
                                     ps_f[32:48, o_g:o_g + 512], AF.Exp)
            ST("se", em_expmg, reads=[B[2]], writes=[f"p2g_{pp}"],
               tag=f"expmg_{c}_{qt}_{g}")

            def em_sum(eng, p2=p2, p2g=p2g):
                eng.matmul(ps_f[0:64, o_g:o_g + 512], allones[:, 0:64],
                           p2[:, 0:512], start=True, stop=False)
                eng.matmul(ps_f[0:64, o_g:o_g + 512], allones[0:16, 0:64],
                           p2g[0:16, 0:512], start=False, stop=False)
                eng.matmul(ps_f[64:128, o_g:o_g + 512], allones[:, 64:128],
                           p2[:, 512:1024], start=True, stop=False)
                return eng.matmul(ps_f[64:128, o_g:o_g + 512],
                                  allones[32:48, 64:128],
                                  p2g[32:48, 0:512], start=False, stop=True)
            ST("te", em_sum, reads=[KA, KB, f"p2g_{pp}",
                                    "allones", B[2]],
               writes=[B[2]], tag=f"sum_{c}_{qt}_{g}")

            def em_rec(se, rec=rec):
                se.activation(rec, ps_f[:, o_g:o_g + 512], AF.Ln)
                return se.activation(rec, rec, AF.Exp, scale=-1.0)
            ST("se", em_rec, reads=[B[2]], writes=[f"rec{pp}"],
               tag=f"rcp_{c}_{qt}_{g}")

            for par in range(2):
                def em_av(eng, qt=qt, h0=h0, p2=p2, p2g=p2g, par=par):
                    last = None
                    for i in range(par, 8, 2):
                        hh = h0 + i
                        pos = (i % 2) * 4 + i // 2
                        ob = o_s + (i // 2) * QTS
                        dst = ps_f[:, ob:ob + QTS][
                            64 * (hh % 2):64 * (hh % 2) + 64, :]
                        eng.matmul(
                            dst,
                            v_tm[:, qt * D + hh * DH: qt * D + hh * DH + DH],
                            p2[:, pos * QTS:(pos + 1) * QTS],
                            start=True, stop=False)
                        vb = 32 * (i % 2)
                        last = eng.matmul(
                            dst, vg[vb:vb + NG, hh * DH:(hh + 1) * DH],
                            p2g[vb:vb + NG,
                                (i // 2) * QTS:(i // 2) * QTS + QTS],
                            start=False, stop=True)
                    return last
                ST("te", em_av,
                   reads=[f"v{qt}", "vg", [KA, KB][par],
                          f"p2g_{pp}"],
                   writes=[B[3]], tag=f"av_{c}_{qt}_{g}{par}")

            def em_ave(v, qt=qt, g=g, rec=rec):
                return v.tensor_tensor(
                    aov[:, 4 * g:4 * g + 4, 128 * qt:128 * qt + 128],
                    ps_f[:, o_s:o_s + 512].rearrange(
                        "p (r t) -> p r t", t=QTS),
                    rec.rearrange("p (r t) -> p r t", t=QTS),
                    op=ALU.mult)
            ST("ve", em_ave, reads=[B[3], f"rec{pp}"], writes=[f"ao{g}_{qt}"],
               tag=f"ave_{c}_{qt}_{g}")

    def stage_w(c, qt):
        pp = qt % 2
        cp = c % 2
        of = ATOF[pp]
        bks = WOBANKS[pp]
        xq = xv_p[cp][:, qt, :]
        mcol, rcol, ncol = mcol_p[pp], rcol_p[pp], ncol_p[pp]
        scol, s2col = scol_p[pp], s2col_p[pp]
        xm_sb = xm_p[pp]

        def em_wo(eng, qt=qt, of=of):
            last = None
            for ih in range(2):
                for jc in range(8):
                    last = eng.matmul(
                        ps_f[:, of + 512 * ih:of + 512 * ih + 512],
                        aov[:, jc, 128 * qt:128 * qt + 128],
                        chunks(wot)[:, jc, 512 * ih:512 * (ih + 1)],
                        start=(jc == 0), stop=(jc == 7))
            return last
        ST("te", em_wo,
           reads=[f"ao{g}_{qt}" for g in range(2)]
           + [f"wt_o{j}" for j in range(8)],
           writes=bks, tag=f"wo_{c}_{qt}")

        z_sb = z_p[pp]

        def em_zev(se, of=of, scol=scol, z_sb=z_sb):
            return se.activation(z_sb, ps_f[:, of:of + 1024], AF.Copy,
                                 accum_out=scol)
        ST("se", em_zev, reads=bks + [], writes=[f"z{pp}", f"scol{pp}"],
           tag=f"lnz_ev_{c}_{qt}")

        def em_zsq(se, s2col=s2col, z_sb=z_sb):
            return se.activation(junk, z_sb, AF.Square, accum_out=s2col)
        ST("se", em_zsq, reads=[f"z{pp}"], writes=["junk", f"s2col{pp}"],
           tag=f"lnz_sq_{c}_{qt}")

        def em_mean(v, mcol=mcol, scol=scol):
            return v.tensor_scalar(mcol, scol, 1.0 / D, None, op0=ALU.mult)
        ST("ve", em_mean, reads=[f"scol{pp}"], writes=[f"mcol{pp}"],
           tag=f"lnz_m_{c}_{qt}")

        def em_m2c(v, mcol=mcol, ncol=ncol):
            return v.tensor_scalar(ncol, mcol, mcol[:, 0:1], -1e-5,
                                   op0=ALU.mult, op1=ALU.add)
        ST("ve", em_m2c, reads=[f"mcol{pp}"], writes=[f"ncol{pp}"],
           tag=f"lnz_m2_{c}_{qt}")

        def em_var(v, rcol=rcol, ncol=ncol, s2col=s2col):
            return v.tensor_scalar(rcol, s2col, 1.0 / D, ncol[:, 0:1],
                                   op0=ALU.mult, op1=ALU.subtract)
        ST("ve", em_var, reads=[f"s2col{pp}", f"ncol{pp}"],
           writes=[f"rcol{pp}"], tag=f"lnz_v_{c}_{qt}")

        def em_rr(se, rcol=rcol, ncol=ncol):
            se.activation(ncol, rcol, AF.Ln)
            return se.activation(rcol, ncol, AF.Exp, scale=-0.5)
        ST("se", em_rr, reads=[f"rcol{pp}"],
           writes=[f"rcol{pp}", f"ncol{pp}"], tag=f"lnz_rr_{c}_{qt}")

        def em_nmr(v, rcol=rcol, mcol=mcol, ncol=ncol):
            return v.tensor_scalar(ncol, rcol, mcol[:, 0:1], -1.0,
                                   op0=ALU.mult, op1=ALU.mult)
        ST("ve", em_nmr, reads=[f"rcol{pp}", f"mcol{pp}"],
           writes=[f"ncol{pp}"], tag=f"lnz_nmr_{c}_{qt}")

        def em_xm(v, xq=xq, ncol=ncol, xm_sb=xm_sb):
            return v.tensor_scalar(xm_sb, xq, ncol[:, 0:1], None, op0=ALU.add)
        ST("ve", em_xm, reads=[f"x{cp}_{qt}", f"ncol{pp}"],
           writes=[f"xm{pp}"], tag=f"lnz_xm_{c}_{qt}")

        def em_x2(v, xq=xq, rcol=rcol, xm_sb=xm_sb, z_sb=z_sb):
            return v.scalar_tensor_tensor(xq, z_sb, rcol[:, 0:1], xm_sb,
                                          op0=ALU.mult, op1=ALU.add)
        ST("ve", em_x2, reads=[f"z{pp}", f"rcol{pp}", f"xm{pp}"],
           writes=[f"x{cp}_{qt}"], tag=f"x2_{c}_{qt}")

    def stage_h2(c, qt):
        pp = qt % 2
        cp = c % 2
        xq = xv_p[cp][:, qt, :]
        hq = hv[:, qt, :]
        ln_stats(xq, [f"x{cp}_{qt}"], pp, f"ln2_{c}_{qt}")

        for hf in range(2):
            def em_n2(v, xq=xq, hq=hq, pp=pp, hf=hf):
                return v.tensor_scalar(hq[:, 512 * hf:512 * (hf + 1)],
                                       xq[:, 512 * hf:512 * (hf + 1)],
                                       mv_p[pp][:, 0:1],
                                       rcol_p[pp][:, 0:1],
                                       op0=ALU.subtract, op1=ALU.mult)
            ST("ve", em_n2,
               reads=[f"x{cp}_{qt}", f"mv{pp}", f"rcol{pp}"],
               writes=[f"h{qt}{'ab'[hf]}"], tag=f"n2_{c}_{qt}{hf}")
        transpose_htm(qt, f"h2_{c}")

    def stage_f(c, qt):
        pp = qt % 2
        cp = c % 2
        of = ATOF[pp]
        bks = WOBANKS[pp]
        xq = xv_p[cp][:, qt, :]
        mcol, rcol, ncol = mcol_p[pp], rcol_p[pp], ncol_p[pp]
        scol, s2col = scol_p[pp], s2col_p[pp]
        xm_sb = xm_p[pp]

        def em_ffn(eng, qt=qt, of=of):
            last = None
            for ih in range(2):
                for jc in range(8):
                    eng.matmul(
                        ps_f[:, of + 512 * ih:of + 512 * ih + 512],
                        hfmv[:, jc, 128 * qt:128 * qt + 128],
                        chunks(Msb)[:, jc, 512 * ih:512 * (ih + 1)],
                        start=(jc == 0), stop=False)
                last = eng.matmul(
                    ps_f[:, of + 512 * ih:of + 512 * ih + 512], ones_r,
                    bias_b[:, 512 * ih:512 * (ih + 1)],
                    start=False, stop=True)
            return last
        ST("te", em_ffn,
           reads=[f"hfm{j}_{qt}" for j in range(8)]
           + [f"M{j}" for j in range(8)] + ["ones_r", "bias_b"],
           writes=bks, tag=f"ffn_{c}_{qt}")

        z_sb = z_p[pp]

        def em_yev(se, of=of, scol=scol, z_sb=z_sb):
            return se.activation(z_sb, ps_f[:, of:of + 1024], AF.Copy,
                                 accum_out=scol)
        ST("se", em_yev, reads=bks, writes=[f"z{pp}", f"scol{pp}"],
           tag=f"lny_ev_{c}_{qt}")

        def em_ysq(se, s2col=s2col, z_sb=z_sb):
            return se.activation(junk, z_sb, AF.Square, accum_out=s2col)
        ST("se", em_ysq, reads=[f"z{pp}"], writes=["junk", f"s2col{pp}"],
           tag=f"lny_sq_{c}_{qt}")

        def em_mean(v, mcol=mcol, scol=scol):
            return v.tensor_scalar(mcol, scol, 1.0 / D, None, op0=ALU.mult)
        ST("ve", em_mean, reads=[f"scol{pp}"], writes=[f"mcol{pp}"],
           tag=f"lny_m_{c}_{qt}")

        def em_m2c(v, mcol=mcol, ncol=ncol):
            return v.tensor_scalar(ncol, mcol, mcol[:, 0:1], -1e-5,
                                   op0=ALU.mult, op1=ALU.add)
        ST("ve", em_m2c, reads=[f"mcol{pp}"], writes=[f"ncol{pp}"],
           tag=f"lny_m2_{c}_{qt}")

        def em_var(v, rcol=rcol, ncol=ncol, s2col=s2col):
            return v.tensor_scalar(rcol, s2col, 1.0 / D, ncol[:, 0:1],
                                   op0=ALU.mult, op1=ALU.subtract)
        ST("ve", em_var, reads=[f"s2col{pp}", f"ncol{pp}"],
           writes=[f"rcol{pp}"], tag=f"lny_v_{c}_{qt}")

        def em_rr(se, rcol=rcol, ncol=ncol):
            se.activation(ncol, rcol, AF.Ln)
            return se.activation(rcol, ncol, AF.Exp, scale=-0.5)
        ST("se", em_rr, reads=[f"rcol{pp}"],
           writes=[f"rcol{pp}", f"ncol{pp}"], tag=f"lny_rr_{c}_{qt}")

        def em_nmr(v, rcol=rcol, mcol=mcol, ncol=ncol):
            return v.tensor_scalar(ncol, rcol, mcol[:, 0:1], -1.0,
                                   op0=ALU.mult, op1=ALU.mult)
        ST("ve", em_nmr, reads=[f"rcol{pp}", f"mcol{pp}"],
           writes=[f"ncol{pp}"], tag=f"lny_nmr_{c}_{qt}")

        def em_xm(v, xq=xq, ncol=ncol, xm_sb=xm_sb):
            return v.tensor_scalar(xm_sb, xq, ncol[:, 0:1], None, op0=ALU.add)
        ST("ve", em_xm, reads=[f"x{cp}_{qt}", f"ncol{pp}"],
           writes=[f"xm{pp}"], tag=f"lny_xm_{c}_{qt}")

        def em_out(v, xq=xq, rcol=rcol, xm_sb=xm_sb, z_sb=z_sb):
            return v.scalar_tensor_tensor(xq, z_sb, rcol[:, 0:1], xm_sb,
                                          op0=ALU.mult, op1=ALU.add)
        ST("ve", em_out, reads=[f"z{pp}", f"rcol{pp}", f"xm{pp}"],
           writes=[f"x{cp}_{qt}"], tag=f"out_{c}_{qt}")

        def em_st(eng, c=c, qt=qt, xq=xq):
            return eng.dma_start(out=out_dst[c][:, qt, :], in_=xq).then_inc(
                sems["d_seq"], 16)
        ST("se", em_st, reads=[f"x{cp}_{qt}"], writes=[], dma_queue="seq",
           tag=f"st_{c}_{qt}")
        if c + 1 < nchunks:
            ncp = (c + 1) % 2
            ld(xv_p[ncp][:, qt, :], x_src[c + 1][:, qt, :], "syq",
               f"x_in{c + 1}_{qt}", [f"x{ncp}_{qt}"])

    for qt in range(4):
        ld(xv_p[0][:, qt, :], x_src[0][:, qt, :],
           "seq" if qt < 2 else "syq", f"x_in0_{qt}",
           [f"x0_{qt}"])
    for c in range(nchunks):
        if c == 0:
            for qt in range(4):
                stage_ln1(c, qt)
        stage_qkv(c)
        if c == 0:
            stage_kgvg()
        stage_p2(c, 0)
        stage_w(c, 0)
        stage_p2(c, 1)
        stage_h2(c, 0)
        stage_w(c, 1)
        stage_f(c, 0)
        stage_p2(c, 2)
        stage_h2(c, 1)
        stage_w(c, 2)
        stage_f(c, 1)
        stage_p2(c, 3)
        stage_h2(c, 2)
        stage_w(c, 3)
        stage_f(c, 2)
        if c + 1 < nchunks:
            stage_ln1(c + 1, 0)
            stage_ln1(c + 1, 1)

        stage_h2(c, 3)
        stage_f(c, 3)
        if c + 1 < nchunks:
            stage_ln1(c + 1, 2)
            stage_ln1(c + 1, 3)

    with (
        nc.semaphore("e_te") as s0, nc.semaphore("e_ve") as s1,
        nc.semaphore("e_se") as s2, nc.semaphore("e_po") as s3,
        nc.semaphore("e_sy") as s4, nc.semaphore("d_syq") as s5,
        nc.semaphore("d_poq") as s6, nc.semaphore("d_seq") as s7,
        nc.semaphore("d_ccq") as s8, nc.semaphore("d_veq") as s9,
        nc.Block() as block,
    ):
        sems.update({"e_te": s0, "e_ve": s1, "e_se": s2, "e_po": s3,
                     "e_sy": s4, "d_syq": s5, "d_poq": s6, "d_seq": s7,
                     "d_ccq": s8, "d_veq": s9})

        qcounts = [(q, n) for q, n in P.dma_count.items() if n]

        def em_fin(eng):
            for q, n in qcounts:
                eng.wait_ge(sems["d_" + q], 16 * n)
            return eng.nop()
        P.step("sy", em_fin, reads=[], writes=[], tag="final")
        P.finalize(block, sems)
    return nc


_NC_CACHE = {}


def kernel(L0_attn_hidden_states, L0_attn_Wq, L0_attn_Wk, L0_attn_Wv,
           L0_attn_Wo, L0_ffn_w_freq_real, L0_ffn_w_freq_imag, L0_ffn_bias):
    x = np.ascontiguousarray(np.asarray(L0_attn_hidden_states, np.float32))
    B, T, _ = x.shape
    if T not in _NC_CACHE:
        _NC_CACHE[T] = build_nc(T, B)
    nc = _NC_CACHE[T]
    cns = host_constants()
    base = {
        "wq": np.ascontiguousarray(np.asarray(L0_attn_Wq, np.float32)),
        "wk": np.ascontiguousarray(np.asarray(L0_attn_Wk, np.float32)),
        "wv": np.ascontiguousarray(np.asarray(L0_attn_Wv, np.float32)),
        "wo": np.ascontiguousarray(np.asarray(L0_attn_Wo, np.float32)),

        "fbias": np.ascontiguousarray(
            np.asarray(L0_ffn_bias, np.float32).reshape(1, D)),
    }
    base.update(cns)
    wr = np.asarray(L0_ffn_w_freq_real, np.float32)
    wi = np.asarray(L0_ffn_w_freq_imag, np.float32)
    Cb = np.asarray(cns["cst_C"])
    Sb = np.asarray(cns["cst_S"])
    in_maps = [dict(base, x=np.ascontiguousarray(x[b]),
                    wr_sl=np.ascontiguousarray(wr[:, 128 * b:128 * (b + 1)]),
                    wi_sl=np.ascontiguousarray(wi[:, 128 * b:128 * (b + 1)]),
                    cst_c_sl=np.ascontiguousarray(Cb[128 * b:128 * (b + 1), :]),
                    cst_s_sl=np.ascontiguousarray(Sb[128 * b:128 * (b + 1), :]))
               for b in range(B)]
    res = run_bass_kernel_spmd(nc, in_maps, core_ids=list(range(B)))
    out = np.stack([np.asarray(r["out"], np.float32) for r in res.results])
    return out



# version 47 (final)
# speedup vs baseline: 1.0351x; 1.0010x over previous
import numpy as np
import ml_dtypes
import concourse.bass as bass
import concourse.mybir as mybir
from concourse.bass_utils import run_bass_kernel_spmd

F32 = mybir.dt.float32
BF16 = mybir.dt.bfloat16
U8 = mybir.dt.uint8
AF = mybir.ActivationFunctionType
ALU = mybir.AluOpType
AX = mybir.AxisListType

D = 1024
H = 16
DH = 64
NG = 16
CHUNK = 512
QTS = 128
NEG = -1.0e30

ENGINES = ("te", "ve", "se", "po", "sy")
QDEPTH = 2


class Step:
    __slots__ = ("eng", "emit", "dma_queue", "done", "waits", "tag", "rec")

    def __init__(self, eng, emit, dma_queue, tag):
        self.eng = eng
        self.emit = emit
        self.dma_queue = dma_queue
        self.done = None
        self.waits = []
        self.tag = tag


class Prog:

    def __init__(self, nc):
        self.nc = nc
        self.steps = []
        self.eng_count = {e: 0 for e in ENGINES}
        self.dma_count = {}
        self.writer = {}
        self.readers = {}

    def step(self, eng, emit, reads=(), writes=(), dma_queue=None, tag=""):
        st = Step(eng, emit, dma_queue, tag)
        reads = list(reads)
        writes = list(writes)
        for k in list(reads):
            if k.startswith("ps"):
                reads.remove(k)
                writes.append(k)
        deps = []
        for k in reads:
            w = self.writer.get(k)
            if w is not None:
                deps.append(w)
        for k in writes:
            w = self.writer.get(k)
            if w is not None:
                deps.append(w)
            deps.extend(self.readers.get(k, ()))
        seen = set()
        for d in deps:
            if id(d) not in seen and d is not st:
                seen.add(id(d))
                st.waits.append(d)
        if dma_queue is None:
            self.eng_count[eng] += 1
            st.done = ("e_" + eng, self.eng_count[eng])
        else:
            n = self.dma_count.get(dma_queue, 0) + 1
            self.dma_count[dma_queue] = n
            st.done = ("d_" + dma_queue, 16 * n)
        for k in writes:
            self.writer[k] = st
            self.readers[k] = []
        for k in reads:
            self.readers.setdefault(k, []).append(st)
        self.steps.append(st)
        return st

    def barrier_deps(self):
        last = {}
        for st in self.steps:
            last[st.done[0]] = st
        return list(last.values())

    def seed(self, key, dep_steps):
        self.readers.setdefault(key, []).extend(dep_steps)

    def _cost(self, st):
        t = st.tag
        if st.dma_queue is not None:
            for pre, c in (("ldw_", 18.0), ("ld_C", 9.0), ("ld_S", 9.0),
                           ("x_in", 4.0), ("st_", 4.0), ("mp_out", 5.0),
                           ("ld_M", 9.0)):
                if t.startswith(pre):
                    return c
            return 1.5
        for pre, c in (("s2_", 0.9), ("s2g_", 1.8), ("sum_", 1.0),
                       ("mn_", 0.9), ("expmg", 0.6), ("expm", 0.6),
                       ("rcp_", 1.0), ("n1_", 0.5), ("n2_", 0.5),
                       ("av_", 1.1), ("ave_", 0.6), ("wo_", 2.2),
                       ("ffn_", 2.6), ("q_fm_mm", 3.6), ("k_fm_mm", 3.6),
                       ("v_mm", 1.9), ("q_fm_ev", 0.6), ("k_fm_ev", 0.6),
                       ("v_ev", 0.6), ("PTr_mm", 7.0), ("QTr_mm", 7.0), ("Mp_mm", 1.8),
                       ("PTr_ev", 0.6), ("QTr_ev", 0.6), ("Mp_ev", 0.6),
                       ("mtT", 0.3), ("lnz_ev", 0.55), ("lny_ev", 0.55),
                       ("lnz_bs", 0.5), ("lny_bs", 0.5),
                       ("wT_", 0.25), ("wTe_", 0.3), ("warm_", 3.5)):
            if t.startswith(pre):
                return c
        if "T_" in t and (t.startswith("h1_") or t.startswith("h2_")):
            return 0.25 if "Te_" not in t else 0.3
        if t.endswith("_bs"):
            return 0.9
        if "_sum" in t or "_sq" in t:
            return 1.3
        if "xm_" in t or "x2_" in t or "out_" in t or t.startswith("n1")  \
           or t.startswith("n2"):
            return 0.9
        return 0.2

    def schedule(self):
        import heapq
        nsteps = len(self.steps)
        for i, st in enumerate(self.steps):
            st.rec = i
        children = [[] for _ in range(nsteps)]
        nleft = [0] * nsteps
        for st in self.steps:
            deps = {id(d): d for d in st.waits}
            nleft[st.rec] = len(deps)
            for d in deps.values():
                children[d.rec].append(st)
        cp = [0.0] * nsteps
        for st in reversed(self.steps):
            m = 0.0
            for ch in children[st.rec]:
                m = max(m, 0.5 + cp[ch.rec])
            cp[st.rec] = self._cost(st) + m
        for st in self.steps:
            if st.tag.startswith("x_in0"):
                cp[st.rec] += 100.0
        lanes = list(ENGINES) + ["q_" + q for q in self.dma_count]
        eng_time = {e: 0.0 for e in lanes}
        pend = {e: [] for e in lanes}
        run = {e: [] for e in lanes}
        finish = [0.0] * nsteps
        ready_at = [0.0] * nsteps

        def lane(st):
            return st.eng if st.dma_queue is None else "q_" + st.dma_queue

        for st in self.steps:
            if nleft[st.rec] == 0:
                heapq.heappush(pend[lane(st)], (0.0, st.rec, st))
        order = {e: [] for e in lanes}
        n_done = 0
        while n_done < nsteps:
            best = None
            for e in lanes:
                while pend[e] and pend[e][0][0] <= eng_time[e]:
                    rt, rec, st = heapq.heappop(pend[e])
                    heapq.heappush(run[e], (-cp[rec], rec, st))
                cand = None
                if run[e]:
                    _, rec, st = run[e][0]
                    cand = (eng_time[e], rec, st, e, False)
                elif pend[e]:
                    rt, rec, st = pend[e][0]
                    cand = (rt, rec, st, e, True)
                if cand and (best is None or (cand[0], cand[1]) < (best[0], best[1])):
                    best = cand
            start, rec, st, e, from_pend = best
            if from_pend:
                heapq.heappop(pend[e])
            else:
                heapq.heappop(run[e])
            f = start + self._cost(st)
            eng_time[e] = f
            finish[rec] = f
            ready_at[rec] = start
            order[e].append(st)
            n_done += 1
            for ch in children[rec]:
                nleft[ch.rec] -= 1
                if nleft[ch.rec] == 0:
                    rt = 0.0
                    for d in ch.waits:
                        lat = 0.5 if (d.eng != ch.eng or d.dma_queue
                                      or ch.dma_queue) else 0.05
                        rt = max(rt, finish[d.rec] + lat)
                    heapq.heappush(pend[lane(ch)], (rt, ch.rec, ch))
        starts = {}
        for e in lanes:
            for pos, st in enumerate(order[e]):
                starts[st.rec] = ready_at[st.rec]
        per_eng = {e: [] for e in ENGINES}
        for st in self.steps:
            per_eng[st.eng].append(st)
        for e in ENGINES:
            per_eng[e].sort(key=lambda st: (st.tag == "final",
                                            starts[st.rec], st.rec))
        eng_n = {e: 0 for e in ENGINES}
        dma_n = {q: 0 for q in self.dma_count}
        for e in ENGINES:
            for st in per_eng[e]:
                if st.dma_queue is None:
                    eng_n[e] += 1
                    st.done = ("e_" + e, eng_n[e])
                else:
                    dma_n[st.dma_queue] += 1
                    st.done = ("d_" + st.dma_queue, 16 * dma_n[st.dma_queue])
        return per_eng

    def finalize(self, block, sems):
        per_eng = self.schedule()
        hw_mark = {}
        for e in ENGINES:
            for st in per_eng[e]:
                resolved = []
                for dep in st.waits:
                    sem_name, val = dep.done
                    k = (e, sem_name)
                    if hw_mark.get(k, -1) < val:
                        hw_mark[k] = val
                        resolved.append((sem_name, val))
                if st.dma_queue is not None:
                    sem_name = "d_" + st.dma_queue
                    prev = st.done[1] - 16 * QDEPTH
                    if prev > 0 and hw_mark.get((e, sem_name), -1) < prev:
                        hw_mark[(e, sem_name)] = prev
                        resolved.append((sem_name, prev))
                st.waits = resolved

        def make_body(e):
            def body(eng):
                for st in per_eng[e]:
                    for sem_name, val in st.waits:
                        eng.wait_ge(sems[sem_name], val)
                    last = st.emit(eng)
                    if st.dma_queue is None:
                        assert last is not None, f"step {st.tag} returned None"
                        last.then_inc(sems["e_" + e], 1)
            return body

        block.tensor(make_body("te"))
        block.vector(make_body("ve"))
        block.scalar(make_body("se"))
        block.gpsimd(make_body("po"))
        block.sync(make_body("sy"))


def host_constants():
    k = np.arange(D)
    ang = 2.0 * np.pi * np.outer(k, k) / D
    cns = {}
    cns["cst_C"] = np.cos(ang).astype(ml_dtypes.bfloat16)
    cns["cst_S"] = np.sin(ang).astype(ml_dtypes.bfloat16)
    i = np.arange(QTS)
    same = (i[:, None] // DH) == (i[None, :] // DH)
    causal = i[None, :] <= i[:, None]
    m2 = (same & causal)
    cns["cst_maskneg2T"] = np.ascontiguousarray(
        np.where(m2.T, 0.0, NEG)).astype(np.float32)
    cns["cst_mask4"] = np.ascontiguousarray(
        np.tile(np.where(m2.T, 0.0, NEG), (1, 4))).astype(
        ml_dtypes.bfloat16)
    mg0 = np.zeros((NG, QTS), np.float32)
    mg0[:, 0:DH] = NEG
    cns["cst_maskg0T"] = mg0
    cns["cst_ident"] = np.eye(128, dtype=ml_dtypes.bfloat16)
    cns["cst_ones128"] = np.ones((128, 128), ml_dtypes.bfloat16)
    sel = np.zeros((128, H * QTS), np.float32)
    for h in range(H):
        sel[h, h * QTS:(h + 1) * QTS] = -1.0
        sel[64 + h, h * QTS:(h + 1) * QTS] = -1.0
    sel[32, 0:QTS] = -1.0
    cns["cst_sel"] = sel.astype(ml_dtypes.bfloat16)
    cns["cst_ones"] = np.ones((1, 128), ml_dtypes.bfloat16)
    return cns


def build_nc(T, ncores=1):
    nchunks = T // CHUNK
    nc = bass.Bass("TRN2", target_bir_lowering=False)

    x_h = nc.dram_tensor("x", [T, D], F32, kind="ExternalInput").ap()
    out_h = nc.dram_tensor("out", [T, D], F32, kind="ExternalOutput").ap()
    wq_h = nc.dram_tensor("wq", [D, D], F32, kind="ExternalInput").ap()
    wk_h = nc.dram_tensor("wk", [D, D], F32, kind="ExternalInput").ap()
    wv_h = nc.dram_tensor("wv", [D, D], F32, kind="ExternalInput").ap()
    wo_h = nc.dram_tensor("wo", [D, D], F32, kind="ExternalInput").ap()
    wr_h = nc.dram_tensor("wr_sl", [D, 128], F32, kind="ExternalInput").ap()
    wi_h = nc.dram_tensor("wi_sl", [D, 128], F32, kind="ExternalInput").ap()
    csl_h = nc.dram_tensor("cst_c_sl", [128, D], BF16, kind="ExternalInput").ap()
    ssl_h = nc.dram_tensor("cst_s_sl", [128, D], BF16, kind="ExternalInput").ap()
    mpart_h = nc.dram_tensor("mpart", [D, D], BF16, kind="Internal").ap()
    mfull_h = nc.dram_tensor("mfull", [D, D], BF16, kind="Internal").ap()
    bias_h = nc.dram_tensor("fbias", [1, D], F32, kind="ExternalInput").ap()
    C_h = nc.dram_tensor("cst_C", [D, D], BF16, kind="ExternalInput").ap()
    S_h = nc.dram_tensor("cst_S", [D, D], BF16, kind="ExternalInput").ap()
    mn2_h = nc.dram_tensor("cst_maskneg2T", [QTS, QTS], F32, kind="ExternalInput").ap()
    m4_h = nc.dram_tensor("cst_mask4", [QTS, 4 * QTS], BF16, kind="ExternalInput").ap()
    mg0_h = nc.dram_tensor("cst_maskg0T", [NG, QTS], F32, kind="ExternalInput").ap()
    o128_h = nc.dram_tensor("cst_ones128", [128, 128], BF16, kind="ExternalInput").ap()
    id_h = nc.dram_tensor("cst_ident", [128, 128], BF16, kind="ExternalInput").ap()
    sel_h = nc.dram_tensor("cst_sel", [128, H * QTS], BF16, kind="ExternalInput").ap()
    ones_h = nc.dram_tensor("cst_ones", [1, 128], BF16, kind="ExternalInput").ap()

    def sbuf(name, shape, dt):
        return nc.alloc_sbuf_tensor(name, shape, dt).ap()

    wqt = sbuf("wqt", [128, 8 * D], BF16)
    wkt = sbuf("wkt", [128, 8 * D], BF16)
    wvt = sbuf("wvt", [128, 8 * D], BF16)
    wot = sbuf("wot", [128, 8 * D], BF16)
    Msb = sbuf("Msb", [128, 8 * D], BF16)
    maskn2T = sbuf("maskn2T", [QTS, QTS], F32)
    maskg0T = sbuf("maskg0T", [NG, QTS], F32)
    allones = sbuf("allones", [128, 128], BF16)
    ident = sbuf("ident", [128, 128], BF16)
    ones_r = sbuf("ones_r", [1, 128], BF16)
    bias_b = sbuf("bias_b", [1, D], BF16)
    kg = sbuf("kg", [128, 8 * NG], BF16)
    vg = sbuf("vg", [48, D], BF16)

    def pair(name, shape, dt):
        return [sbuf(name + "0", shape, dt), sbuf(name + "1", shape, dt)]

    mcol_p = pair("mcol", [128, 1], F32)
    rcol_p = pair("rcol", [128, 1], F32)
    ncol_p = pair("ncol", [128, 1], F32)
    scol_p = pair("scol", [128, 1], F32)
    s2col_p = pair("s2col", [128, 1], F32)
    rcol1_p = pair("rcol1", [128, 1], F32)
    ncol1_p = pair("ncol1", [128, 1], F32)
    bnb_p = pair("bnb", [128, 12], F32)
    mv_p = pair("mv", [128, 2], F32)
    bnb1_p = pair("bnb1", [128, 12], F32)
    mv1_p = pair("mv1", [128, 2], F32)
    epsc = sbuf("epsc", [128, 1], F32)

    AREN = 115 * 1024
    arena = nc.alloc_sbuf_tensor("arena", [128, AREN], U8)
    ar_f = arena.bitcast(F32).ap()
    ar_b = arena.bitcast(BF16).ap()

    off = [0]
    extents = []

    def carve(nbytes, align=64):
        off[0] = (off[0] + align - 1) // align * align
        o = off[0]
        off[0] += nbytes
        assert off[0] <= AREN, f"arena overflow {off[0]}"
        extents.append((o, off[0]))
        return o

    def cf(cols):
        o = carve(cols * 4)
        return ar_f[:, o // 4: o // 4 + cols]

    def cb(cols):
        o = carve(cols * 2)
        return ar_b[:, o // 2: o // 2 + cols]

    extA = {}

    def mark(name):
        extA.setdefault(name, extents[-1])
    offA = off[0]
    C_sb = cb(8 * D)
    mark("C")
    S_sb = cb(8 * D)
    mark("S")
    Sn_sb = S_sb
    wrsl_sb = cb(8 * 128)
    mark("wrsl")
    wisl_sb = cb(8 * 128)
    mark("wisl")
    csl_sb = cb(D)
    mark("csl")
    ssl_sb = cb(D)
    mark("ssl")
    PTr_sb = cb(D)
    mark("PTr")
    QTr_sb = cb(D)
    mark("QTr")
    Mp_sb = cb(8 * D)
    mark("Mp")
    Wst_sb = cb(8 * D)
    mark("WstA")
    Wst2_sb = cb(8 * D)
    mark("WstB")
    end_A = off[0]

    off[0] = offA
    extM = {}

    def markM(keys):
        extM[extents[-1]] = keys
    x_p = [cf(4 * D), cf(4 * D)]
    markM([f"x1_{q}" for q in range(4)])
    extM[extents[-2]] = [f"x0_{q}" for q in range(4)]
    xm_p = [cf(D), cf(D)]
    markM(["xm1"])
    extM[extents[-2]] = ["xm0"]
    h_tm = cb(4 * D)
    markM([f"h{q}{s}" for q in range(4) for s in "ab"])
    h_fm = cb(4 * D)
    markM([f"hfm{j}_{q}" for j in range(8) for q in range(4)])
    q_fm = cb(4 * D)
    markM([f"q_fm{i}" for i in range(8)])
    k_fm = cb(4 * D)
    markM([f"k_fm{i}" for i in range(8)])
    v_tm = cb(4 * D)
    markM([f"v{q}" for q in range(4)])
    ao_fm = cb(4 * D)
    markM([f"ao{g}_{q}" for g in range(2) for q in range(4)])
    p2_p = [cb(8 * QTS), cb(8 * QTS)]
    markM(["p2a0_1", "p2b0_1"])
    extM[extents[-2]] = ["p2a0_0", "p2b0_0"]
    p2B_p = [cb(8 * QTS), cb(8 * QTS)]
    markM(["p2a1_1", "p2b1_1"])
    extM[extents[-2]] = ["p2a1_0", "p2b1_0"]
    p2g_p = [cb(512), cb(512)]
    markM(["p2g_1"])
    extM[extents[-2]] = ["p2g_0"]
    z_p = [cb(D), cb(D)]
    markM(["z1a", "z1b"])
    extM[extents[-2]] = ["z0a", "z0b"]
    rec_p = [cf(512), cf(512)]
    markM(["rec1"])
    extM[extents[-2]] = ["rec0"]
    junk = cb(D)
    markM(["junk"])
    end_main = off[0]
    assert max(end_A, end_main) <= AREN, (end_A, end_main)

    psA = nc.alloc_psum_tensor("psA", [128, 4096], F32)
    ps_f = psA.ap()
    ps_b = psA.bitcast(BF16).ap()

    ATOF = [0, 2048]
    WOBANKS = [["ps0", "ps1"], ["ps4", "ps5"]]
    PARKA_BF = [2688, 5760]
    def parkA(pb):
        return ps_b[:, PARKA_BF[pb]: PARKA_BF[pb] + 128]

    P = Prog(nc)
    ST = P.step
    sems = {}

    def ld(dst, src, queue, tag, writes, reads=()):
        def em(eng):
            return eng.dma_start(out=dst, in_=src).then_inc(
                sems["d_" + queue], 16)
        return ST({"syq": "sy", "poq": "po", "seq": "se",
                   "veq": "ve"}[queue], em,
                  reads=reads, writes=writes, dma_queue=queue, tag=tag)

    ld(maskn2T, mn2_h, "syq", "ld_mn2t", ["maskn2T"])
    ld(maskg0T, mg0_h, "syq", "ld_mg0t", ["maskg0T"])

    def em_eps(v):
        return v.memset(epsc, 1e-5)
    ST("ve", em_eps, reads=[], writes=["epsc"], tag="epsc")
    ld(allones, o128_h, "syq", "ld_o128", ["allones"])
    ld(ident, id_h, "syq", "ld_id", ["ident"])
    ld(ones_r, ones_h, "syq", "ld_ones", ["ones_r"])
    ld(bias_b, bias_h, "poq", "ld_bias", ["bias_b"])

    def chunks(t):
        return t.rearrange("p (c d) -> p c d", d=D)

    ld(chunks(C_sb), C_h.rearrange("(j p) m -> p j m", p=128),
       "syq", "ld_C", [f"C{jc}" for jc in range(8)])
    ld(chunks(S_sb), S_h.rearrange("(j p) m -> p j m", p=128),
       "syq", "ld_S", [f"S{jc}" for jc in range(8)])
    ld(wrsl_sb.rearrange("p (b m) -> p b m", m=128),
       wr_h.rearrange("(b p) m -> p b m", p=128), "poq", "ld_wrsl", ["wrsl"])
    ld(wisl_sb.rearrange("p (b m) -> p b m", m=128),
       wi_h.rearrange("(b p) m -> p b m", p=128), "poq", "ld_wisl", ["wisl"])
    ld(csl_sb, csl_h, "syq", "ld_csl", ["csl"])
    ld(ssl_sb, ssl_h, "syq", "ld_ssl", ["ssl"])

    def sneg_steps():
        for jc in range(8):
            def em(eng, jc=jc):
                return eng.activation(chunks(Sn_sb)[:, jc, :],
                                      chunks(S_sb)[:, jc, :], AF.Copy,
                                      scale=-1.0)
            ST("se", em, reads=[f"S{jc}"], writes=[f"S{jc}", f"Sn{jc}"],
               tag=f"sneg{jc}")

    ck = [f"C{j}" for j in range(8)]
    sk = [f"S{j}" for j in range(8)]
    snk = [f"Sn{j}" for j in range(8)]

    def shard_chain(dst_sb, dkey, l_a, ka, l_b, kb, r_a, r_b, rka, rkb,
                    scale, tag):
        def em_mm(eng):
            last = None
            for ah in range(2):
                for b in range(8):
                    last = eng.matmul(
                        ps_f[:, 512 * ah:512 * ah + 512],
                        l_a.rearrange("p (b m) -> p b m", m=128)[:, b, :],
                        chunks(r_a)[:, b, 512 * ah:512 * (ah + 1)],
                        start=(b == 0), stop=False)
                for b in range(8):
                    last = eng.matmul(
                        ps_f[:, 512 * ah:512 * ah + 512],
                        l_b.rearrange("p (b m) -> p b m", m=128)[:, b, :],
                        chunks(r_b)[:, b, 512 * ah:512 * (ah + 1)],
                        start=False, stop=(b == 7))
            return last
        ST("te", em_mm, reads=[ka, kb] + rka + rkb,
           writes=["ps0", "ps1"], tag=f"{tag}_mm")

        def em_ev(eng):
            return eng.activation(dst_sb, ps_f[:, 0:1024], AF.Copy,
                                  scale=scale)
        ST("se", em_ev, reads=["ps0", "ps1"], writes=[dkey],
           tag=f"{tag}_ev")

    shard_chain(PTr_sb, "PTr", wrsl_sb, "wrsl", wisl_sb, "wisl",
                C_sb, S_sb, ck, sk, 1.0, "PTr")
    sneg_steps()

    def em_snl(eng):
        return eng.activation(ssl_sb, ssl_sb, AF.Copy, scale=-1.0)
    ST("se", em_snl, reads=["ssl"], writes=["ssl", "snl"], tag="snl")

    shard_chain(QTr_sb, "QTr", wisl_sb, "wisl", wrsl_sb, "wrsl",
                C_sb, Sn_sb, ck, snk, 1.0, "QTr")

    for tc in range(0, 8, 2):
        def em_mm(eng, tc=tc):
            last = None
            for k in range(4):
                t2 = tc + k // 2
                ah = k % 2
                eng.matmul(ps_f[:, 512 * k:512 * k + 512],
                           PTr_sb[:, 128 * t2:128 * (t2 + 1)],
                           csl_sb[:, 512 * ah:512 * (ah + 1)],
                           start=True, stop=False)
                last = eng.matmul(ps_f[:, 512 * k:512 * k + 512],
                                  QTr_sb[:, 128 * t2:128 * (t2 + 1)],
                                  ssl_sb[:, 512 * ah:512 * (ah + 1)],
                                  start=False, stop=True)
            return last
        ST("te", em_mm, reads=["PTr", "QTr", "csl", "snl"],
           writes=["ps0", "ps1", "ps2", "ps3"], tag=f"Mp_mm{tc}")

        def em_ev(eng, tc=tc):
            return eng.activation(
                chunks(Mp_sb).transpose([0, 1, 2])[:, tc:tc + 2, :]
                .rearrange("p a b -> p (a b)"),
                ps_f[:, 0:2048], AF.Copy, scale=1.0 / D)
        ST("se", em_ev, reads=["ps0", "ps1", "ps2", "ps3"],
           writes=[f"Mp{tc}", f"Mp{tc + 1}"], tag=f"Mp_ev{tc}")

    ld(mpart_h.rearrange("(tc p) u -> p tc u", p=128),
       Mp_sb.rearrange("p (tc u) -> p tc u", u=D), "syq", "mp_out",
       ["mpartd"], reads=[f"Mp{j}" for j in range(8)])
    if ncores > 1:
        def em_cc(eng):
            return eng.collective_compute(
                "AllReduce", ALU.add,
                replica_groups=[list(range(ncores))],
                ins=[mpart_h.opt()], outs=[mfull_h.opt()])
        ST("po", em_cc, reads=["mpartd"], writes=["mfulld"],
           tag="allreduce")
    else:
        ld(mfull_h, mpart_h, "poq", "mp_copy", ["mfulld"], reads=["mpartd"])
    ld(chunks(Msb), mfull_h.rearrange("(j p) m -> p j m", p=128),
       "syq", "ld_M", [f"M{jc}" for jc in range(8)], reads=["mfulld"])

    for (w_hbm, wt_sb, nm, st_sb, sk, q) in (
            (wq_h, wqt, "q", Wst_sb, "WstA", "poq"),
            (wk_h, wkt, "k", Wst2_sb, "WstB", "poq"),
            (wv_h, wvt, "v", Wst_sb, "WstA", "poq"),
            (wo_h, wot, "o", Wst2_sb, "WstB", "poq")):
        ld(chunks(st_sb), w_hbm.rearrange("(i p) m -> p i m", p=128),
           q, f"ldw_{nm}", [f"{sk}{ic}" for ic in range(8)])
        for ic in range(8):
            for jc in range(8):
                pb = (ic * 8 + jc) % 2

                def em_t(eng, ic=ic, jc=jc, pb=pb, st_sb=st_sb):
                    return eng.transpose(
                        parkA(pb),
                        chunks(st_sb)[:, ic, 128 * jc:128 * (jc + 1)], ident)
                ST("te", em_t, reads=[f"{sk}{ic}", "ident"],
                   writes=["ps2" if pb == 0 else "ps5"], tag=f"wT_{nm}{ic}_{jc}")

                def em_e(eng, wt_sb=wt_sb, ic=ic, jc=jc, pb=pb):
                    return eng.tensor_copy(
                        chunks(wt_sb)[:, jc, 128 * ic:128 * (ic + 1)],
                        parkA(pb))
                ST("ve", em_e, reads=["ps2" if pb == 0 else "ps5"], writes=[f"wt_{nm}{jc}"],
                   tag=f"wTe_{nm}{ic}_{jc}")

    barrier = P.barrier_deps()
    for mkeys in extM.values():
        for k in mkeys:
            P.seed(k, barrier)

    x_src = x_h.rearrange("(c q p) d -> c p q d", p=128, q=4)
    out_dst = out_h.rearrange("(c q p) d -> c p q d", p=128, q=4)
    hfmv = h_fm.rearrange("p (j t) -> p j t", t=CHUNK)
    qfmv = q_fm.rearrange("p (i t) -> p i t", t=CHUNK)
    kfmv = k_fm.rearrange("p (i t) -> p i t", t=CHUNK)
    aov = ao_fm.rearrange("p (r t) -> p r t", t=CHUNK)
    kgv = kg.rearrange("p (j g) -> p j g", g=NG)
    vtv = v_tm.rearrange("p (q i) -> p q i", i=D)
    xv_p = [x_p[0].rearrange("p (q d) -> p q d", d=D),
            x_p[1].rearrange("p (q d) -> p q d", d=D)]
    hv = h_tm.rearrange("p (q d) -> p q d", d=D)

    def qslice(hh, qt, a=0, b=QTS):
        return qfmv[:, hh // 2, 128 * qt + a:128 * qt + b][
            64 * (hh % 2):64 * (hh % 2) + 64, :]

    def kslice(hh, qt, a=0, b=QTS):
        return kfmv[:, hh // 2, 128 * qt + a:128 * qt + b][
            64 * (hh % 2):64 * (hh % 2) + 64, :]

    def kgsl(hh):
        return kgv[:, hh // 2, :][64 * (hh % 2):64 * (hh % 2) + 64, :]

    def ln_stats(src_ap, src_keys, pp, tag, own=False):
        if own:
            bnb, mv = bnb1_p[pp], mv1_p[pp]
            rcol, ncol = rcol1_p[pp], ncol1_p[pp]
            sx = "b"
        else:
            bnb, mv = bnb_p[pp], mv_p[pp]
            rcol, ncol = rcol_p[pp], ncol_p[pp]
            sx = ""
        K = lambda n: n + sx + str(pp)

        def em_bs(v):
            v.bn_stats(bnb[:, 0:6], src_ap[:, 0:512])
            return v.bn_stats(bnb[:, 6:12], src_ap[:, 512:1024])
        ST("ve", em_bs, reads=src_keys, writes=[K("bnb")], tag=f"{tag}_bs")

        def em_ba(v):
            return v.bn_aggr(mv, bnb)
        ST("ve", em_ba, reads=[K("bnb")], writes=[K("mv")],
           tag=f"{tag}_ba")

        def em_rr(se):
            se.activation(ncol, mv[:, 1:2], AF.Ln, bias=epsc[:, 0:1])
            return se.activation(rcol, ncol, AF.Exp, scale=-0.5)
        ST("se", em_rr, reads=[K("mv"), "epsc"],
           writes=[K("rcol"), K("ncol")], tag=f"{tag}_rr")

    def transpose_htm(qt, tag):
        pp = qt % 2
        bk = WOBANKS[pp] + WOBANKS[1 - pp]
        c0 = [2 * ATOF[pp], 2 * ATOF[pp] + 1024,
              2 * ATOF[1 - pp], 2 * ATOF[1 - pp] + 1024]

        def park(i):
            return ps_b[:, c0[i]: c0[i] + 128]
        for jc in range(8):
            pb = jc % 4

            def em_t(eng, qt=qt, jc=jc, pb=pb, park=park):
                return eng.transpose(
                    park(pb),
                    h_tm.rearrange("p (q j d) -> p q j d", q=4, j=8)[:, qt, jc, :],
                    ident)
            ST("te", em_t, reads=[f"h{qt}{'ab'[jc // 4]}", "ident"],
               writes=[bk[pb]], tag=f"{tag}T_{qt}_{jc}")

            if pb % 2 == 0:
                def em_e(eng, qt=qt, jc=jc, pb=pb, park=park):
                    return eng.tensor_copy(
                        hfmv[:, jc, 128 * qt:128 * qt + 128], park(pb))
                ST("ve", em_e, reads=[bk[pb]], writes=[f"hfm{jc}_{qt}"],
                   tag=f"{tag}Te_{qt}_{jc}")
            else:
                def em_e(eng, qt=qt, jc=jc, pb=pb, park=park):
                    return eng.copy(
                        hfmv[:, jc, 128 * qt:128 * qt + 128], park(pb))
                ST("se", em_e, reads=[bk[pb]], writes=[f"hfm{jc}_{qt}"],
                   tag=f"{tag}Te_{qt}_{jc}")

    def stage_ln1(c, qt):
        pp = qt % 2
        cp = c % 2
        xq = xv_p[cp][:, qt, :]
        hq = hv[:, qt, :]
        ln_stats(xq, [f"x{cp}_{qt}"], pp, f"ln1_{c}_{qt}", own=True)

        for hf in range(2):
            def em_n1(v, xq=xq, hq=hq, pp=pp, hf=hf):
                return v.tensor_scalar(hq[:, 512 * hf:512 * (hf + 1)],
                                       xq[:, 512 * hf:512 * (hf + 1)],
                                       mv1_p[pp][:, 0:1],
                                       rcol1_p[pp][:, 0:1],
                                       op0=ALU.subtract, op1=ALU.mult)
            ST("ve", em_n1,
               reads=[f"x{cp}_{qt}", f"mvb{pp}", f"rcolb{pp}"],
               writes=[f"h{qt}{'ab'[hf]}"], tag=f"n1_{c}_{qt}{hf}")
        transpose_htm(qt, f"h1_{c}")

    def stage_qkv(c):
        hfm_keys = [f"hfm{j}_{q}" for j in range(8) for q in range(4)]
        rot = [("ps0", 0), ("ps1", 512), ("ps3", 1536), ("ps4", 2048)]
        rr = [0]
        for (wt_sb, wtk, dst, dk, scale) in (
                (wqt, "q", q_fm, "q_fm", 1.0), (wkt, "k", k_fm, "k_fm", 0.125)):
            for ic0 in range(0, 8, 2):
                pk = [rot[rr[0] % 4], rot[(rr[0] + 1) % 4]]
                rr[0] += 2

                def em_mm(eng, wt_sb=wt_sb, ic0=ic0, pk=pk):
                    last = None
                    for k in range(2):
                        ic = ic0 + k
                        c0 = pk[k][1]
                        for jc in range(8):
                            last = eng.matmul(
                                ps_f[:, c0:c0 + 512],
                                chunks(wt_sb)[:, jc, 128 * ic:128 * (ic + 1)],
                                hfmv[:, jc, :],
                                start=(jc == 0), stop=(jc == 7))
                    return last
                ST("te", em_mm,
                   reads=hfm_keys + [f"wt_{wtk}{j}" for j in range(8)],
                   writes=[pk[0][0], pk[1][0]], tag=f"{dk}_mm{c}_{ic0}")

                for k in range(2):
                    def em_ev(eng, dst=dst, ic=ic0 + k, c0=pk[k][1],
                              scale=scale):
                        return eng.tensor_scalar(
                            dst.rearrange("p (i t) -> p i t", t=CHUNK)[:, ic, :],
                            ps_f[:, c0:c0 + 512], scale, None, op0=ALU.mult)
                    ST("ve", em_ev, reads=[pk[k][0]], writes=[f"{dk}{ic0 + k}"],
                       tag=f"{dk}_ev{c}_{ic0 + k}")
        for qt in range(4):
            for ih in range(2):
                bk, c0 = rot[rr[0] % 4]
                rr[0] += 1

                def em_mm(eng, qt=qt, ih=ih, c0=c0):
                    last = None
                    for jc in range(8):
                        last = eng.matmul(
                            ps_f[:, c0:c0 + 512],
                            hfmv[:, jc, 128 * qt:128 * qt + 128],
                            chunks(wvt)[:, jc, 512 * ih:512 * (ih + 1)],
                            start=(jc == 0), stop=(jc == 7))
                    return last
                ST("te", em_mm,
                   reads=[f"hfm{j}_{qt}" for j in range(8)]
                   + [f"wt_v{j}" for j in range(8)],
                   writes=[bk], tag=f"v_mm{c}_{qt}_{ih}")

                def em_ev(eng, qt=qt, ih=ih, c0=c0):
                    return eng.tensor_copy(
                        vtv[:, qt, 512 * ih:512 * (ih + 1)],
                        ps_f[:, c0:c0 + 512])
                ST("ve", em_ev, reads=[bk], writes=[f"v{qt}"],
                   tag=f"v_ev{c}_{qt}_{ih}")

    def stage_kgvg():
        def em_kg(v):
            return v.tensor_copy(kgv, kfmv[:, :, 0:NG])
        ST("ve", em_kg, reads=[f"k_fm{j}" for j in range(8)],
           writes=["kg"], tag="kg")

        def em_vg(v):
            v.tensor_copy(vg[0:NG, :], vtv[0:NG, 0, :])
            return v.tensor_copy(vg[32:32 + NG, :], vtv[0:NG, 0, :])
        ST("ve", em_vg, reads=["v0"], writes=["vg"], tag="vg")

    def stage_p2(c, qt):
        pp = qt % 2
        OF = ATOF[pp]
        B = [f"ps{4 * pp + i}" for i in range(4)]
        p2g, rec = p2g_p[pp], rec_p[pp]
        o_q = [OF, OF + 512]
        o_g = OF + 1024
        o_s = OF + 1536
        for g in range(2):
            h0 = 8 * g
            p2 = (p2_p if g == 0 else p2B_p)[pp]
            KA, KB = f"p2a{g}_{pp}", f"p2b{g}_{pp}"

            for quarter in range(2):
                def em_raw(eng, qt=qt, h0=h0, quarter=quarter):
                    last = None
                    for i in range(4):
                        hh = h0 + 2 * i + quarter
                        o = o_q[quarter] + i * QTS
                        last = eng.matmul(ps_f[:, o:o + QTS], kslice(hh, qt),
                                          qslice(hh, qt), start=(i == 0),
                                          stop=(i == 3))
                    return last
                ST("te", em_raw,
                   reads=[f"k_fm{j}" for j in range(8)]
                   + [f"q_fm{j}" for j in range(8)],
                   writes=[B[quarter]], tag=f"s2_{c}_{qt}_{g}{quarter}")

            for quarter in range(2):
                def em_mn(v, quarter=quarter):
                    o = o_q[quarter]
                    return v.tensor_tensor(
                        ps_f[:, o:o + 512].rearrange("p (h k) -> p h k", k=QTS),
                        ps_f[:, o:o + 512].rearrange("p (h k) -> p h k", k=QTS),
                        maskn2T.unsqueeze(1).broadcast_to([QTS, 4, QTS]),
                        op=ALU.add)
                ST("ve", em_mn, reads=[B[quarter], "maskn2T"],
                   writes=[B[quarter]], tag=f"mn_{c}_{qt}_{g}{quarter}")

            def em_graw(eng, qt=qt, h0=h0):
                last = None
                for par in range(2):
                    pb = 32 * par
                    for i in range(4):
                        hh = h0 + 2 * i + par
                        o = o_g + i * QTS
                        last = eng.matmul(ps_f[pb:pb + 16, o:o + QTS],
                                          kgsl(hh), qslice(hh, qt),
                                          start=(i == 0), stop=(i == 3))
                return last
            ST("te", em_graw,
               reads=[f"q_fm{j}" for j in range(8)] + ["kg"],
               writes=[B[2]], tag=f"s2g_{c}_{qt}_{g}")

            if c == 0 and qt == 0:
                def em_mng(v):
                    v.tensor_tensor(
                        ps_f[0:16, o_g:o_g + 512].rearrange(
                            "p (h k) -> p h k", k=QTS),
                        ps_f[0:16, o_g:o_g + 512].rearrange(
                            "p (h k) -> p h k", k=QTS),
                        maskg0T.unsqueeze(1).broadcast_to([NG, 4, QTS]),
                        op=ALU.add)
                    return v.tensor_tensor(
                        ps_f[32:48, o_g:o_g + 512].rearrange(
                            "p (h k) -> p h k", k=QTS),
                        ps_f[32:48, o_g:o_g + 512].rearrange(
                            "p (h k) -> p h k", k=QTS),
                        maskg0T.unsqueeze(1).broadcast_to([NG, 4, QTS]),
                        op=ALU.add)
                ST("ve", em_mng, reads=[B[2], "maskg0T"], writes=[B[2]],
                   tag=f"mng_{c}_{qt}_{g}")

            for quarter in range(2):
                def em_expm(se, p2=p2, quarter=quarter):
                    return se.activation(
                        p2[:, 512 * quarter:512 * (quarter + 1)],
                        ps_f[:, o_q[quarter]:o_q[quarter] + 512], AF.Exp)
                ST("se", em_expm, reads=[B[quarter]],
                   writes=[[KA, KB][quarter]],
                   tag=f"expm_{c}_{qt}_{g}{quarter}")

            def em_expmg(se, p2g=p2g):
                se.activation(p2g[0:16, 0:512],
                              ps_f[0:16, o_g:o_g + 512], AF.Exp)
                return se.activation(p2g[32:48, 0:512],
                                     ps_f[32:48, o_g:o_g + 512], AF.Exp)
            ST("se", em_expmg, reads=[B[2]], writes=[f"p2g_{pp}"],
               tag=f"expmg_{c}_{qt}_{g}")

            def em_sum(eng, p2=p2, p2g=p2g):
                eng.matmul(ps_f[0:64, o_g:o_g + 512], allones[:, 0:64],
                           p2[:, 0:512], start=True, stop=False)
                eng.matmul(ps_f[0:64, o_g:o_g + 512], allones[0:16, 0:64],
                           p2g[0:16, 0:512], start=False, stop=False)
                eng.matmul(ps_f[64:128, o_g:o_g + 512], allones[:, 64:128],
                           p2[:, 512:1024], start=True, stop=False)
                return eng.matmul(ps_f[64:128, o_g:o_g + 512],
                                  allones[32:48, 64:128],
                                  p2g[32:48, 0:512], start=False, stop=True)
            ST("te", em_sum, reads=[KA, KB, f"p2g_{pp}",
                                    "allones", B[2]],
               writes=[B[2]], tag=f"sum_{c}_{qt}_{g}")

            def em_rec(se, rec=rec):
                se.activation(rec, ps_f[:, o_g:o_g + 512], AF.Ln)
                return se.activation(rec, rec, AF.Exp, scale=-1.0)
            ST("se", em_rec, reads=[B[2]], writes=[f"rec{pp}"],
               tag=f"rcp_{c}_{qt}_{g}")

            for par in range(2):
                def em_av(eng, qt=qt, h0=h0, p2=p2, p2g=p2g, par=par):
                    last = None
                    for i in range(par, 8, 2):
                        hh = h0 + i
                        pos = (i % 2) * 4 + i // 2
                        ob = o_s + (i // 2) * QTS
                        dst = ps_f[:, ob:ob + QTS][
                            64 * (hh % 2):64 * (hh % 2) + 64, :]
                        eng.matmul(
                            dst,
                            v_tm[:, qt * D + hh * DH: qt * D + hh * DH + DH],
                            p2[:, pos * QTS:(pos + 1) * QTS],
                            start=True, stop=False)
                        vb = 32 * (i % 2)
                        last = eng.matmul(
                            dst, vg[vb:vb + NG, hh * DH:(hh + 1) * DH],
                            p2g[vb:vb + NG,
                                (i // 2) * QTS:(i // 2) * QTS + QTS],
                            start=False, stop=True)
                    return last
                ST("te", em_av,
                   reads=[f"v{qt}", "vg", [KA, KB][par],
                          f"p2g_{pp}"],
                   writes=[B[3]], tag=f"av_{c}_{qt}_{g}{par}")

            def em_ave(v, qt=qt, g=g, rec=rec):
                return v.tensor_tensor(
                    aov[:, 4 * g:4 * g + 4, 128 * qt:128 * qt + 128],
                    ps_f[:, o_s:o_s + 512].rearrange(
                        "p (r t) -> p r t", t=QTS),
                    rec.rearrange("p (r t) -> p r t", t=QTS),
                    op=ALU.mult)
            ST("ve", em_ave, reads=[B[3], f"rec{pp}"], writes=[f"ao{g}_{qt}"],
               tag=f"ave_{c}_{qt}_{g}")

    def stage_w(c, qt):
        pp = qt % 2
        cp = c % 2
        of = ATOF[pp]
        bks = WOBANKS[pp]
        xq = xv_p[cp][:, qt, :]
        mcol, rcol, ncol = mcol_p[pp], rcol_p[pp], ncol_p[pp]
        scol, s2col = scol_p[pp], s2col_p[pp]
        xm_sb = xm_p[pp]

        for ih in range(2):
            def em_wo(eng, qt=qt, of=of, ih=ih):
                last = None
                for jc in range(8):
                    last = eng.matmul(
                        ps_f[:, of + 512 * ih:of + 512 * ih + 512],
                        aov[:, jc, 128 * qt:128 * qt + 128],
                        chunks(wot)[:, jc, 512 * ih:512 * (ih + 1)],
                        start=(jc == 0), stop=(jc == 7))
                return last
            ST("te", em_wo,
               reads=[f"ao{g}_{qt}" for g in range(2)]
               + [f"wt_o{j}" for j in range(8)],
               writes=[bks[ih]], tag=f"wo_{c}_{qt}{ih}")

        z_sb = z_p[pp]

        bnb, mv = bnb_p[pp], mv_p[pp]
        for hf in range(2):
            def em_zev(se, of=of, z_sb=z_sb, hf=hf):
                return se.activation(
                    z_sb[:, 512 * hf:512 * (hf + 1)],
                    ps_f[:, of + 512 * hf:of + 512 * (hf + 1)], AF.Copy)
            ST("se", em_zev, reads=[bks[hf]], writes=[f"z{pp}{'ab'[hf]}"],
               tag=f"lnz_ev_{c}_{qt}{hf}")

            def em_zbs(v, z_sb=z_sb, hf=hf, bnb=bnb):
                return v.bn_stats(bnb[:, 6 * hf:6 * hf + 6],
                                  z_sb[:, 512 * hf:512 * (hf + 1)])
            ST("ve", em_zbs, reads=[f"z{pp}{'ab'[hf]}"], writes=[f"bnb{pp}"],
               tag=f"lnz_bs_{c}_{qt}{hf}")

        def em_zba(v, bnb=bnb, mv=mv):
            return v.bn_aggr(mv, bnb)
        ST("ve", em_zba, reads=[f"bnb{pp}"], writes=[f"mv{pp}"],
           tag=f"lnz_ba_{c}_{qt}")

        def em_rr(se, mv=mv, rcol=rcol, ncol=ncol):
            se.activation(ncol, mv[:, 1:2], AF.Ln, bias=epsc[:, 0:1])
            return se.activation(rcol, ncol, AF.Exp, scale=-0.5)
        ST("se", em_rr, reads=[f"mv{pp}", "epsc"],
           writes=[f"rcol{pp}", f"ncol{pp}"], tag=f"lnz_rr_{c}_{qt}")

        def em_nmr(v, rcol=rcol, mv=mv, ncol=ncol):
            return v.tensor_scalar(ncol, rcol, mv[:, 0:1], -1.0,
                                   op0=ALU.mult, op1=ALU.mult)
        ST("ve", em_nmr, reads=[f"rcol{pp}", f"mv{pp}"],
           writes=[f"ncol{pp}"], tag=f"lnz_nmr_{c}_{qt}")

        def em_xm(v, xq=xq, ncol=ncol, xm_sb=xm_sb):
            return v.tensor_scalar(xm_sb, xq, ncol[:, 0:1], None, op0=ALU.add)
        ST("ve", em_xm, reads=[f"x{cp}_{qt}", f"ncol{pp}"],
           writes=[f"xm{pp}"], tag=f"lnz_xm_{c}_{qt}")

        def em_x2(v, xq=xq, rcol=rcol, xm_sb=xm_sb, z_sb=z_sb):
            return v.scalar_tensor_tensor(xq, z_sb, rcol[:, 0:1], xm_sb,
                                          op0=ALU.mult, op1=ALU.add)
        ST("ve", em_x2, reads=[f"z{pp}a", f"z{pp}b", f"rcol{pp}",
                               f"xm{pp}"],
           writes=[f"x{cp}_{qt}"], tag=f"x2_{c}_{qt}")

    def stage_h2(c, qt):
        pp = qt % 2
        cp = c % 2
        xq = xv_p[cp][:, qt, :]
        hq = hv[:, qt, :]
        ln_stats(xq, [f"x{cp}_{qt}"], pp, f"ln2_{c}_{qt}")

        for hf in range(2):
            def em_n2(v, xq=xq, hq=hq, pp=pp, hf=hf):
                return v.tensor_scalar(hq[:, 512 * hf:512 * (hf + 1)],
                                       xq[:, 512 * hf:512 * (hf + 1)],
                                       mv_p[pp][:, 0:1],
                                       rcol_p[pp][:, 0:1],
                                       op0=ALU.subtract, op1=ALU.mult)
            ST("ve", em_n2,
               reads=[f"x{cp}_{qt}", f"mv{pp}", f"rcol{pp}"],
               writes=[f"h{qt}{'ab'[hf]}"], tag=f"n2_{c}_{qt}{hf}")
        transpose_htm(qt, f"h2_{c}")

    def stage_f(c, qt):
        pp = qt % 2
        cp = c % 2
        of = ATOF[pp]
        bks = WOBANKS[pp]
        xq = xv_p[cp][:, qt, :]
        mcol, rcol, ncol = mcol_p[pp], rcol_p[pp], ncol_p[pp]
        scol, s2col = scol_p[pp], s2col_p[pp]
        xm_sb = xm_p[pp]

        for ih in range(2):
            def em_ffn(eng, qt=qt, of=of, ih=ih):
                for jc in range(8):
                    eng.matmul(
                        ps_f[:, of + 512 * ih:of + 512 * ih + 512],
                        hfmv[:, jc, 128 * qt:128 * qt + 128],
                        chunks(Msb)[:, jc, 512 * ih:512 * (ih + 1)],
                        start=(jc == 0), stop=False)
                return eng.matmul(
                    ps_f[:, of + 512 * ih:of + 512 * ih + 512], ones_r,
                    bias_b[:, 512 * ih:512 * (ih + 1)],
                    start=False, stop=True)
            ST("te", em_ffn,
               reads=[f"hfm{j}_{qt}" for j in range(8)]
               + [f"M{j}" for j in range(8)] + ["ones_r", "bias_b"],
               writes=[bks[ih]], tag=f"ffn_{c}_{qt}{ih}")

        z_sb = z_p[pp]

        bnb, mv = bnb_p[pp], mv_p[pp]
        for hf in range(2):
            def em_yev(se, of=of, z_sb=z_sb, hf=hf):
                return se.activation(
                    z_sb[:, 512 * hf:512 * (hf + 1)],
                    ps_f[:, of + 512 * hf:of + 512 * (hf + 1)], AF.Copy)
            ST("se", em_yev, reads=[bks[hf]], writes=[f"z{pp}{'ab'[hf]}"],
               tag=f"lny_ev_{c}_{qt}{hf}")

            def em_ybs(v, z_sb=z_sb, hf=hf, bnb=bnb):
                return v.bn_stats(bnb[:, 6 * hf:6 * hf + 6],
                                  z_sb[:, 512 * hf:512 * (hf + 1)])
            ST("ve", em_ybs, reads=[f"z{pp}{'ab'[hf]}"], writes=[f"bnb{pp}"],
               tag=f"lny_bs_{c}_{qt}{hf}")

        def em_yba(v, bnb=bnb, mv=mv):
            return v.bn_aggr(mv, bnb)
        ST("ve", em_yba, reads=[f"bnb{pp}"], writes=[f"mv{pp}"],
           tag=f"lny_ba_{c}_{qt}")

        def em_rr(se, mv=mv, rcol=rcol, ncol=ncol):
            se.activation(ncol, mv[:, 1:2], AF.Ln, bias=epsc[:, 0:1])
            return se.activation(rcol, ncol, AF.Exp, scale=-0.5)
        ST("se", em_rr, reads=[f"mv{pp}", "epsc"],
           writes=[f"rcol{pp}", f"ncol{pp}"], tag=f"lny_rr_{c}_{qt}")

        def em_nmr(v, rcol=rcol, mv=mv, ncol=ncol):
            return v.tensor_scalar(ncol, rcol, mv[:, 0:1], -1.0,
                                   op0=ALU.mult, op1=ALU.mult)
        ST("ve", em_nmr, reads=[f"rcol{pp}", f"mv{pp}"],
           writes=[f"ncol{pp}"], tag=f"lny_nmr_{c}_{qt}")

        def em_xm(v, xq=xq, ncol=ncol, xm_sb=xm_sb):
            return v.tensor_scalar(xm_sb, xq, ncol[:, 0:1], None, op0=ALU.add)
        ST("ve", em_xm, reads=[f"x{cp}_{qt}", f"ncol{pp}"],
           writes=[f"xm{pp}"], tag=f"lny_xm_{c}_{qt}")

        def em_out(v, xq=xq, rcol=rcol, xm_sb=xm_sb, z_sb=z_sb):
            return v.scalar_tensor_tensor(xq, z_sb, rcol[:, 0:1], xm_sb,
                                          op0=ALU.mult, op1=ALU.add)
        ST("ve", em_out, reads=[f"z{pp}a", f"z{pp}b", f"rcol{pp}",
                                f"xm{pp}"],
           writes=[f"x{cp}_{qt}"], tag=f"out_{c}_{qt}")

        def em_st(eng, c=c, qt=qt, xq=xq):
            return eng.dma_start(out=out_dst[c][:, qt, :], in_=xq).then_inc(
                sems["d_seq"], 16)
        ST("se", em_st, reads=[f"x{cp}_{qt}"], writes=[], dma_queue="seq",
           tag=f"st_{c}_{qt}")
        if c + 1 < nchunks:
            ncp = (c + 1) % 2
            ld(xv_p[ncp][:, qt, :], x_src[c + 1][:, qt, :], "syq",
               f"x_in{c + 1}_{qt}", [f"x{ncp}_{qt}"])

    for qt in range(4):
        ld(xv_p[0][:, qt, :], x_src[0][:, qt, :],
           "seq" if qt < 2 else "syq", f"x_in0_{qt}",
           [f"x0_{qt}"])
    for c in range(nchunks):
        if c == 0:
            for qt in range(4):
                stage_ln1(c, qt)
        stage_qkv(c)
        if c == 0:
            stage_kgvg()
        stage_p2(c, 0)
        stage_w(c, 0)
        stage_p2(c, 1)
        stage_h2(c, 0)
        stage_w(c, 1)
        stage_f(c, 0)
        stage_p2(c, 2)
        stage_h2(c, 1)
        stage_w(c, 2)
        stage_f(c, 1)
        stage_p2(c, 3)
        stage_h2(c, 2)
        stage_w(c, 3)
        stage_f(c, 2)
        if c + 1 < nchunks:
            stage_ln1(c + 1, 0)
            stage_ln1(c + 1, 1)

        stage_h2(c, 3)
        stage_f(c, 3)
        if c + 1 < nchunks:
            stage_ln1(c + 1, 2)
            stage_ln1(c + 1, 3)

    with (
        nc.semaphore("e_te") as s0, nc.semaphore("e_ve") as s1,
        nc.semaphore("e_se") as s2, nc.semaphore("e_po") as s3,
        nc.semaphore("e_sy") as s4, nc.semaphore("d_syq") as s5,
        nc.semaphore("d_poq") as s6, nc.semaphore("d_seq") as s7,
        nc.semaphore("d_ccq") as s8, nc.semaphore("d_veq") as s9,
        nc.Block() as block,
    ):
        sems.update({"e_te": s0, "e_ve": s1, "e_se": s2, "e_po": s3,
                     "e_sy": s4, "d_syq": s5, "d_poq": s6, "d_seq": s7,
                     "d_ccq": s8, "d_veq": s9})

        qcounts = [(q, n) for q, n in P.dma_count.items() if n]

        def em_fin(eng):
            for q, n in qcounts:
                eng.wait_ge(sems["d_" + q], 16 * n)
            return eng.nop()
        P.step("sy", em_fin, reads=[], writes=[], tag="final")
        P.finalize(block, sems)
    return nc


_NC_CACHE = {}


def kernel(L0_attn_hidden_states, L0_attn_Wq, L0_attn_Wk, L0_attn_Wv,
           L0_attn_Wo, L0_ffn_w_freq_real, L0_ffn_w_freq_imag, L0_ffn_bias):
    x = np.ascontiguousarray(np.asarray(L0_attn_hidden_states, np.float32))
    B, T, _ = x.shape
    if T not in _NC_CACHE:
        _NC_CACHE[T] = build_nc(T, B)
    nc = _NC_CACHE[T]
    cns = host_constants()
    base = {
        "wq": np.ascontiguousarray(np.asarray(L0_attn_Wq, np.float32)),
        "wk": np.ascontiguousarray(np.asarray(L0_attn_Wk, np.float32)),
        "wv": np.ascontiguousarray(np.asarray(L0_attn_Wv, np.float32)),
        "wo": np.ascontiguousarray(np.asarray(L0_attn_Wo, np.float32)),

        "fbias": np.ascontiguousarray(
            np.asarray(L0_ffn_bias, np.float32).reshape(1, D)),
    }
    base.update(cns)
    wr = np.asarray(L0_ffn_w_freq_real, np.float32)
    wi = np.asarray(L0_ffn_w_freq_imag, np.float32)
    Cb = np.asarray(cns["cst_C"])
    Sb = np.asarray(cns["cst_S"])
    in_maps = [dict(base, x=np.ascontiguousarray(x[b]),
                    wr_sl=np.ascontiguousarray(wr[:, 128 * b:128 * (b + 1)]),
                    wi_sl=np.ascontiguousarray(wi[:, 128 * b:128 * (b + 1)]),
                    cst_c_sl=np.ascontiguousarray(Cb[128 * b:128 * (b + 1), :]),
                    cst_s_sl=np.ascontiguousarray(Sb[128 * b:128 * (b + 1), :]))
               for b in range(B)]
    res = run_bass_kernel_spmd(nc, in_maps, core_ids=list(range(B)))
    out = np.stack([np.asarray(r["out"], np.float32) for r in res.results])
    return out



# version 48
# speedup vs baseline: 1.0563x; 1.0205x over previous
import numpy as np
import ml_dtypes
import concourse.bass as bass
import concourse.mybir as mybir
from concourse.bass_utils import run_bass_kernel_spmd

F32 = mybir.dt.float32
BF16 = mybir.dt.bfloat16
U8 = mybir.dt.uint8
AF = mybir.ActivationFunctionType
ALU = mybir.AluOpType
AX = mybir.AxisListType

D = 1024
H = 16
DH = 64
NG = 16
CHUNK = 512
QTS = 128
NEG = -1.0e30

ENGINES = ("te", "ve", "se", "po", "sy")
QDEPTH = 2


class Step:
    __slots__ = ("eng", "emit", "dma_queue", "done", "waits", "tag", "rec")

    def __init__(self, eng, emit, dma_queue, tag):
        self.eng = eng
        self.emit = emit
        self.dma_queue = dma_queue
        self.done = None
        self.waits = []
        self.tag = tag


class Prog:

    def __init__(self, nc):
        self.nc = nc
        self.steps = []
        self.eng_count = {e: 0 for e in ENGINES}
        self.dma_count = {}
        self.writer = {}
        self.readers = {}

    def step(self, eng, emit, reads=(), writes=(), dma_queue=None, tag=""):
        st = Step(eng, emit, dma_queue, tag)
        reads = list(reads)
        writes = list(writes)
        for k in list(reads):
            if k.startswith("ps"):
                reads.remove(k)
                writes.append(k)
        deps = []
        for k in reads:
            w = self.writer.get(k)
            if w is not None:
                deps.append(w)
        for k in writes:
            w = self.writer.get(k)
            if w is not None:
                deps.append(w)
            deps.extend(self.readers.get(k, ()))
        seen = set()
        for d in deps:
            if id(d) not in seen and d is not st:
                seen.add(id(d))
                st.waits.append(d)
        if dma_queue is None:
            self.eng_count[eng] += 1
            st.done = ("e_" + eng, self.eng_count[eng])
        else:
            n = self.dma_count.get(dma_queue, 0) + 1
            self.dma_count[dma_queue] = n
            st.done = ("d_" + dma_queue, 16 * n)
        for k in writes:
            self.writer[k] = st
            self.readers[k] = []
        for k in reads:
            self.readers.setdefault(k, []).append(st)
        self.steps.append(st)
        return st

    def barrier_deps(self):
        last = {}
        for st in self.steps:
            last[st.done[0]] = st
        return list(last.values())

    def seed(self, key, dep_steps):
        self.readers.setdefault(key, []).extend(dep_steps)

    def _cost(self, st):
        t = st.tag
        if st.dma_queue is not None:
            for pre, c in (("ldw_", 18.0), ("ld_C", 9.0), ("ld_S", 9.0),
                           ("x_in", 4.0), ("st_", 4.0), ("mp_out", 5.0),
                           ("ld_M", 9.0)):
                if t.startswith(pre):
                    return c
            return 1.5
        for pre, c in (("s2_", 0.9), ("s2g_", 1.8), ("sum_", 1.0),
                       ("mn_", 0.9), ("expmg", 0.6), ("expm", 0.6),
                       ("rcp_", 1.0), ("n1_", 0.5), ("n2_", 0.5),
                       ("av_", 1.1), ("ave_", 0.6), ("wo_", 2.2),
                       ("ffn_", 2.6), ("q_fm_mm", 3.6), ("k_fm_mm", 3.6),
                       ("v_mm", 1.9), ("q_fm_ev", 0.6), ("k_fm_ev", 0.6),
                       ("v_ev", 0.6), ("PTr_mm", 7.0), ("QTr_mm", 7.0), ("Mp_mm", 1.8),
                       ("PTr_ev", 0.6), ("QTr_ev", 0.6), ("Mp_ev", 0.6),
                       ("mtT", 0.3), ("lnz_ev", 0.55), ("lny_ev", 0.55),
                       ("lnz_bs", 0.5), ("lny_bs", 0.5),
                       ("wT_", 0.25), ("wTe_", 0.3), ("warm_", 3.5)):
            if t.startswith(pre):
                return c
        if "T_" in t and (t.startswith("h1_") or t.startswith("h2_")):
            return 0.25 if "Te_" not in t else 0.3
        if t.endswith("_bs"):
            return 0.9
        if "_sum" in t or "_sq" in t:
            return 1.3
        if "xm_" in t or "x2_" in t or "out_" in t or t.startswith("n1")  \
           or t.startswith("n2"):
            return 0.9
        return 0.2

    def schedule(self):
        import heapq
        nsteps = len(self.steps)
        for i, st in enumerate(self.steps):
            st.rec = i
        children = [[] for _ in range(nsteps)]
        nleft = [0] * nsteps
        for st in self.steps:
            deps = {id(d): d for d in st.waits}
            nleft[st.rec] = len(deps)
            for d in deps.values():
                children[d.rec].append(st)
        cp = [0.0] * nsteps
        for st in reversed(self.steps):
            m = 0.0
            for ch in children[st.rec]:
                m = max(m, 0.5 + cp[ch.rec])
            cp[st.rec] = self._cost(st) + m
        for st in self.steps:
            if st.tag.startswith("x_in0"):
                cp[st.rec] += 100.0
        lanes = list(ENGINES) + ["q_" + q for q in self.dma_count]
        eng_time = {e: 0.0 for e in lanes}
        pend = {e: [] for e in lanes}
        run = {e: [] for e in lanes}
        finish = [0.0] * nsteps
        ready_at = [0.0] * nsteps

        def lane(st):
            return st.eng if st.dma_queue is None else "q_" + st.dma_queue

        for st in self.steps:
            if nleft[st.rec] == 0:
                heapq.heappush(pend[lane(st)], (0.0, st.rec, st))
        order = {e: [] for e in lanes}
        n_done = 0
        while n_done < nsteps:
            best = None
            for e in lanes:
                while pend[e] and pend[e][0][0] <= eng_time[e]:
                    rt, rec, st = heapq.heappop(pend[e])
                    heapq.heappush(run[e], (-cp[rec], rec, st))
                cand = None
                if run[e]:
                    _, rec, st = run[e][0]
                    cand = (eng_time[e], rec, st, e, False)
                elif pend[e]:
                    rt, rec, st = pend[e][0]
                    cand = (rt, rec, st, e, True)
                if cand and (best is None or (cand[0], cand[1]) < (best[0], best[1])):
                    best = cand
            start, rec, st, e, from_pend = best
            if from_pend:
                heapq.heappop(pend[e])
            else:
                heapq.heappop(run[e])
            f = start + self._cost(st)
            eng_time[e] = f
            finish[rec] = f
            ready_at[rec] = start
            order[e].append(st)
            n_done += 1
            for ch in children[rec]:
                nleft[ch.rec] -= 1
                if nleft[ch.rec] == 0:
                    rt = 0.0
                    for d in ch.waits:
                        lat = 0.5 if (d.eng != ch.eng or d.dma_queue
                                      or ch.dma_queue) else 0.05
                        rt = max(rt, finish[d.rec] + lat)
                    heapq.heappush(pend[lane(ch)], (rt, ch.rec, ch))
        starts = {}
        for e in lanes:
            for pos, st in enumerate(order[e]):
                starts[st.rec] = ready_at[st.rec]
        per_eng = {e: [] for e in ENGINES}
        for st in self.steps:
            per_eng[st.eng].append(st)
        for e in ENGINES:
            per_eng[e].sort(key=lambda st: (st.tag == "final",
                                            starts[st.rec], st.rec))
        eng_n = {e: 0 for e in ENGINES}
        dma_n = {q: 0 for q in self.dma_count}
        for e in ENGINES:
            for st in per_eng[e]:
                if st.dma_queue is None:
                    eng_n[e] += 1
                    st.done = ("e_" + e, eng_n[e])
                else:
                    dma_n[st.dma_queue] += 1
                    st.done = ("d_" + st.dma_queue, 16 * dma_n[st.dma_queue])
        return per_eng

    def finalize(self, block, sems):
        per_eng = self.schedule()
        hw_mark = {}
        for e in ENGINES:
            for st in per_eng[e]:
                resolved = []
                for dep in st.waits:
                    sem_name, val = dep.done
                    k = (e, sem_name)
                    if hw_mark.get(k, -1) < val:
                        hw_mark[k] = val
                        resolved.append((sem_name, val))
                if st.dma_queue is not None:
                    sem_name = "d_" + st.dma_queue
                    prev = st.done[1] - 16 * QDEPTH
                    if prev > 0 and hw_mark.get((e, sem_name), -1) < prev:
                        hw_mark[(e, sem_name)] = prev
                        resolved.append((sem_name, prev))
                st.waits = resolved

        def make_body(e):
            def body(eng):
                for st in per_eng[e]:
                    for sem_name, val in st.waits:
                        eng.wait_ge(sems[sem_name], val)
                    last = st.emit(eng)
                    if st.dma_queue is None:
                        assert last is not None, f"step {st.tag} returned None"
                        last.then_inc(sems["e_" + e], 1)
            return body

        block.tensor(make_body("te"))
        block.vector(make_body("ve"))
        block.scalar(make_body("se"))
        block.gpsimd(make_body("po"))
        block.sync(make_body("sy"))


def host_constants():
    k = np.arange(D)
    ang = 2.0 * np.pi * np.outer(k, k) / D
    cns = {}
    cns["cst_C"] = np.cos(ang).astype(ml_dtypes.bfloat16)
    cns["cst_S"] = np.sin(ang).astype(ml_dtypes.bfloat16)
    i = np.arange(QTS)
    same = (i[:, None] // DH) == (i[None, :] // DH)
    causal = i[None, :] <= i[:, None]
    m2 = (same & causal)
    cns["cst_maskneg2T"] = np.ascontiguousarray(
        np.where(m2.T, 0.0, NEG)).astype(np.float32)
    cns["cst_mask4"] = np.ascontiguousarray(
        np.tile(np.where(m2.T, 0.0, NEG), (1, 4))).astype(
        ml_dtypes.bfloat16)
    mg0 = np.zeros((NG, QTS), np.float32)
    mg0[:, 0:DH] = NEG
    cns["cst_maskg0T"] = mg0
    cns["cst_ident"] = np.eye(128, dtype=ml_dtypes.bfloat16)
    cns["cst_ones128"] = np.ones((128, 128), ml_dtypes.bfloat16)
    sel = np.zeros((128, H * QTS), np.float32)
    for h in range(H):
        sel[h, h * QTS:(h + 1) * QTS] = -1.0
        sel[64 + h, h * QTS:(h + 1) * QTS] = -1.0
    sel[32, 0:QTS] = -1.0
    cns["cst_sel"] = sel.astype(ml_dtypes.bfloat16)
    cns["cst_ones"] = np.ones((1, 128), ml_dtypes.bfloat16)
    return cns


def build_nc(T, ncores=1):
    nchunks = T // CHUNK
    nc = bass.Bass("TRN2", target_bir_lowering=False)

    x_h = nc.dram_tensor("x", [T, D], F32, kind="ExternalInput").ap()
    out_h = nc.dram_tensor("out", [T, D], F32, kind="ExternalOutput").ap()
    wq_h = nc.dram_tensor("wq", [D, D], F32, kind="ExternalInput").ap()
    wk_h = nc.dram_tensor("wk", [D, D], F32, kind="ExternalInput").ap()
    wv_h = nc.dram_tensor("wv", [D, D], F32, kind="ExternalInput").ap()
    wo_h = nc.dram_tensor("wo", [D, D], F32, kind="ExternalInput").ap()
    wr_h = nc.dram_tensor("wr_sl", [D, 128], F32, kind="ExternalInput").ap()
    wi_h = nc.dram_tensor("wi_sl", [D, 128], F32, kind="ExternalInput").ap()
    csl_h = nc.dram_tensor("cst_c_sl", [128, D], BF16, kind="ExternalInput").ap()
    ssl_h = nc.dram_tensor("cst_s_sl", [128, D], BF16, kind="ExternalInput").ap()
    mpart_h = nc.dram_tensor("mpart", [D, D], BF16, kind="Internal").ap()
    mfull_h = nc.dram_tensor("mfull", [D, D], BF16, kind="Internal").ap()
    bias_h = nc.dram_tensor("fbias", [1, D], F32, kind="ExternalInput").ap()
    C_h = nc.dram_tensor("cst_C", [D, D], BF16, kind="ExternalInput").ap()
    S_h = nc.dram_tensor("cst_S", [D, D], BF16, kind="ExternalInput").ap()
    mn2_h = nc.dram_tensor("cst_maskneg2T", [QTS, QTS], F32, kind="ExternalInput").ap()
    m4_h = nc.dram_tensor("cst_mask4", [QTS, 4 * QTS], BF16, kind="ExternalInput").ap()
    mg0_h = nc.dram_tensor("cst_maskg0T", [NG, QTS], F32, kind="ExternalInput").ap()
    o128_h = nc.dram_tensor("cst_ones128", [128, 128], BF16, kind="ExternalInput").ap()
    id_h = nc.dram_tensor("cst_ident", [128, 128], BF16, kind="ExternalInput").ap()
    sel_h = nc.dram_tensor("cst_sel", [128, H * QTS], BF16, kind="ExternalInput").ap()
    ones_h = nc.dram_tensor("cst_ones", [1, 128], BF16, kind="ExternalInput").ap()

    def sbuf(name, shape, dt):
        return nc.alloc_sbuf_tensor(name, shape, dt).ap()

    wqt = sbuf("wqt", [128, 8 * D], BF16)
    wkt = sbuf("wkt", [128, 8 * D], BF16)
    wvt = sbuf("wvt", [128, 8 * D], BF16)
    wot = sbuf("wot", [128, 8 * D], BF16)
    Msb = sbuf("Msb", [128, 8 * D], BF16)
    maskn2T = sbuf("maskn2T", [QTS, QTS], F32)
    maskg0T = sbuf("maskg0T", [NG, QTS], F32)
    allones = sbuf("allones", [128, 128], BF16)
    ident = sbuf("ident", [128, 128], BF16)
    ones_r = sbuf("ones_r", [1, 128], BF16)
    bias_b = sbuf("bias_b", [1, D], BF16)
    kg = sbuf("kg", [128, 8 * NG], BF16)
    vg = sbuf("vg", [48, D], BF16)

    def pair(name, shape, dt):
        return [sbuf(name + "0", shape, dt), sbuf(name + "1", shape, dt)]

    mcol_p = pair("mcol", [128, 1], F32)
    rcol_p = pair("rcol", [128, 1], F32)
    ncol_p = pair("ncol", [128, 1], F32)
    scol_p = pair("scol", [128, 1], F32)
    s2col_p = pair("s2col", [128, 1], F32)
    rcol1_p = pair("rcol1", [128, 1], F32)
    ncol1_p = pair("ncol1", [128, 1], F32)
    bnb_p = pair("bnb", [128, 12], F32)
    mv_p = pair("mv", [128, 2], F32)
    bnb1_p = pair("bnb1", [128, 12], F32)
    mv1_p = pair("mv1", [128, 2], F32)
    epsc = sbuf("epsc", [128, 1], F32)

    AREN = 115 * 1024
    arena = nc.alloc_sbuf_tensor("arena", [128, AREN], U8)
    ar_f = arena.bitcast(F32).ap()
    ar_b = arena.bitcast(BF16).ap()

    off = [0]
    extents = []

    def carve(nbytes, align=64):
        off[0] = (off[0] + align - 1) // align * align
        o = off[0]
        off[0] += nbytes
        assert off[0] <= AREN, f"arena overflow {off[0]}"
        extents.append((o, off[0]))
        return o

    def cf(cols):
        o = carve(cols * 4)
        return ar_f[:, o // 4: o // 4 + cols]

    def cb(cols):
        o = carve(cols * 2)
        return ar_b[:, o // 2: o // 2 + cols]

    extA = {}

    def mark(name):
        extA.setdefault(name, extents[-1])
    offA = off[0]
    C_sb = cb(8 * D)
    mark("C")
    S_sb = cb(8 * D)
    mark("S")
    Sn_sb = S_sb
    wrsl_sb = cb(8 * 128)
    mark("wrsl")
    wisl_sb = cb(8 * 128)
    mark("wisl")
    csl_sb = cb(D)
    mark("csl")
    ssl_sb = cb(D)
    mark("ssl")
    PTr_sb = cb(D)
    mark("PTr")
    QTr_sb = cb(D)
    mark("QTr")
    Mp_sb = cb(8 * D)
    mark("Mp")
    Wst_sb = cb(8 * D)
    mark("WstA")
    Wst2_sb = cb(8 * D)
    mark("WstB")
    end_A = off[0]

    off[0] = offA
    extM = {}

    def markM(keys):
        extM[extents[-1]] = keys
    x_p = [cf(4 * D), cf(4 * D)]
    markM([f"x1_{q}" for q in range(4)])
    extM[extents[-2]] = [f"x0_{q}" for q in range(4)]
    xm_p = [cf(D), cf(D)]
    markM(["xm1"])
    extM[extents[-2]] = ["xm0"]
    h_tm = cb(4 * D)
    markM([f"h{q}{s}" for q in range(4) for s in "ab"])
    h_fm = cb(4 * D)
    markM([f"hfm{j}_{q}" for j in range(8) for q in range(4)])
    q_fm = cb(4 * D)
    markM([f"q_fm{i}" for i in range(8)])
    k_fm = cb(4 * D)
    markM([f"k_fm{i}" for i in range(8)])
    v_tm = cb(4 * D)
    markM([f"v{q}" for q in range(4)])
    ao_fm = cb(4 * D)
    markM([f"ao{g}_{q}" for g in range(2) for q in range(4)])
    p2_p = [cb(8 * QTS), cb(8 * QTS)]
    markM(["p2a0_1", "p2b0_1"])
    extM[extents[-2]] = ["p2a0_0", "p2b0_0"]
    p2B_p = [cb(8 * QTS), cb(8 * QTS)]
    markM(["p2a1_1", "p2b1_1"])
    extM[extents[-2]] = ["p2a1_0", "p2b1_0"]
    p2g_p = [cb(512), cb(512)]
    markM(["p2g_1"])
    extM[extents[-2]] = ["p2g_0"]
    z_p = [cb(D), cb(D)]
    markM(["z1a", "z1b"])
    extM[extents[-2]] = ["z0a", "z0b"]
    rec_p = [cf(512), cf(512)]
    markM(["rec1"])
    extM[extents[-2]] = ["rec0"]
    junk = cb(D)
    markM(["junk"])
    end_main = off[0]
    assert max(end_A, end_main) <= AREN, (end_A, end_main)

    psA = nc.alloc_psum_tensor("psA", [128, 4096], F32)
    ps_f = psA.ap()
    ps_b = psA.bitcast(BF16).ap()

    ATOF = [0, 2048]
    WOBANKS = [["ps0", "ps1"], ["ps4", "ps5"]]
    PARKA_BF = [2688, 5760]
    def parkA(pb):
        return ps_b[:, PARKA_BF[pb]: PARKA_BF[pb] + 128]

    P = Prog(nc)
    ST = P.step
    sems = {}

    def ld(dst, src, queue, tag, writes, reads=()):
        def em(eng):
            return eng.dma_start(out=dst, in_=src).then_inc(
                sems["d_" + queue], 16)
        return ST({"syq": "sy", "poq": "po", "seq": "se",
                   "veq": "ve"}[queue], em,
                  reads=reads, writes=writes, dma_queue=queue, tag=tag)

    ld(maskn2T, mn2_h, "syq", "ld_mn2t", ["maskn2T"])
    ld(maskg0T, mg0_h, "syq", "ld_mg0t", ["maskg0T"])

    def em_eps(v):
        return v.memset(epsc, 1e-5)
    ST("ve", em_eps, reads=[], writes=["epsc"], tag="epsc")
    ld(allones, o128_h, "syq", "ld_o128", ["allones"])
    ld(ident, id_h, "syq", "ld_id", ["ident"])
    ld(ones_r, ones_h, "syq", "ld_ones", ["ones_r"])
    ld(bias_b, bias_h, "poq", "ld_bias", ["bias_b"])

    def chunks(t):
        return t.rearrange("p (c d) -> p c d", d=D)

    ld(chunks(C_sb), C_h.rearrange("(j p) m -> p j m", p=128),
       "syq", "ld_C", [f"C{jc}" for jc in range(8)])
    ld(chunks(S_sb), S_h.rearrange("(j p) m -> p j m", p=128),
       "syq", "ld_S", [f"S{jc}" for jc in range(8)])
    ld(wrsl_sb.rearrange("p (b m) -> p b m", m=128),
       wr_h.rearrange("(b p) m -> p b m", p=128), "poq", "ld_wrsl", ["wrsl"])
    ld(wisl_sb.rearrange("p (b m) -> p b m", m=128),
       wi_h.rearrange("(b p) m -> p b m", p=128), "poq", "ld_wisl", ["wisl"])
    ld(csl_sb, csl_h, "syq", "ld_csl", ["csl"])
    ld(ssl_sb, ssl_h, "syq", "ld_ssl", ["ssl"])

    def sneg_steps():
        for jc in range(8):
            def em(eng, jc=jc):
                return eng.activation(chunks(Sn_sb)[:, jc, :],
                                      chunks(S_sb)[:, jc, :], AF.Copy,
                                      scale=-1.0)
            ST("se", em, reads=[f"S{jc}"], writes=[f"S{jc}", f"Sn{jc}"],
               tag=f"sneg{jc}")

    ck = [f"C{j}" for j in range(8)]
    sk = [f"S{j}" for j in range(8)]
    snk = [f"Sn{j}" for j in range(8)]

    def shard_chain(dst_sb, dkey, l_a, ka, l_b, kb, r_a, r_b, rka, rkb,
                    scale, tag):
        def em_mm(eng):
            last = None
            for ah in range(2):
                for b in range(8):
                    last = eng.matmul(
                        ps_f[:, 512 * ah:512 * ah + 512],
                        l_a.rearrange("p (b m) -> p b m", m=128)[:, b, :],
                        chunks(r_a)[:, b, 512 * ah:512 * (ah + 1)],
                        start=(b == 0), stop=False)
                for b in range(8):
                    last = eng.matmul(
                        ps_f[:, 512 * ah:512 * ah + 512],
                        l_b.rearrange("p (b m) -> p b m", m=128)[:, b, :],
                        chunks(r_b)[:, b, 512 * ah:512 * (ah + 1)],
                        start=False, stop=(b == 7))
            return last
        ST("te", em_mm, reads=[ka, kb] + rka + rkb,
           writes=["ps0", "ps1"], tag=f"{tag}_mm")

        def em_ev(eng):
            return eng.activation(dst_sb, ps_f[:, 0:1024], AF.Copy,
                                  scale=scale)
        ST("se", em_ev, reads=["ps0", "ps1"], writes=[dkey],
           tag=f"{tag}_ev")

    shard_chain(PTr_sb, "PTr", wrsl_sb, "wrsl", wisl_sb, "wisl",
                C_sb, S_sb, ck, sk, 1.0, "PTr")
    sneg_steps()

    def em_snl(eng):
        return eng.activation(ssl_sb, ssl_sb, AF.Copy, scale=-1.0)
    ST("se", em_snl, reads=["ssl"], writes=["ssl", "snl"], tag="snl")

    shard_chain(QTr_sb, "QTr", wisl_sb, "wisl", wrsl_sb, "wrsl",
                C_sb, Sn_sb, ck, snk, 1.0, "QTr")

    for tc in range(0, 8, 2):
        def em_mm(eng, tc=tc):
            last = None
            for k in range(4):
                t2 = tc + k // 2
                ah = k % 2
                eng.matmul(ps_f[:, 512 * k:512 * k + 512],
                           PTr_sb[:, 128 * t2:128 * (t2 + 1)],
                           csl_sb[:, 512 * ah:512 * (ah + 1)],
                           start=True, stop=False)
                last = eng.matmul(ps_f[:, 512 * k:512 * k + 512],
                                  QTr_sb[:, 128 * t2:128 * (t2 + 1)],
                                  ssl_sb[:, 512 * ah:512 * (ah + 1)],
                                  start=False, stop=True)
            return last
        ST("te", em_mm, reads=["PTr", "QTr", "csl", "snl"],
           writes=["ps0", "ps1", "ps2", "ps3"], tag=f"Mp_mm{tc}")

        def em_ev(eng, tc=tc):
            return eng.activation(
                chunks(Mp_sb).transpose([0, 1, 2])[:, tc:tc + 2, :]
                .rearrange("p a b -> p (a b)"),
                ps_f[:, 0:2048], AF.Copy, scale=1.0 / D)
        ST("se", em_ev, reads=["ps0", "ps1", "ps2", "ps3"],
           writes=[f"Mp{tc}", f"Mp{tc + 1}"], tag=f"Mp_ev{tc}")

    ld(mpart_h.rearrange("(tc p) u -> p tc u", p=128),
       Mp_sb.rearrange("p (tc u) -> p tc u", u=D), "syq", "mp_out",
       ["mpartd"], reads=[f"Mp{j}" for j in range(8)])
    if ncores > 1:
        def em_cc(eng):
            return eng.collective_compute(
                "AllReduce", ALU.add,
                replica_groups=[list(range(ncores))],
                ins=[mpart_h.opt()], outs=[mfull_h.opt()])
        ST("po", em_cc, reads=["mpartd"], writes=["mfulld"],
           tag="allreduce")
    else:
        ld(mfull_h, mpart_h, "poq", "mp_copy", ["mfulld"], reads=["mpartd"])
    ld(chunks(Msb), mfull_h.rearrange("(j p) m -> p j m", p=128),
       "syq", "ld_M", [f"M{jc}" for jc in range(8)], reads=["mfulld"])

    for (w_hbm, wt_sb, nm, st_sb, sk, q) in (
            (wq_h, wqt, "q", Wst_sb, "WstA", "poq"),
            (wk_h, wkt, "k", Wst2_sb, "WstB", "poq"),
            (wv_h, wvt, "v", Wst_sb, "WstA", "poq"),
            (wo_h, wot, "o", Wst2_sb, "WstB", "poq")):
        ld(chunks(st_sb), w_hbm.rearrange("(i p) m -> p i m", p=128),
           q, f"ldw_{nm}", [f"{sk}{ic}" for ic in range(8)])
        for ic in range(8):
            for jc in range(8):
                pb = (ic * 8 + jc) % 2

                def em_t(eng, ic=ic, jc=jc, pb=pb, st_sb=st_sb):
                    return eng.transpose(
                        parkA(pb),
                        chunks(st_sb)[:, ic, 128 * jc:128 * (jc + 1)], ident)
                ST("te", em_t, reads=[f"{sk}{ic}", "ident"],
                   writes=["ps2" if pb == 0 else "ps5"], tag=f"wT_{nm}{ic}_{jc}")

                def em_e(eng, wt_sb=wt_sb, ic=ic, jc=jc, pb=pb):
                    return eng.tensor_copy(
                        chunks(wt_sb)[:, jc, 128 * ic:128 * (ic + 1)],
                        parkA(pb))
                ST("ve", em_e, reads=["ps2" if pb == 0 else "ps5"], writes=[f"wt_{nm}{jc}"],
                   tag=f"wTe_{nm}{ic}_{jc}")

    barrier = P.barrier_deps()
    for mkeys in extM.values():
        for k in mkeys:
            P.seed(k, barrier)

    x_src = x_h.rearrange("(c q p) d -> c p q d", p=128, q=4)
    out_dst = out_h.rearrange("(c q p) d -> c p q d", p=128, q=4)
    hfmv = h_fm.rearrange("p (j t) -> p j t", t=CHUNK)
    qfmv = q_fm.rearrange("p (i t) -> p i t", t=CHUNK)
    kfmv = k_fm.rearrange("p (i t) -> p i t", t=CHUNK)
    aov = ao_fm.rearrange("p (r t) -> p r t", t=CHUNK)
    kgv = kg.rearrange("p (j g) -> p j g", g=NG)
    vtv = v_tm.rearrange("p (q i) -> p q i", i=D)
    xv_p = [x_p[0].rearrange("p (q d) -> p q d", d=D),
            x_p[1].rearrange("p (q d) -> p q d", d=D)]
    hv = h_tm.rearrange("p (q d) -> p q d", d=D)

    def qslice(hh, qt, a=0, b=QTS):
        return qfmv[:, hh // 2, 128 * qt + a:128 * qt + b][
            64 * (hh % 2):64 * (hh % 2) + 64, :]

    def kslice(hh, qt, a=0, b=QTS):
        return kfmv[:, hh // 2, 128 * qt + a:128 * qt + b][
            64 * (hh % 2):64 * (hh % 2) + 64, :]

    def kgsl(hh):
        return kgv[:, hh // 2, :][64 * (hh % 2):64 * (hh % 2) + 64, :]

    def ln_stats(src_ap, src_keys, pp, tag, own=False):
        if own:
            bnb, mv = bnb1_p[pp], mv1_p[pp]
            rcol, ncol = rcol1_p[pp], ncol1_p[pp]
            sx = "b"
        else:
            bnb, mv = bnb_p[pp], mv_p[pp]
            rcol, ncol = rcol_p[pp], ncol_p[pp]
            sx = ""
        K = lambda n: n + sx + str(pp)

        def em_bs(v):
            v.bn_stats(bnb[:, 0:6], src_ap[:, 0:512])
            return v.bn_stats(bnb[:, 6:12], src_ap[:, 512:1024])
        ST("ve", em_bs, reads=src_keys, writes=[K("bnb")], tag=f"{tag}_bs")

        def em_ba(v):
            return v.bn_aggr(mv, bnb)
        ST("ve", em_ba, reads=[K("bnb")], writes=[K("mv")],
           tag=f"{tag}_ba")

        def em_rr(se):
            se.activation(ncol, mv[:, 1:2], AF.Ln, bias=epsc[:, 0:1])
            return se.activation(rcol, ncol, AF.Exp, scale=-0.5)
        ST("se", em_rr, reads=[K("mv"), "epsc"],
           writes=[K("rcol"), K("ncol")], tag=f"{tag}_rr")

    def transpose_htm(qt, tag):
        pp = qt % 2
        bk = WOBANKS[pp] + WOBANKS[1 - pp]
        c0 = [2 * ATOF[pp], 2 * ATOF[pp] + 1024,
              2 * ATOF[1 - pp], 2 * ATOF[1 - pp] + 1024]

        def park(i):
            return ps_b[:, c0[i]: c0[i] + 128]
        for jc in range(8):
            pb = jc % 4

            def em_t(eng, qt=qt, jc=jc, pb=pb, park=park):
                return eng.transpose(
                    park(pb),
                    h_tm.rearrange("p (q j d) -> p q j d", q=4, j=8)[:, qt, jc, :],
                    ident)
            ST("te", em_t, reads=[f"h{qt}{'ab'[jc // 4]}", "ident"],
               writes=[bk[pb]], tag=f"{tag}T_{qt}_{jc}")

            if pb % 2 == 0:
                def em_e(eng, qt=qt, jc=jc, pb=pb, park=park):
                    return eng.tensor_copy(
                        hfmv[:, jc, 128 * qt:128 * qt + 128], park(pb))
                ST("ve", em_e, reads=[bk[pb]], writes=[f"hfm{jc}_{qt}"],
                   tag=f"{tag}Te_{qt}_{jc}")
            else:
                def em_e(eng, qt=qt, jc=jc, pb=pb, park=park):
                    return eng.copy(
                        hfmv[:, jc, 128 * qt:128 * qt + 128], park(pb))
                ST("se", em_e, reads=[bk[pb]], writes=[f"hfm{jc}_{qt}"],
                   tag=f"{tag}Te_{qt}_{jc}")

    def stage_ln1(c, qt):
        pp = qt % 2
        cp = c % 2
        xq = xv_p[cp][:, qt, :]
        hq = hv[:, qt, :]
        ln_stats(xq, [f"x{cp}_{qt}"], pp, f"ln1_{c}_{qt}", own=True)

        for hf in range(2):
            def em_n1(v, xq=xq, hq=hq, pp=pp, hf=hf):
                return v.tensor_scalar(hq[:, 512 * hf:512 * (hf + 1)],
                                       xq[:, 512 * hf:512 * (hf + 1)],
                                       mv1_p[pp][:, 0:1],
                                       rcol1_p[pp][:, 0:1],
                                       op0=ALU.subtract, op1=ALU.mult)
            ST("ve", em_n1,
               reads=[f"x{cp}_{qt}", f"mvb{pp}", f"rcolb{pp}"],
               writes=[f"h{qt}{'ab'[hf]}"], tag=f"n1_{c}_{qt}{hf}")
        transpose_htm(qt, f"h1_{c}")

    def stage_qkv(c):
        hfm_keys = [f"hfm{j}_{q}" for j in range(8) for q in range(4)]
        rot = [("ps0", 0), ("ps1", 512), ("ps3", 1536), ("ps4", 2048)]
        rr = [0]
        for (wt_sb, wtk, dst, dk, scale) in (
                (wqt, "q", q_fm, "q_fm", 1.0), (wkt, "k", k_fm, "k_fm", 0.125)):
            for ic0 in range(0, 8, 2):
                pk = [rot[rr[0] % 4], rot[(rr[0] + 1) % 4]]
                rr[0] += 2

                def em_mm(eng, wt_sb=wt_sb, ic0=ic0, pk=pk):
                    last = None
                    for k in range(2):
                        ic = ic0 + k
                        c0 = pk[k][1]
                        for jc in range(8):
                            last = eng.matmul(
                                ps_f[:, c0:c0 + 512],
                                chunks(wt_sb)[:, jc, 128 * ic:128 * (ic + 1)],
                                hfmv[:, jc, :],
                                start=(jc == 0), stop=(jc == 7))
                    return last
                ST("te", em_mm,
                   reads=hfm_keys + [f"wt_{wtk}{j}" for j in range(8)],
                   writes=[pk[0][0], pk[1][0]], tag=f"{dk}_mm{c}_{ic0}")

                for k in range(2):
                    def em_ev(eng, dst=dst, ic=ic0 + k, c0=pk[k][1],
                              scale=scale):
                        return eng.tensor_scalar(
                            dst.rearrange("p (i t) -> p i t", t=CHUNK)[:, ic, :],
                            ps_f[:, c0:c0 + 512], scale, None, op0=ALU.mult)
                    ST("ve", em_ev, reads=[pk[k][0]], writes=[f"{dk}{ic0 + k}"],
                       tag=f"{dk}_ev{c}_{ic0 + k}")
        for qt in range(4):
            for ih in range(2):
                bk, c0 = rot[rr[0] % 4]
                rr[0] += 1

                def em_mm(eng, qt=qt, ih=ih, c0=c0):
                    last = None
                    for jc in range(8):
                        last = eng.matmul(
                            ps_f[:, c0:c0 + 512],
                            hfmv[:, jc, 128 * qt:128 * qt + 128],
                            chunks(wvt)[:, jc, 512 * ih:512 * (ih + 1)],
                            start=(jc == 0), stop=(jc == 7))
                    return last
                ST("te", em_mm,
                   reads=[f"hfm{j}_{qt}" for j in range(8)]
                   + [f"wt_v{j}" for j in range(8)],
                   writes=[bk], tag=f"v_mm{c}_{qt}_{ih}")

                def em_ev(eng, qt=qt, ih=ih, c0=c0):
                    return eng.tensor_copy(
                        vtv[:, qt, 512 * ih:512 * (ih + 1)],
                        ps_f[:, c0:c0 + 512])
                ST("ve", em_ev, reads=[bk], writes=[f"v{qt}"],
                   tag=f"v_ev{c}_{qt}_{ih}")

    def stage_kgvg():
        def em_kg(v):
            return v.tensor_copy(kgv, kfmv[:, :, 0:NG])
        ST("ve", em_kg, reads=[f"k_fm{j}" for j in range(8)],
           writes=["kg"], tag="kg")

        def em_vg(v):
            v.tensor_copy(vg[0:NG, :], vtv[0:NG, 0, :])
            return v.tensor_copy(vg[32:32 + NG, :], vtv[0:NG, 0, :])
        ST("ve", em_vg, reads=["v0"], writes=["vg"], tag="vg")

    def stage_p2(c, qt):
        pp = qt % 2
        OF = ATOF[pp]
        B = [f"ps{4 * pp + i}" for i in range(4)]
        p2g, rec = p2g_p[pp], rec_p[pp]
        o_q = [OF, OF + 512]
        o_g = OF + 1024
        o_s = OF + 1536
        for g in range(2):
            h0 = 8 * g
            p2 = (p2_p if g == 0 else p2B_p)[pp]
            KA, KB = f"p2a{g}_{pp}", f"p2b{g}_{pp}"

            for quarter in range(2):
                def em_raw(eng, qt=qt, h0=h0, quarter=quarter):
                    last = None
                    for i in range(4):
                        hh = h0 + 2 * i + quarter
                        o = o_q[quarter] + i * QTS
                        last = eng.matmul(ps_f[:, o:o + QTS], kslice(hh, qt),
                                          qslice(hh, qt), start=(i == 0),
                                          stop=(i == 3))
                    return last
                ST("te", em_raw,
                   reads=[f"k_fm{j}" for j in range(4 * g, 4 * g + 4)]
                   + [f"q_fm{j}" for j in range(4 * g, 4 * g + 4)],
                   writes=[B[quarter]], tag=f"s2_{c}_{qt}_{g}{quarter}")

            for quarter in range(2):
                def em_mn(v, quarter=quarter):
                    o = o_q[quarter]
                    return v.tensor_tensor(
                        ps_f[:, o:o + 512].rearrange("p (h k) -> p h k", k=QTS),
                        ps_f[:, o:o + 512].rearrange("p (h k) -> p h k", k=QTS),
                        maskn2T.unsqueeze(1).broadcast_to([QTS, 4, QTS]),
                        op=ALU.add)
                ST("ve", em_mn, reads=[B[quarter], "maskn2T"],
                   writes=[B[quarter]], tag=f"mn_{c}_{qt}_{g}{quarter}")

            def em_graw(eng, qt=qt, h0=h0):
                last = None
                for par in range(2):
                    pb = 32 * par
                    for i in range(4):
                        hh = h0 + 2 * i + par
                        o = o_g + i * QTS
                        last = eng.matmul(ps_f[pb:pb + 16, o:o + QTS],
                                          kgsl(hh), qslice(hh, qt),
                                          start=(i == 0), stop=(i == 3))
                return last
            ST("te", em_graw,
               reads=[f"q_fm{j}" for j in range(4 * g, 4 * g + 4)] + ["kg"],
               writes=[B[2]], tag=f"s2g_{c}_{qt}_{g}")

            if c == 0 and qt == 0:
                def em_mng(v):
                    v.tensor_tensor(
                        ps_f[0:16, o_g:o_g + 512].rearrange(
                            "p (h k) -> p h k", k=QTS),
                        ps_f[0:16, o_g:o_g + 512].rearrange(
                            "p (h k) -> p h k", k=QTS),
                        maskg0T.unsqueeze(1).broadcast_to([NG, 4, QTS]),
                        op=ALU.add)
                    return v.tensor_tensor(
                        ps_f[32:48, o_g:o_g + 512].rearrange(
                            "p (h k) -> p h k", k=QTS),
                        ps_f[32:48, o_g:o_g + 512].rearrange(
                            "p (h k) -> p h k", k=QTS),
                        maskg0T.unsqueeze(1).broadcast_to([NG, 4, QTS]),
                        op=ALU.add)
                ST("ve", em_mng, reads=[B[2], "maskg0T"], writes=[B[2]],
                   tag=f"mng_{c}_{qt}_{g}")

            for quarter in range(2):
                def em_expm(se, p2=p2, quarter=quarter):
                    return se.activation(
                        p2[:, 512 * quarter:512 * (quarter + 1)],
                        ps_f[:, o_q[quarter]:o_q[quarter] + 512], AF.Exp)
                ST("se", em_expm, reads=[B[quarter]],
                   writes=[[KA, KB][quarter]],
                   tag=f"expm_{c}_{qt}_{g}{quarter}")

            def em_expmg(se, p2g=p2g):
                se.activation(p2g[0:16, 0:512],
                              ps_f[0:16, o_g:o_g + 512], AF.Exp)
                return se.activation(p2g[32:48, 0:512],
                                     ps_f[32:48, o_g:o_g + 512], AF.Exp)
            ST("se", em_expmg, reads=[B[2]], writes=[f"p2g_{pp}"],
               tag=f"expmg_{c}_{qt}_{g}")

            def em_sum(eng, p2=p2, p2g=p2g):
                eng.matmul(ps_f[0:64, o_g:o_g + 512], allones[:, 0:64],
                           p2[:, 0:512], start=True, stop=False)
                eng.matmul(ps_f[0:64, o_g:o_g + 512], allones[0:16, 0:64],
                           p2g[0:16, 0:512], start=False, stop=False)
                eng.matmul(ps_f[64:128, o_g:o_g + 512], allones[:, 64:128],
                           p2[:, 512:1024], start=True, stop=False)
                return eng.matmul(ps_f[64:128, o_g:o_g + 512],
                                  allones[32:48, 64:128],
                                  p2g[32:48, 0:512], start=False, stop=True)
            ST("te", em_sum, reads=[KA, KB, f"p2g_{pp}",
                                    "allones", B[2]],
               writes=[B[2]], tag=f"sum_{c}_{qt}_{g}")

            def em_rec(se, rec=rec):
                se.activation(rec, ps_f[:, o_g:o_g + 512], AF.Ln)
                return se.activation(rec, rec, AF.Exp, scale=-1.0)
            ST("se", em_rec, reads=[B[2]], writes=[f"rec{pp}"],
               tag=f"rcp_{c}_{qt}_{g}")

            for par in range(2):
                def em_av(eng, qt=qt, h0=h0, p2=p2, p2g=p2g, par=par):
                    last = None
                    for i in range(par, 8, 2):
                        hh = h0 + i
                        pos = (i % 2) * 4 + i // 2
                        ob = o_s + (i // 2) * QTS
                        dst = ps_f[:, ob:ob + QTS][
                            64 * (hh % 2):64 * (hh % 2) + 64, :]
                        eng.matmul(
                            dst,
                            v_tm[:, qt * D + hh * DH: qt * D + hh * DH + DH],
                            p2[:, pos * QTS:(pos + 1) * QTS],
                            start=True, stop=False)
                        vb = 32 * (i % 2)
                        last = eng.matmul(
                            dst, vg[vb:vb + NG, hh * DH:(hh + 1) * DH],
                            p2g[vb:vb + NG,
                                (i // 2) * QTS:(i // 2) * QTS + QTS],
                            start=False, stop=True)
                    return last
                ST("te", em_av,
                   reads=[f"v{qt}", "vg", [KA, KB][par],
                          f"p2g_{pp}"],
                   writes=[B[3]], tag=f"av_{c}_{qt}_{g}{par}")

            def em_ave(v, qt=qt, g=g, rec=rec):
                return v.tensor_tensor(
                    aov[:, 4 * g:4 * g + 4, 128 * qt:128 * qt + 128],
                    ps_f[:, o_s:o_s + 512].rearrange(
                        "p (r t) -> p r t", t=QTS),
                    rec.rearrange("p (r t) -> p r t", t=QTS),
                    op=ALU.mult)
            ST("ve", em_ave, reads=[B[3], f"rec{pp}"], writes=[f"ao{g}_{qt}"],
               tag=f"ave_{c}_{qt}_{g}")

    def stage_w(c, qt):
        pp = qt % 2
        cp = c % 2
        of = ATOF[pp]
        bks = WOBANKS[pp]
        xq = xv_p[cp][:, qt, :]
        mcol, rcol, ncol = mcol_p[pp], rcol_p[pp], ncol_p[pp]
        scol, s2col = scol_p[pp], s2col_p[pp]
        xm_sb = xm_p[pp]

        for ih in range(2):
            def em_wo(eng, qt=qt, of=of, ih=ih):
                last = None
                for jc in range(8):
                    last = eng.matmul(
                        ps_f[:, of + 512 * ih:of + 512 * ih + 512],
                        aov[:, jc, 128 * qt:128 * qt + 128],
                        chunks(wot)[:, jc, 512 * ih:512 * (ih + 1)],
                        start=(jc == 0), stop=(jc == 7))
                return last
            ST("te", em_wo,
               reads=[f"ao{g}_{qt}" for g in range(2)]
               + [f"wt_o{j}" for j in range(8)],
               writes=[bks[ih]], tag=f"wo_{c}_{qt}{ih}")

        z_sb = z_p[pp]

        bnb, mv = bnb_p[pp], mv_p[pp]
        for hf in range(2):
            def em_zev(se, of=of, z_sb=z_sb, hf=hf):
                return se.activation(
                    z_sb[:, 512 * hf:512 * (hf + 1)],
                    ps_f[:, of + 512 * hf:of + 512 * (hf + 1)], AF.Copy)
            ST("se", em_zev, reads=[bks[hf]], writes=[f"z{pp}{'ab'[hf]}"],
               tag=f"lnz_ev_{c}_{qt}{hf}")

            def em_zbs(v, z_sb=z_sb, hf=hf, bnb=bnb):
                return v.bn_stats(bnb[:, 6 * hf:6 * hf + 6],
                                  z_sb[:, 512 * hf:512 * (hf + 1)])
            ST("ve", em_zbs, reads=[f"z{pp}{'ab'[hf]}"], writes=[f"bnb{pp}"],
               tag=f"lnz_bs_{c}_{qt}{hf}")

        def em_zba(v, bnb=bnb, mv=mv):
            return v.bn_aggr(mv, bnb)
        ST("ve", em_zba, reads=[f"bnb{pp}"], writes=[f"mv{pp}"],
           tag=f"lnz_ba_{c}_{qt}")

        def em_rr(se, mv=mv, rcol=rcol, ncol=ncol):
            se.activation(ncol, mv[:, 1:2], AF.Ln, bias=epsc[:, 0:1])
            return se.activation(rcol, ncol, AF.Exp, scale=-0.5)
        ST("se", em_rr, reads=[f"mv{pp}", "epsc"],
           writes=[f"rcol{pp}", f"ncol{pp}"], tag=f"lnz_rr_{c}_{qt}")

        def em_nmr(v, rcol=rcol, mv=mv, ncol=ncol):
            return v.tensor_scalar(ncol, rcol, mv[:, 0:1], -1.0,
                                   op0=ALU.mult, op1=ALU.mult)
        ST("ve", em_nmr, reads=[f"rcol{pp}", f"mv{pp}"],
           writes=[f"ncol{pp}"], tag=f"lnz_nmr_{c}_{qt}")

        def em_xm(v, xq=xq, ncol=ncol, xm_sb=xm_sb):
            return v.tensor_scalar(xm_sb, xq, ncol[:, 0:1], None, op0=ALU.add)
        ST("ve", em_xm, reads=[f"x{cp}_{qt}", f"ncol{pp}"],
           writes=[f"xm{pp}"], tag=f"lnz_xm_{c}_{qt}")

        def em_x2(v, xq=xq, rcol=rcol, xm_sb=xm_sb, z_sb=z_sb):
            return v.scalar_tensor_tensor(xq, z_sb, rcol[:, 0:1], xm_sb,
                                          op0=ALU.mult, op1=ALU.add)
        ST("ve", em_x2, reads=[f"z{pp}a", f"z{pp}b", f"rcol{pp}",
                               f"xm{pp}"],
           writes=[f"x{cp}_{qt}"], tag=f"x2_{c}_{qt}")

    def stage_h2(c, qt):
        pp = qt % 2
        cp = c % 2
        xq = xv_p[cp][:, qt, :]
        hq = hv[:, qt, :]
        ln_stats(xq, [f"x{cp}_{qt}"], pp, f"ln2_{c}_{qt}")

        for hf in range(2):
            def em_n2(v, xq=xq, hq=hq, pp=pp, hf=hf):
                return v.tensor_scalar(hq[:, 512 * hf:512 * (hf + 1)],
                                       xq[:, 512 * hf:512 * (hf + 1)],
                                       mv_p[pp][:, 0:1],
                                       rcol_p[pp][:, 0:1],
                                       op0=ALU.subtract, op1=ALU.mult)
            ST("ve", em_n2,
               reads=[f"x{cp}_{qt}", f"mv{pp}", f"rcol{pp}"],
               writes=[f"h{qt}{'ab'[hf]}"], tag=f"n2_{c}_{qt}{hf}")
        transpose_htm(qt, f"h2_{c}")

    def stage_f(c, qt):
        pp = qt % 2
        cp = c % 2
        of = ATOF[pp]
        bks = WOBANKS[pp]
        xq = xv_p[cp][:, qt, :]
        mcol, rcol, ncol = mcol_p[pp], rcol_p[pp], ncol_p[pp]
        scol, s2col = scol_p[pp], s2col_p[pp]
        xm_sb = xm_p[pp]

        for ih in range(2):
            def em_ffn(eng, qt=qt, of=of, ih=ih):
                for jc in range(8):
                    eng.matmul(
                        ps_f[:, of + 512 * ih:of + 512 * ih + 512],
                        hfmv[:, jc, 128 * qt:128 * qt + 128],
                        chunks(Msb)[:, jc, 512 * ih:512 * (ih + 1)],
                        start=(jc == 0), stop=False)
                return eng.matmul(
                    ps_f[:, of + 512 * ih:of + 512 * ih + 512], ones_r,
                    bias_b[:, 512 * ih:512 * (ih + 1)],
                    start=False, stop=True)
            ST("te", em_ffn,
               reads=[f"hfm{j}_{qt}" for j in range(8)]
               + [f"M{j}" for j in range(8)] + ["ones_r", "bias_b"],
               writes=[bks[ih]], tag=f"ffn_{c}_{qt}{ih}")

        z_sb = z_p[pp]

        bnb, mv = bnb_p[pp], mv_p[pp]
        for hf in range(2):
            def em_yev(se, of=of, z_sb=z_sb, hf=hf):
                return se.activation(
                    z_sb[:, 512 * hf:512 * (hf + 1)],
                    ps_f[:, of + 512 * hf:of + 512 * (hf + 1)], AF.Copy)
            ST("se", em_yev, reads=[bks[hf]], writes=[f"z{pp}{'ab'[hf]}"],
               tag=f"lny_ev_{c}_{qt}{hf}")

            def em_ybs(v, z_sb=z_sb, hf=hf, bnb=bnb):
                return v.bn_stats(bnb[:, 6 * hf:6 * hf + 6],
                                  z_sb[:, 512 * hf:512 * (hf + 1)])
            ST("ve", em_ybs, reads=[f"z{pp}{'ab'[hf]}"], writes=[f"bnb{pp}"],
               tag=f"lny_bs_{c}_{qt}{hf}")

        def em_yba(v, bnb=bnb, mv=mv):
            return v.bn_aggr(mv, bnb)
        ST("ve", em_yba, reads=[f"bnb{pp}"], writes=[f"mv{pp}"],
           tag=f"lny_ba_{c}_{qt}")

        def em_rr(se, mv=mv, rcol=rcol, ncol=ncol):
            se.activation(ncol, mv[:, 1:2], AF.Ln, bias=epsc[:, 0:1])
            return se.activation(rcol, ncol, AF.Exp, scale=-0.5)
        ST("se", em_rr, reads=[f"mv{pp}", "epsc"],
           writes=[f"rcol{pp}", f"ncol{pp}"], tag=f"lny_rr_{c}_{qt}")

        def em_nmr(v, rcol=rcol, mv=mv, ncol=ncol):
            return v.tensor_scalar(ncol, rcol, mv[:, 0:1], -1.0,
                                   op0=ALU.mult, op1=ALU.mult)
        ST("ve", em_nmr, reads=[f"rcol{pp}", f"mv{pp}"],
           writes=[f"ncol{pp}"], tag=f"lny_nmr_{c}_{qt}")

        def em_xm(v, xq=xq, ncol=ncol, xm_sb=xm_sb):
            return v.tensor_scalar(xm_sb, xq, ncol[:, 0:1], None, op0=ALU.add)
        ST("ve", em_xm, reads=[f"x{cp}_{qt}", f"ncol{pp}"],
           writes=[f"xm{pp}"], tag=f"lny_xm_{c}_{qt}")

        def em_out(v, xq=xq, rcol=rcol, xm_sb=xm_sb, z_sb=z_sb):
            return v.scalar_tensor_tensor(xq, z_sb, rcol[:, 0:1], xm_sb,
                                          op0=ALU.mult, op1=ALU.add)
        ST("ve", em_out, reads=[f"z{pp}a", f"z{pp}b", f"rcol{pp}",
                                f"xm{pp}"],
           writes=[f"x{cp}_{qt}"], tag=f"out_{c}_{qt}")

        def em_st(eng, c=c, qt=qt, xq=xq):
            return eng.dma_start(out=out_dst[c][:, qt, :], in_=xq).then_inc(
                sems["d_seq"], 16)
        ST("se", em_st, reads=[f"x{cp}_{qt}"], writes=[], dma_queue="seq",
           tag=f"st_{c}_{qt}")
        if c + 1 < nchunks:
            ncp = (c + 1) % 2
            ld(xv_p[ncp][:, qt, :], x_src[c + 1][:, qt, :], "syq",
               f"x_in{c + 1}_{qt}", [f"x{ncp}_{qt}"])

    for qt in range(4):
        ld(xv_p[0][:, qt, :], x_src[0][:, qt, :],
           "seq" if qt < 2 else "syq", f"x_in0_{qt}",
           [f"x0_{qt}"])
    for c in range(nchunks):
        if c == 0:
            for qt in range(4):
                stage_ln1(c, qt)
        stage_qkv(c)
        if c == 0:
            stage_kgvg()
        stage_p2(c, 0)
        stage_w(c, 0)
        stage_p2(c, 1)
        stage_h2(c, 0)
        stage_w(c, 1)
        stage_f(c, 0)
        stage_p2(c, 2)
        stage_h2(c, 1)
        stage_w(c, 2)
        stage_f(c, 1)
        stage_p2(c, 3)
        stage_h2(c, 2)
        stage_w(c, 3)
        stage_f(c, 2)
        if c + 1 < nchunks:
            stage_ln1(c + 1, 0)
            stage_ln1(c + 1, 1)

        stage_h2(c, 3)
        stage_f(c, 3)
        if c + 1 < nchunks:
            stage_ln1(c + 1, 2)
            stage_ln1(c + 1, 3)

    with (
        nc.semaphore("e_te") as s0, nc.semaphore("e_ve") as s1,
        nc.semaphore("e_se") as s2, nc.semaphore("e_po") as s3,
        nc.semaphore("e_sy") as s4, nc.semaphore("d_syq") as s5,
        nc.semaphore("d_poq") as s6, nc.semaphore("d_seq") as s7,
        nc.semaphore("d_ccq") as s8, nc.semaphore("d_veq") as s9,
        nc.Block() as block,
    ):
        sems.update({"e_te": s0, "e_ve": s1, "e_se": s2, "e_po": s3,
                     "e_sy": s4, "d_syq": s5, "d_poq": s6, "d_seq": s7,
                     "d_ccq": s8, "d_veq": s9})

        qcounts = [(q, n) for q, n in P.dma_count.items() if n]

        def em_fin(eng):
            for q, n in qcounts:
                eng.wait_ge(sems["d_" + q], 16 * n)
            return eng.nop()
        P.step("sy", em_fin, reads=[], writes=[], tag="final")
        P.finalize(block, sems)
    return nc


_NC_CACHE = {}


def kernel(L0_attn_hidden_states, L0_attn_Wq, L0_attn_Wk, L0_attn_Wv,
           L0_attn_Wo, L0_ffn_w_freq_real, L0_ffn_w_freq_imag, L0_ffn_bias):
    x = np.ascontiguousarray(np.asarray(L0_attn_hidden_states, np.float32))
    B, T, _ = x.shape
    if T not in _NC_CACHE:
        _NC_CACHE[T] = build_nc(T, B)
    nc = _NC_CACHE[T]
    cns = host_constants()
    base = {
        "wq": np.ascontiguousarray(np.asarray(L0_attn_Wq, np.float32)),
        "wk": np.ascontiguousarray(np.asarray(L0_attn_Wk, np.float32)),
        "wv": np.ascontiguousarray(np.asarray(L0_attn_Wv, np.float32)),
        "wo": np.ascontiguousarray(np.asarray(L0_attn_Wo, np.float32)),

        "fbias": np.ascontiguousarray(
            np.asarray(L0_ffn_bias, np.float32).reshape(1, D)),
    }
    base.update(cns)
    wr = np.asarray(L0_ffn_w_freq_real, np.float32)
    wi = np.asarray(L0_ffn_w_freq_imag, np.float32)
    Cb = np.asarray(cns["cst_C"])
    Sb = np.asarray(cns["cst_S"])
    in_maps = [dict(base, x=np.ascontiguousarray(x[b]),
                    wr_sl=np.ascontiguousarray(wr[:, 128 * b:128 * (b + 1)]),
                    wi_sl=np.ascontiguousarray(wi[:, 128 * b:128 * (b + 1)]),
                    cst_c_sl=np.ascontiguousarray(Cb[128 * b:128 * (b + 1), :]),
                    cst_s_sl=np.ascontiguousarray(Sb[128 * b:128 * (b + 1), :]))
               for b in range(B)]
    res = run_bass_kernel_spmd(nc, in_maps, core_ids=list(range(B)))
    out = np.stack([np.asarray(r["out"], np.float32) for r in res.results])
    return out

